# version 09
# baseline (speedup 1.0000x reference)
import numpy as np
import ml_dtypes
import concourse.bass as bass
import concourse.mybir as mybir
from concourse.bass_utils import run_bass_kernel_spmd

F32 = mybir.dt.float32
BF16 = mybir.dt.bfloat16
I32 = mybir.dt.int32
ALU = mybir.AluOpType
ACT = mybir.ActivationFunctionType
AX = mybir.AxisListType.X
BF = ml_dtypes.bfloat16

P = 128
D = 1024
NDC = D // P
NEG = -1.0e30

ENGINES = ("sync", "gpsimd", "scalar", "vector", "tensor")



def _rope_tables(T):
    i = np.arange(0, 32, 2, dtype=np.float32) / np.float32(32.0)
    inv_freq = (1.0 / (10000.0 ** i.astype(np.float64)))
    t = np.arange(T, dtype=np.float64)
    fr = np.outer(t, inv_freq)
    cos = np.cos(fr).astype(np.float32)
    sin = np.sin(fr).astype(np.float32)
    rows = (np.arange(64) % 32) // 2
    C = cos[:, rows].T.copy()
    S = sin[:, rows].T.copy()
    return np.ascontiguousarray(C), np.ascontiguousarray(S)


def _j_mat(n):
    idx = np.arange(n)
    src = np.where(idx % 2 == 0, idx + 1, idx - 1)
    sgn = np.where(idx % 2 == 0, -1.0, 1.0).astype(np.float32)
    return src, sgn


def _wj(wt):
    d_in, d_out = wt.shape
    src, sgn = _j_mat(d_out)
    return np.ascontiguousarray(wt[:, src] * sgn[None, :])


def _own_positions(T, parity):
    nblk = T // P
    blocks = [b for b in range(nblk) if b % 2 == parity]
    return np.concatenate([np.arange(b * P, (b + 1) * P) for b in blocks])


def _build_host(inputs, T):
    g = {}
    t = lambda a: np.ascontiguousarray(np.asarray(a).T)

    wq_t = t(inputs["L0_attn_Wq"])
    wkv_t = t(inputs["L0_attn_Wkv"])
    wk_t = np.ascontiguousarray(wkv_t[:, 0:256])
    wv_t = np.ascontiguousarray(wkv_t[:, 256:512])
    HORDER = [0, 4, 1, 5, 2, 6, 3, 7, 8, 12, 9, 13, 10, 14, 11, 15]
    qperm = np.concatenate([np.arange(h * 64, (h + 1) * 64) for h in HORDER])
    wq_p = np.ascontiguousarray(wq_t[:, qperm])
    g["a0_wq"] = wq_p
    g["a0_wqj"] = _wj(wq_p)
    g["a0_wk"] = wk_t
    g["a0_wkj"] = _wj(wk_t)
    g["a0_wv"] = wv_t
    wout_t = t(inputs["L0_attn_Wout"])
    g["a0_wout"] = np.ascontiguousarray(wout_t[qperm, :])
    lam = np.asarray(inputs["L0_attn_lambda_init"], np.float32)
    g["a0_lambda"] = np.ascontiguousarray(lam[HORDER].reshape(1, 16))
    rw = np.asarray(inputs["L0_attn_rms_weight"], np.float32)
    g["a0_rmsw2"] = np.ascontiguousarray(np.tile(rw, 2).reshape(P, 1))

    for k in ["wq", "wk", "wv", "wo"]:
        g["f0_" + k] = t(inputs["L0_ffn_" + k])
    g["f0_gate"] = t(inputs["L0_ffn_gate_w"])
    for k in ["e0_up", "e0_down", "e1_w1", "e1_w2", "e1_down",
              "e2_l1", "e2_l2", "e2_l3", "e2_l4", "e3_down", "e3_up", "e3_out"]:
        g["f0_" + k] = t(inputs["L0_ffn_" + k])
    g["f0_n1"] = np.asarray(inputs["L0_ffn_n1_w"]).reshape(1, D)
    g["f0_n2"] = np.asarray(inputs["L0_ffn_n2_w"]).reshape(1, D)
    g["f0_qn_t"] = np.ascontiguousarray(np.tile(np.asarray(inputs["L0_ffn_qn_w"], np.float32), 16).reshape(1, D))
    g["f0_kn_t"] = np.ascontiguousarray(np.tile(np.asarray(inputs["L0_ffn_kn_w"], np.float32), 16).reshape(1, D))
    g["ident"] = np.eye(P, dtype=np.float32)

    for k in ["Wq", "Wk", "Wv", "Wo", "W1", "W2"]:
        g["a1_" + k] = np.ascontiguousarray(np.asarray(inputs["L1_attn_" + k], dtype=np.float32))
    for k in ["bq", "bk", "bv", "bo", "b1", "b2",
              "lnq_s", "lnq_b", "lnk_s", "lnk_b", "lnv_s", "lnv_b", "lnm_s", "lnm_b"]:
        v = np.asarray(inputs["L1_attn_" + k], dtype=np.float32)
        g["a1_" + k] = v.reshape(1, -1)

    g["f1_wg"] = t(inputs["L1_ffn_w_gate"])
    g["f1_wu"] = t(inputs["L1_ffn_w_up"])
    g["f1_wd"] = t(inputs["L1_ffn_w_down"])

    C, S = _rope_tables(T)
    g["rope_ck"] = C
    g["rope_sk"] = S
    g["onesrow"] = np.ones((1, P), np.float32)
    sel64 = np.zeros((P, 2), np.float32)
    sel64[0:64, 0] = 1.0
    sel64[64:128, 1] = 1.0
    g["sel64"] = sel64
    sel32 = np.zeros((P, 4), np.float32)
    for j in range(4):
        sel32[32 * j:32 * (j + 1), j] = 1.0
    g["sel32"] = sel32
    rep64 = np.zeros((2, P), np.float32)
    rep64[0, 0:64] = 1.0
    rep64[1, 64:128] = 1.0
    g["rep64"] = rep64
    rep32 = np.zeros((4, P), np.float32)
    for j in range(4):
        rep32[j, 32 * j:32 * (j + 1)] = 1.0
    g["rep32"] = rep32
    tri = np.triu(np.full((P, P), NEG, np.float32), k=1)
    g["_tri"] = tri
    zap1 = np.zeros((P, 1), np.float32)
    zap1[0:32] = 1.0
    zap1[64:96] = 1.0
    g["zap1"] = zap1
    g["zap2"] = (1.0 - zap1).astype(np.float32)
    return g


def _percore_host(inputs, g, T, core):
    b = core // 2
    par = core % 2
    pos = _own_positions(T, par)
    x = np.asarray(inputs["L0_attn_x"], dtype=np.float32)
    d = {}
    d["x_own"] = np.ascontiguousarray(x[b][pos])
    ppos = _own_positions(T, 1 - par)
    d["x_par"] = np.ascontiguousarray(x[b][ppos])
    loc = np.concatenate([pos, ppos])
    d["rope_ckl"] = np.ascontiguousarray(g["rope_ck"][:, loc])
    d["rope_skl"] = np.ascontiguousarray(g["rope_sk"][:, loc])
    tri = g["_tri"]
    zero = np.zeros((P, P), np.float32)
    full = np.full((P, P), NEG, np.float32)
    dm = np.stack([tri, full] if par == 0 else [tri, zero])
    d["diagmask"] = np.ascontiguousarray(dm)
    d["corner"] = np.asarray([[-999.0 * np.sqrt(float(D)) if par == 0 else 0.0]], np.float32)
    return d



class Sem:
    __slots__ = ("h", "n", "name")

    def __init__(self, h, name):
        self.h = h
        self.n = 0
        self.name = name

    def inc(self, instr, k=1):
        instr.then_inc(self.h, k)
        self.n += k
        return instr

    def wait(self, eng, val=None):
        eng.wait_ge(self.h, self.n if val is None else val)


class Ctx:

    def __init__(self, nc):
        self.nc = nc
        self.q = {e: [] for e in ENGINES}
        self._stack = []
        self.sems = {}
        self.dma_seen = []
        self.sbuf_bytes = 0

    def sem(self, name):
        if name not in self.sems:
            h = self.nc.semaphore(name)
            hh = h.__enter__()
            self._stack.append(h)
            self.sems[name] = Sem(hh, name)
        return self.sems[name]

    def init_pool(self, k=24):
        self.pool = [self.sem(f"ch{i}") for i in range(k)]
        self.pool_ptr = 0

    def take(self, k):
        assert self.pool_ptr + k <= len(self.pool), "chan pool exhausted in phase"
        out = self.pool[self.pool_ptr:self.pool_ptr + k]
        self.pool_ptr += k
        return out

    def sbuf(self, name, shape, dt):
        cm = self.nc.sbuf_tensor(name, shape, dt)
        t = cm.__enter__()
        self._stack.append(cm)
        self.sbuf_bytes += shape[1] * mybir.dt.size(dt)
        return t

    def psum(self, name, shape=(P, 512), dt=F32):
        cm = self.nc.psum_tensor(name, list(shape), dt)
        t = cm.__enter__()
        self._stack.append(cm)
        return t

    def emit(self, eng, fn):
        self.q[eng].append(fn)

    def barrier(self, tag):
        for ch in self.dma_seen:
            v = ch.n
            self.emit("sync", lambda eng, h=ch.h, v=v: eng.wait_ge(h, v))
        s = self.sem("bar")
        for e in ENGINES:
            def go(eng, s=s):
                eng.nop().then_inc(s.h, 1)
            self.emit(e, go)
        s.n += len(ENGINES)
        tgt = s.n
        for e in ENGINES:
            def wt(eng, s=s, tgt=tgt):
                eng.wait_ge(s.h, tgt)
            self.emit(e, wt)
        self.pool_ptr = 0

    def finalize(self):
        with self.nc.Block() as block:
            for ename in ENGINES:
                dec = getattr(block, ename)
                fns = self.q[ename]

                def body(eng, fns=fns):
                    for f in fns:
                        f(eng)

                dec(body)
        for cm in reversed(self._stack):
            cm.__exit__(None, None, None)



def _cfg(T):
    c = {}
    c["T"] = T
    c["TPC"] = T // 2
    c["NT"] = c["TPC"] // P
    c["NKC"] = T // P
    return c


def build_program(T, debug=()):
    cfg = _cfg(T)
    TPC, NT, NKC = cfg["TPC"], cfg["NT"], cfg["NKC"]
    nc = bass.Bass("TRN2", target_bir_lowering=False, debug=False)
    ctx = Ctx(nc)
    dbg = {}

    def dram_in(name, shape, dt=F32):
        return nc.dram_tensor(name, list(shape), dt, kind="ExternalInput").ap()

    def dram_int(name, shape, dt=F32):
        return nc.dram_tensor(name, list(shape), dt).ap()

    def dram_out(name, shape, dt=F32):
        return nc.dram_tensor(name, list(shape), dt, kind="ExternalOutput").ap()

    def dbg_out(name, shape, dt=F32):
        if name in debug:
            dbg[name] = dram_out("dbg_" + name, shape, dt)
            return dbg[name]
        return None

    A = {}
    A["x_own"] = dram_in("x_own", [TPC, D])
    A["x_par"] = dram_in("x_par", [TPC, D])
    for n, sh in [
        ("a0_wq", [D, D]), ("a0_wqj", [D, D]), ("a0_wk", [D, 256]), ("a0_wkj", [D, 256]),
        ("a0_wv", [D, 256]), ("a0_wout", [D, D]), ("a0_lambda", [1, 16]), ("a0_rmsw2", [P, 1]),
        ("f0_wq", [D, D]), ("f0_wk", [D, D]), ("f0_wv", [D, D]), ("f0_wo", [D, D]),
        ("f0_gate", [D, 4]),
        ("f0_e0_up", [D, 2048]), ("f0_e0_down", [2048, D]),
        ("f0_e1_w1", [D, 2048]), ("f0_e1_w2", [D, 2048]), ("f0_e1_down", [2048, D]),
        ("f0_e2_l1", [D, 2048]), ("f0_e2_l2", [2048, 2048]), ("f0_e2_l3", [2048, 2048]),
        ("f0_e2_l4", [2048, D]),
        ("f0_e3_down", [D, D]), ("f0_e3_up", [D, 2048]), ("f0_e3_out", [2048, D]),
        ("f0_n1", [1, D]), ("f0_n2", [1, D]), ("f0_qn_t", [1, D]), ("f0_kn_t", [1, D]),
        ("ident", [P, P]),
        ("a1_Wq", [D, D]), ("a1_Wk", [D, D]), ("a1_Wv", [D, D]), ("a1_Wo", [D, D]),
        ("a1_W1", [D, 2048]), ("a1_W2", [2048, D]),
        ("a1_bq", [1, D]), ("a1_bk", [1, D]), ("a1_bv", [1, D]), ("a1_bo", [1, D]),
        ("a1_b1", [1, 2048]), ("a1_b2", [1, D]),
        ("a1_lnq_s", [1, D]), ("a1_lnq_b", [1, D]), ("a1_lnk_s", [1, D]), ("a1_lnk_b", [1, D]),
        ("a1_lnv_s", [1, D]), ("a1_lnv_b", [1, D]), ("a1_lnm_s", [1, D]), ("a1_lnm_b", [1, D]),
        ("f1_wg", [D, 4096]), ("f1_wu", [D, 4096]), ("f1_wd", [4096, D]),
        ("rope_ckl", [64, T]), ("rope_skl", [64, T]),
        ("onesrow", [1, P]), ("sel64", [P, 2]), ("sel32", [P, 4]),
        ("rep64", [2, P]), ("rep32", [4, P]),
        ("diagmask", [2, P, P]), ("corner", [1, 1]), ("zap1", [P, 1]), ("zap2", [P, 1]),
    ]:
        A[n] = dram_in(n, sh)

    A["xn_sp"] = dram_int("xn_sp", [TPC, D])
    A["xnT_sp"] = dram_int("xnT_sp", [P, NDC * T], BF16)
    A["att0_sp"] = dram_int("att0_sp", [TPC, D])
    A["xmid_sp"] = dram_int("xmid_sp", [TPC, D])
    A["xsum_sp"] = dram_int("xsum_sp", [TPC, D])
    A["xn1_sp"] = dram_int("xn1_sp", [TPC, D])
    A["xn1T_sp"] = dram_int("xn1T_sp", [P, NDC * TPC], BF16)
    A["gu_sp"] = dram_int("gu_sp", [TPC, 4096])
    A["xacc_sp"] = dram_int("xacc_sp", [TPC, D])
    A["h_sp"] = dram_int("h_sp", [TPC, D])
    A["bv1_sp"] = dram_int("bv1_sp", [1, 2048])
    A["cc_in"] = dram_int("cc_in", [P, 2 * NDC * TPC + 128], BF16)
    A["cc_out"] = dram_int("cc_out", [2, P, 2 * NDC * TPC + 128], BF16)
    A["wq_sp"] = {}
    for _n, _di, _do in [("f1_wg", D, 4096), ("f1_wu", D, 4096), ("f1_wd", 4096, D)]:
        A["wq_sp"][_n] = dram_int(_n + "_q", [_di, _do], BF16)
    A["ew_sp"] = {}
    for _n, _di, _do, _down in [
        ("f0_e0_up", D, 2048, False), ("f0_e0_down", 2048, 768, True),
        ("f0_e1_w1", D, 2048, False), ("f0_e1_w2", D, 2048, False),
        ("f0_e1_down", 2048, 768, True),
        ("f0_e2_l1", D, 2048, False), ("f0_e2_l2", 2048, 2048, False),
        ("f0_e2_l3", 2048, 2048, False), ("f0_e2_l4", 2048, 768, True),
        ("f0_e3_down", D, D, False), ("f0_e3_up", D, 2048, False),
        ("f0_e3_out", 2048, 768, True),
    ]:
        A["ew_sp"][_n] = dram_int(_n + "_bq", [_di, _do], BF16)
    A["k0f_sp"] = dram_int("k0f_sp", [P, NDC * T], BF16)
    A["v0f_sp"] = dram_int("v0f_sp", [P, (T // P) * D], BF16)
    A["y"] = dram_out("y", [TPC, D])

    sb = {}
    sb["stage"] = ctx.sbuf("stage", [P, 4 * D], F32)
    sb["stat"] = ctx.sbuf("stat", [P, 1800], F32)
    sb["bf"] = ctx.sbuf("bfs", [P, 4096], BF16)
    sb["xnT"] = ctx.sbuf("xnT", [P, max(NDC * T, 32768)], BF16)
    sb["wbf"] = ctx.sbuf("wbf", [P, 16384], BF16)
    sb["qt"] = ctx.sbuf("qt", [P, max(NDC * TPC, 4096)], BF16)
    sb["kvc"] = ctx.sbuf("kvc", [P, 16384], BF16)
    sb["rst"] = ctx.sbuf("rst", [P, 2048], F32)
    sb["bwork"] = ctx.sbuf("bwork", [P, 512], F32)
    sb["small"] = ctx.sbuf("small", [P, 768], F32)

    psA = ctx.psum("psA", (P, 2048))
    psB = ctx.psum("psB", (P, 2048))

    LD = ctx.sem("ld")
    G = ctx.sem("gd")
    TP = ctx.sem("tp")
    VD = ctx.sem("vd")
    SD = ctx.sem("sd")
    MM = ctx.sem("mm")
    ST = ctx.sem("st")

    E = ctx.emit

    def dma(qeng, dst, src, sem):
        def go(eng, dst=dst, src=src, h=sem.h):
            eng.dma_start(out=dst, in_=src).then_inc(h, 16)
        E(qeng, go)
        sem.n += 16
        return sem.n

    def dmat(dst, src):
        def go(eng, dst=dst, src=src, h=TP.h):
            eng.dma_start_transpose(dst, src).then_inc(h, 16)
        E("scalar", go)
        TP.n += 16
        return TP.n

    def wait(eng_name, sem, val=None):
        v = sem.n if val is None else val
        E(eng_name, lambda eng, h=sem.h, v=v: eng.wait_ge(h, v))

    def vop(fn):
        v = VD.n
        E("vector", lambda eng, fn=fn, h=VD.h, v=v: (eng.wait_ge(h, v), fn(eng).then_inc(h, 1)))
        VD.n += 1
        return VD.n

    def sop(fn):
        v = SD.n
        E("scalar", lambda eng, fn=fn, h=SD.h, v=v: (eng.wait_ge(h, v), fn(eng).then_inc(h, 1)))
        SD.n += 1
        return SD.n

    def top(fn, inc=True):
        if inc:
            E("tensor", lambda eng, fn=fn, h=MM.h: fn(eng).then_inc(h, 1))
            MM.n += 1
            return MM.n
        E("tensor", lambda eng, fn=fn: fn(eng))
        return MM.n

    ctx.init_pool(40)

    def chans(name, k):
        return ctx.take(k)

    def psem(name):
        return ctx.take(1)[0]

    def t_dst(buf, blk, tot=None):
        tot = T if tot is None else tot
        return buf[:, 0:NDC * tot].rearrange("p (c t) -> p c t", t=tot)[:, :, blk * P:(blk + 1) * P]

    def groups(total, w=512):
        out = []
        o = 0
        while o < total:
            out.append((o, min(w, total - o)))
            o += min(w, total - o)
        return out

    def bcast64(ap2d):
        return ap2d.unsqueeze(2).broadcast_to([P, ap2d.shape[1], 64])

    def dma_c(qeng, dst, src, ch, slow=False):
        if ch not in ctx.dma_seen:
            ctx.dma_seen.append(ch)
        def go(eng, dst=dst, src=src, h=ch.h, slow=slow):
            eng.dma_start(out=dst, in_=src,
                          allow_slow_non_contiguous=slow).then_inc(h, 16)
        E(qeng, go)
        ch.n += 16
        return ch.n

    def dmat_c(dst, src, ch):
        if ch not in ctx.dma_seen:
            ctx.dma_seen.append(ch)
        def go(eng, dst=dst, src=src, h=ch.h):
            eng.dma_start_transpose(dst, src).then_inc(h, 16)
        E("sync", go)
        ch.n += 16
        return ch.n

    def ln_finalize(nt2, eps, base=0):
        st = sb["stat"]
        o = base
        mu_raw, ss = st[:, o:o + nt2], st[:, o + 64:o + 64 + nt2]
        mu, rstd = st[:, o + 128:o + 128 + nt2], st[:, o + 192:o + 192 + nt2]
        var = st[:, o + 256:o + 256 + nt2]
        tmp = st[:, o + 320:o + 320 + nt2]
        invd = 1.0 / D
        wait("vector", SD)
        vop(lambda eng: eng.tensor_scalar(mu, mu_raw, invd, None, ALU.mult))
        vop(lambda eng: eng.tensor_scalar(var, ss, invd, None, ALU.mult))
        vop(lambda eng: eng.tensor_tensor(tmp, mu, mu, ALU.mult))
        vop(lambda eng: eng.tensor_tensor(var, var, tmp, ALU.subtract))
        vop(lambda eng: eng.tensor_scalar(var, var, eps, None, ALU.add))
        wait("scalar", VD)
        sop(lambda eng: eng.activation(out=tmp, in_=var, func=ACT.Sqrt))
        wait("vector", SD)
        vop(lambda eng: eng.reciprocal(rstd, tmp))
        return mu, rstd

    def phase1():
        st = sb["stat"]
        nt2 = 2 * NT
        assert nt2 <= 64
        Lx = chans("p1lx", 2)
        Sp = chans("p1sp", 2)
        Tp = chans("p1tp", 2)
        vmark, smark = [], []
        for i in range(nt2):
            src = A["x_own"] if i < NT else A["x_par"]
            row = (i % NT) * P
            sl = i % 2
            xa = sb["stage"][:, sl * D:(sl + 1) * D]
            if i >= 2:
                wait("sync", VD, vmark[i - 2])
                wait("sync", SD, smark[i - 2])
            dma_c("sync", xa, src[row:row + P, :], Lx[sl])
            wait("vector", Lx[sl])
            wait("scalar", Lx[sl])
            v = vop(lambda eng, xa=xa, c=st[:, i:i + 1]: eng.reduce_sum(c, xa, AX))
            s_ = sop(lambda eng, xa=xa, c=st[:, 64 + i:64 + i + 1],
                     sq=sb["stage"][:, (2 + i % 2) * D:(3 + i % 2) * D]:
                     eng.activation(out=sq, in_=xa, func=ACT.Square, accum_out=c))
            vmark.append(v)
            smark.append(s_)
        ln_finalize(nt2, 1e-5)
        wait("sync", VD)
        wait("sync", SD)
        wait("vector", SD)
        vts, vcast, sqm = [], [], []
        for i in range(nt2):
            own = i < NT
            row = (i % NT) * P
            src = A["x_own"] if own else A["x_par"]
            sl = i % 2
            xa = sb["stage"][:, sl * D:(sl + 1) * D]
            xn = sb["stage"][:, (2 + sl) * D:(3 + sl) * D]
            xb = sb["bf"][:, sl * D:(sl + 1) * D]
            if i >= 2:
                wait("sync", VD, vts[i - 2])
            dma_c("sync", xa, src[row:row + P, :], Lx[sl])
            wait("vector", Lx[sl])
            if i >= 2:
                wait("vector", Sp[sl])
                wait("vector", SD, sqm[i - 2])
            mu = st[:, 128 + i:128 + i + 1]
            r = st[:, 192 + i:192 + i + 1]
            v1 = vop(lambda eng, xn=xn, xa=xa, mu=mu, r=r:
                     eng.tensor_scalar(xn, xa, mu, r, ALU.subtract, ALU.mult))
            vts.append(v1)
            if i >= 2:
                wait("vector", Tp[sl])
            v2 = vop(lambda eng, xb=xb, xn=xn: eng.tensor_copy(xb, xn))
            vcast.append(v2)
            wait("scalar", VD, v1)
            sqm.append(sop(lambda eng, xn=xn, i=i, sq=sb["bf"][:, (2 + i % 2) * D:(3 + i % 2) * D]:
                eng.activation(out=sq, in_=xn, func=ACT.Square, accum_out=st[:, 384 + i:385 + i])))
            if own:
                wait("sync", VD, v1)
                dma_c("sync", A["xn_sp"][row:row + P, :], xn, Sp[sl])
            blk = (i % NT) if own else NT + (i % NT)
            wait("sync", VD, v2)
            dmat_c(t_dst(sb["xnT"], blk), xb, Tp[sl])
        wait("vector", SD)
        vop(lambda eng: eng.tensor_scalar(st[:, 448:448 + nt2], st[:, 384:384 + nt2],
                                          1.0 / D, 1.1920929e-07, ALU.mult, ALU.add))
        v = vop(lambda eng: eng.tensor_scalar(st[:, 448:448 + nt2], st[:, 448:448 + nt2],
                                              1.0, None, ALU.mult))
        wait("scalar", VD, v)
        sr = sop(lambda eng: eng.activation(out=st[:, 384:384 + nt2], in_=st[:, 448:448 + nt2],
                                            func=ACT.Sqrt))
        wait("vector", SD, sr)
        vop(lambda eng: eng.reciprocal(st[:, 448:448 + nt2], st[:, 384:384 + nt2]))
        for c in (Tp[0], Tp[1]):
            wait("sync", c)
        dma_c("sync", A["xnT_sp"][:], sb["xnT"][:, 0:NDC * T], LD)

    phase1()
    ctx.barrier("p1")

    SC0 = float(32.0 ** -0.5)

    SM_DM0, SM_DM1 = 0, 128
    SM_LREP = 256
    SM_ONES = 272
    SM_RMSW = 273
    SM_SEL32 = 274
    SM_SEL64 = 278
    SM_KMAX = 280
    SM_ONESROW = 283
    SM_REP64 = 411
    SM_LSIG = 539
    SM_ZAP1 = 555
    SM_ZAP2 = 556

    def phase2_setup():
        sm = sb["small"]
        C = psem("p2c")
        dma_c("sync", sm[:, SM_DM0:SM_DM0 + 128], A["diagmask"][0], C)
        dma_c("sync", sm[:, SM_DM1:SM_DM1 + 128], A["diagmask"][1], C)
        dma_c("sync", sm[:, SM_SEL32:SM_SEL32 + 4], A["sel32"], C)
        dma_c("sync", sm[:, SM_SEL64:SM_SEL64 + 2], A["sel64"], C)
        dma_c("sync", sm[0:2, SM_REP64:SM_REP64 + 128], A["rep64"], C)
        dma_c("sync", sm[0:1, SM_ONESROW:SM_ONESROW + 128], A["onesrow"], C)
        dma_c("sync", sm[32:33, SM_ONESROW:SM_ONESROW + 128], A["onesrow"], C)
        dma_c("sync", sm[0:1, SM_LSIG:SM_LSIG + 16], A["a0_lambda"], C)
        dma_c("sync", sm[:, SM_RMSW:SM_RMSW + 1], A["a0_rmsw2"], C)
        dma_c("sync", sm[:, SM_ZAP1:SM_ZAP1 + 1], A["zap1"], C)
        dma_c("sync", sm[:, SM_ZAP2:SM_ZAP2 + 1], A["zap2"], C)
        wait("vector", C)
        wait("scalar", C)
        vop(lambda eng: eng.memset(sm[:, SM_ONES:SM_ONES + 1], 1.0))
        vop(lambda eng: eng.tensor_scalar(sm[:, SM_RMSW:SM_RMSW + 1],
                                          sm[:, SM_RMSW:SM_RMSW + 1], -1.0, None, ALU.mult))
        s1 = sop(lambda eng: eng.activation(out=sm[0:1, SM_LSIG:SM_LSIG + 16],
                                            in_=sm[0:1, SM_LSIG:SM_LSIG + 16], func=ACT.Sigmoid))
        wait("tensor", SD, s1)
        wait("tensor", VD)
        m = top(lambda eng: eng.matmul(psA[0:P, 0:16], sm[0:1, SM_ONESROW:SM_ONESROW + 128],
                                       sm[0:1, SM_LSIG:SM_LSIG + 16], start=True, stop=True))
        wait("vector", MM, m)
        vop(lambda eng: eng.tensor_copy(sm[:, SM_LREP:SM_LREP + 16], psA[:, 0:16]))

    def stream_weight(src_ap, nchunks, ncols, dst_base, Wch, fold=None):
        wait("sync", VD)
        marks = []
        for c in range(nchunks):
            sl = c % 2
            stg = sb["stage"][:, sl * D:sl * D + ncols]
            if c >= 2:
                wait("sync", VD, marks[c - 2])
            dma_c("sync", stg, src_ap[c * P:(c + 1) * P, :], Wch[sl])
            wait("vector", Wch[sl])
            dst = sb["wbf"][:, dst_base + c * ncols: dst_base + (c + 1) * ncols]
            if fold is None:
                v = vop(lambda eng, d=dst, s=stg: eng.tensor_copy(d, s))
            else:
                v = vop(lambda eng, d=dst, s=stg, f=fold(c):
                        eng.tensor_scalar(d, s, f, None, ALU.mult))
            marks.append(v)
        return marks

    def rope_proj(n_oc, wbase, wcols, TT_, out_buf, out_base, out_stride, Rch):
        wbase2 = wbase + NDC * wcols
        for oc in range(n_oc):
            for gi, (go, gw) in enumerate(groups(TT_)):
                sl = gi % 2
                cs = sb["rst"][:, sl * 1024: sl * 1024 + gw]
                sn = sb["rst"][:, sl * 1024 + 512: sl * 1024 + 512 + gw]
                colr = slice(go, go + gw)
                wait("sync", VD)
                dma_c("sync", cs[0:64, :], A["rope_ckl"][0:64, colr], Rch[sl])
                dma_c("sync", cs[64:128, :], A["rope_ckl"][0:64, colr], Rch[sl])
                dma_c("sync", sn[0:64, :], A["rope_skl"][0:64, colr], Rch[sl])
                dma_c("sync", sn[64:128, :], A["rope_skl"][0:64, colr], Rch[sl])
                wait("tensor", VD)
                for c in range(NDC):
                    top(lambda eng, c=c, oc=oc, go=go, gw=gw:
                        eng.matmul(psA[:, 0:gw],
                                   sb["wbf"][:, wbase + c * wcols + oc * P: wbase + c * wcols + (oc + 1) * P],
                                   sb["xnT"][:, c * T + go: c * T + go + gw],
                                   start=(c == 0), stop=(c == NDC - 1)), inc=(c == NDC - 1))
                for c in range(NDC):
                    top(lambda eng, c=c, oc=oc, go=go, gw=gw:
                        eng.matmul(psB[:, 0:gw],
                                   sb["wbf"][:, wbase2 + c * wcols + oc * P: wbase2 + c * wcols + (oc + 1) * P],
                                   sb["xnT"][:, c * T + go: c * T + go + gw],
                                   start=(c == 0), stop=(c == NDC - 1)), inc=(c == NDC - 1))
                wait("vector", MM)
                wait("vector", Rch[sl])
                t1 = sb["stage"][:, 2 * D:2 * D + gw]
                t2 = sb["stage"][:, 2 * D + 512:2 * D + 512 + gw]
                vop(lambda eng, t1=t1, cs=cs, gw=gw: eng.tensor_tensor(t1, psA[:, 0:gw], cs, ALU.mult))
                vop(lambda eng, t2=t2, sn=sn, gw=gw: eng.tensor_tensor(t2, psB[:, 0:gw], sn, ALU.mult))
                dst = out_buf[:, out_base + oc * out_stride + go:
                              out_base + oc * out_stride + go + gw]
                vop(lambda eng, d=dst, t1=t1, t2=t2: eng.tensor_tensor(d, t1, t2, ALU.add))

    def phase2_proj():
        Wch = chans("p2w", 2)
        Rch = chans("p2r", 2)
        stream_weight(A["a0_wk"], NDC, 256, 0, Wch)
        stream_weight(A["a0_wkj"], NDC, 256, 2048, Wch)
        stream_weight(A["a0_wv"], NDC, 256, 4096, Wch)
        rope_proj(2, 0, 256, T, sb["kvc"], 0, T, Rch)
        mk = []
        for tt in range(NKC):
            wait("tensor", VD)
            for c in range(NDC):
                top(lambda eng, c=c, tt=tt:
                    eng.matmul(psA[:, 512:768],
                               sb["xnT"][:, c * T + tt * P:c * T + (tt + 1) * P],
                               sb["wbf"][:, 4096 + c * 256: 4096 + (c + 1) * 256],
                               start=(c == 0), stop=(c == NDC - 1)), inc=(c == NDC - 1))
            wait("vector", MM)
            v = vop(lambda eng, tt=tt: eng.tensor_copy(
                sb["kvc"][:, 2 * T + tt * 256: 2 * T + (tt + 1) * 256], psA[:, 512:768]))
            mk.append(v)
        wait("sync", MM)
        wait("sync", VD)
        stream_weight(A["a0_wq"], NDC, D, 0, Wch)
        stream_weight(A["a0_wqj"], NDC, D, 8192, Wch)
        rope_proj(NDC, 0, D, TPC, sb["qt"], 0, TPC, Rch)

    def phase2_bounds():
        sm = sb["small"]
        for cc in range(2):
            for hi, (ho, hw) in enumerate(groups(T, 2048)):
                sq = sb["stage"][:, 0:hw]
                s_ = sop(lambda eng, cc=cc, ho=ho, hw=hw, sq=sq: eng.activation(
                    out=sq, in_=sb["kvc"][:, cc * T + ho: cc * T + ho + hw], func=ACT.Square))
                wait("tensor", SD, s_)
                wait("tensor", VD)
                for gi, (go, gw) in enumerate(groups(hw)):
                    top(lambda eng, go=go, gw=gw, sq=sq:
                        eng.matmul(psB[0:4, go:go + gw], sm[:, SM_SEL32:SM_SEL32 + 4],
                                   sq[:, go:go + gw], start=True, stop=True),
                        inc=(go + gw == hw))
                wait("vector", MM)
                vop(lambda eng, hi=hi, hw=hw: eng.reduce_max(
                    sb["rst"][0:4, hi:hi + 1], psB[0:4, 0:hw], AX))
                wait("scalar", VD)
            nh = len(groups(T, 2048))
            v = vop(lambda eng, cc=cc, nh=nh: eng.reduce_max(
                sm[0:4, SM_KMAX + cc:SM_KMAX + cc + 1], sb["rst"][0:4, 0:nh], AX))
        BT = psem("p2bt")
        vop(lambda eng: eng.memset(sb["bf"][0:16, 0:TPC], 0.0))
        for j in range(NDC):
            sq = sb["stage"][:, 0:TPC]
            s_ = sop(lambda eng, j=j, sq=sq: eng.activation(
                out=sq, in_=sb["qt"][:, j * TPC:(j + 1) * TPC], func=ACT.Square))
            wait("tensor", SD, s_)
            wait("tensor", VD)
            for gi, (go, gw) in enumerate(groups(TPC)):
                top(lambda eng, go=go, gw=gw, sq=sq: eng.matmul(
                    psA[0:4, go:go + gw], sm[:, SM_SEL32:SM_SEL32 + 4],
                    sq[:, go:go + gw], start=True, stop=True), inc=(go + gw == TPC))
            wait("vector", MM)
            cc = 0 if j < NDC // 2 else 1
            v = vop(lambda eng, cc=cc: eng.tensor_scalar(
                sb["rst"][0:4, 0:TPC], psA[0:4, 0:TPC],
                sm[0:4, SM_KMAX + cc:SM_KMAX + cc + 1], None, ALU.mult))
            wait("scalar", VD, v)
            s2 = sop(lambda eng: eng.activation(
                out=sb["rst"][0:4, 0:TPC], in_=sb["rst"][0:4, 0:TPC],
                func=ACT.Sqrt, scale=SC0 * SC0))
            wait("vector", SD, s2)
            wait("vector", BT)
            v = vop(lambda eng: eng.tensor_scalar(
                sb["bf"][0:4, 0:TPC], sb["rst"][0:4, 0:TPC], -1.0, None, ALU.mult))
            wait("sync", VD, v)
            dmat_c(sb["bf"][:, 2048 + j * NT * 16: 2048 + (j + 1) * NT * 16]
                   .rearrange("p (t s) -> p t s", s=16),
                   sb["bf"][0:16, 0:TPC], BT)
        wait("vector", BT)
        vop(lambda eng: eng.tensor_copy(
            sb["bwork"][:, 0:NT * 4 * NDC],
            sb["bf"][:, 2048:2048 + NT * 16 * NDC]
            .rearrange("p (g s) -> p g s", s=16)[:, :, 0:4]))

    def phase2_attn():
        EW = sb["xnT"]
        E = {}
        E[(0, 0)], E[(0, 1)], E[(1, 0)], E[(1, 1)] = 0, T, 2 * T, 3 * T
        PTB = 4 * T
        st = sb["stat"]
        sm = sb["small"]
        TpP = chans("p2pt", 2)
        NT2 = NT // 2
        wait("scalar", LD)
        wait("vector", LD)
        pend_pv = [None]
        bank_sd = {0: 0, 1: 0}
        pv_evac = [0]
        cmb_mark = [0]

        def pt_plane(nbm, hh, x):
            return PTB + (hh * 2 * nbm + x) * 256

        def flush_pv():
            if pend_pv[0] is None:
                return
            j, g, nbm, gA = pend_pv[0]
            pend_pv[0] = None
            zm = 0
            for hh in range(2):
                base = pt_plane(nbm, hh, 2 * nbm - 2)
                zm = vop(lambda eng, base=base: eng.memset(
                    EW[:, base:base + 512].rearrange("p (x s q) -> p x s q", s=2, q=P)
                    [:, :, 0, :], 0.0))
            wait("tensor", VD, zm)
            wait("tensor", VD, pv_evac[0])
            wait("tensor", SD, bank_sd[1])
            wait("tensor", TpP[0])
            wait("tensor", TpP[1])
            for x in range(2 * nbm):
                bb, r = x >> 1, x & 1
                tile = bb if r == 0 else NT + bb
                for hh in range(2):
                    ps = psA if hh == 0 else psB
                    top(lambda eng, x=x, tile=tile, hh=hh, gA=gA, nbm=nbm, ps=ps:
                        eng.matmul(ps[64 * hh:64 * hh + 64, 1536:1536 + 256],
                                   sb["kvc"][:, 2 * T + tile * 256 + (gA + hh) * 64:
                                             2 * T + tile * 256 + (gA + hh + 1) * 64],
                                   EW[:, pt_plane(nbm, hh, x):pt_plane(nbm, hh, x) + 256],
                                   start=(x == 0), stop=(x == 2 * nbm - 1),
                                   tile_position=(0, 64 * hh), skip_group_check=True),
                        inc=(x == 2 * nbm - 1 and hh == 1))
            wait("vector", MM)
            vop(lambda eng, j=j, g=g: eng.tensor_copy(
                sb["wbf"][0:64, j * TPC + 2 * g * P: j * TPC + (2 * g + 2) * P],
                psA[0:64, 1536:1536 + 256]))
            pv_evac[0] = vop(lambda eng, j=j, g=g: eng.tensor_copy(
                sb["wbf"][64:128, j * TPC + 2 * g * P: j * TPC + (2 * g + 2) * P],
                psB[64:128, 1536:1536 + 256]))

        def do_slot(j, s, kcb, nbm):
            nb = s + 1
            w = nb * P
            grps = groups(w, 1024)
            ngr = len(grps)
            segs = [(half, run, go, gw) for half in (1, 0)
                    for run in range(2) for (go, gw) in grps]
            first_exp = True
            first_h0 = True
            for i, (half, run, go, gw) in enumerate(segs):
                bcol = (i % 2) * 1024
                qzb = half * TPC
                tof = 0 if run == 0 else NT
                gi = grps.index((go, gw))
                if bank_sd[i % 2]:
                    wait("tensor", SD, bank_sd[i % 2])
                m = None
                for qo, qw in groups(gw):
                    top(lambda eng, go=go, qo=qo, qw=qw, qzb=qzb, s=s, kcb=kcb,
                        tof=tof, bcol=bcol:
                        eng.matmul(psA[:, bcol + qo:bcol + qo + qw],
                                   sb["bf"][0:64, qzb + s * P: qzb + (s + 1) * P],
                                   sb["kvc"][0:64, kcb + tof * P + go + qo:
                                             kcb + tof * P + go + qo + qw],
                                   start=True, stop=True, tile_position=(0, 0)),
                        inc=False)
                    m = top(lambda eng, go=go, qo=qo, qw=qw, qzb=qzb, s=s, kcb=kcb,
                            tof=tof, bcol=bcol:
                            eng.matmul(psB[:, bcol + qo:bcol + qo + qw],
                                       sb["bf"][64:128, qzb + s * P: qzb + (s + 1) * P],
                                       sb["kvc"][64:128, kcb + tof * P + go + qo:
                                                 kcb + tof * P + go + qo + qw],
                                       start=True, stop=True, tile_position=(64, 0)),
                            inc=True)
                has_diag = go <= s * P < go + gw
                if has_diag:
                    dcol = bcol + s * P - go
                    dm = sm[:, SM_DM0:SM_DM0 + 128] if run == 0 else sm[:, SM_DM1:SM_DM1 + 128]
                    wait("vector", MM, m)
                    va = vop(lambda eng, dcol=dcol, dm=dm: eng.tensor_tensor(
                        psA[:, dcol:dcol + P], psA[:, dcol:dcol + P], dm, ALU.add))
                    vb = vop(lambda eng, dcol=dcol, dm=dm: eng.tensor_tensor(
                        psB[:, dcol:dcol + P], psB[:, dcol:dcol + P], dm, ALU.add))
                if first_exp:
                    wait("scalar", VD, cmb_mark[0])
                    first_exp = False
                if half == 0 and first_h0:
                    wait("scalar", TpP[0])
                    wait("scalar", TpP[1])
                    first_h0 = False
                if has_diag:
                    wait("scalar", VD, vb)
                else:
                    wait("scalar", MM, m)
                for hh, ps in ((0, psA), (1, psB)):
                    strm = j * NT * 4 + s * 4 + 2 * hh + half
                    bias = sb["bwork"][:, strm:strm + 1]
                    zcol = st[:, 16 * hh + half * 2 * ngr + run * ngr + gi:
                              16 * hh + half * 2 * ngr + run * ngr + gi + 1]
                    e_ = sop(lambda eng, go=go, gw=gw, w=w, run=run, hh=hh, half=half,
                             bias=bias, zcol=zcol, ps=ps, bcol=bcol:
                             eng.activation(
                                 out=EW[:, E[(hh, half)]:E[(hh, half)] + 2 * w]
                                 .rearrange("p (b r q) -> p b r q", r=2, q=P)
                                 [:, go // P:(go + gw) // P, run, :],
                                 in_=ps[:, bcol:bcol + gw], func=ACT.Exp,
                                 bias=bias, scale=SC0, accum_out=zcol))
                bank_sd[i % 2] = SD.n
            wait("vector", SD, bank_sd[(len(segs) - 1) % 2])
            for hh in range(2):
                for half in range(2):
                    cbase = 16 * hh + half * 2 * ngr
                    vop(lambda eng, hh=hh, half=half, cbase=cbase, ngr=ngr:
                        eng.reduce_sum(st[:, 32 + 2 * hh + half:33 + 2 * hh + half],
                                       st[:, cbase:cbase + 2 * ngr], AX))
            vop(lambda eng: eng.reciprocal(st[:, 36:40], st[:, 32:36]))
            vop(lambda eng, j=j: eng.tensor_tensor(
                st[:, 37:38], st[:, 37:38], sm[:, SM_LREP + 2 * j:SM_LREP + 2 * j + 1],
                ALU.mult))
            vop(lambda eng, j=j: eng.tensor_tensor(
                st[:, 39:40], st[:, 39:40], sm[:, SM_LREP + 2 * j + 1:SM_LREP + 2 * j + 2],
                ALU.mult))
            w2 = 2 * w
            for hh in range(2):
                e1 = EW[:, E[(hh, 0)]:E[(hh, 0)] + w2]
                e2 = EW[:, E[(hh, 1)]:E[(hh, 1)] + w2]
                vop(lambda eng, e1=e1, hh=hh: eng.tensor_scalar(
                    e1, e1, st[:, 36 + 2 * hh:37 + 2 * hh], None, ALU.mult))
                vf = vop(lambda eng, e1=e1, e2=e2, hh=hh: eng.scalar_tensor_tensor(
                    e1, e2, st[:, 37 + 2 * hh:38 + 2 * hh], e1, ALU.mult, ALU.subtract))
                cmb_mark[0] = vf

        def do_dt(j, s, nbm, si):
            w2 = 2 * (s + 1) * P
            for hh in range(2):
                e1 = EW[:, E[(hh, 0)]:E[(hh, 0)] + w2]
                wait("sync", VD, cmb_mark[0])
                dmat_c(EW[:, pt_plane(nbm, hh, 0):pt_plane(nbm, hh, 0) + 2 * nbm * 256]
                       .rearrange("p (x s q) -> p x s q", s=2, q=P)
                       [:, 0:2 * (s + 1), si, :], e1, TpP[hh])

        for j in range(NDC):
            kcb = (0 if j < NDC // 2 else 1) * T
            gA = 2 * (j // (NDC // 2))
            wait("vector", MM)
            v1 = vop(lambda eng, j=j: eng.tensor_scalar(
                sb["bf"][:, 0:TPC], sb["qt"][:, j * TPC:(j + 1) * TPC],
                sm[:, SM_ZAP1:SM_ZAP1 + 1], None, ALU.mult))
            v2 = vop(lambda eng, j=j: eng.tensor_scalar(
                sb["bf"][:, TPC:2 * TPC], sb["qt"][:, j * TPC:(j + 1) * TPC],
                sm[:, SM_ZAP2:SM_ZAP2 + 1], None, ALU.mult))
            wait("tensor", VD, v2)
            for g in range(NT2):
                s0, s1 = 2 * g, 2 * g + 1
                nbm = 2 * g + 2
                do_slot(j, s0, kcb, nbm)
                flush_pv()
                wait("sync", MM)
                do_dt(j, s0, nbm, 0)
                wait("tensor", VD, pv_evac[0])
                do_slot(j, s1, kcb, nbm)
                do_dt(j, s1, nbm, 1)
                pend_pv[0] = (j, g, nbm, gA)
        flush_pv()

    def phase2_rms_wout():
        sm = sb["small"]
        atT = sb["wbf"]
        Wch = chans("p2wo", 2)
        marks = []
        for c in range(NDC):
            sl = c % 2
            stg = sb["stage"][:, sl * D:(sl + 1) * D]
            if c >= 2:
                wait("sync", VD, marks[c - 2])
            dma_c("sync", stg, A["a0_wout"][c * P:(c + 1) * P, :], Wch[sl])
            wait("vector", Wch[sl])
            v = vop(lambda eng, c=c, stg=stg: eng.tensor_scalar(
                sb["kvc"][:, c * D:(c + 1) * D], stg, sm[:, SM_RMSW:SM_RMSW + 1], None, ALU.mult))
            marks.append(v)
        wait("scalar", VD, marks[-1])
        wait("tensor", VD, marks[-1])
        for j in range(NDC):
            sq = sb["stage"][:, 0:TPC]
            s_ = sop(lambda eng, j=j, sq=sq: eng.activation(
                out=sq, in_=atT[:, j * TPC:(j + 1) * TPC], func=ACT.Square))
            wait("tensor", SD, s_)
            wait("tensor", VD)
            for go, gw in groups(TPC):
                top(lambda eng, go=go, gw=gw, sq=sq: eng.matmul(
                    psA[0:2, go:go + gw], sm[:, SM_SEL64:SM_SEL64 + 2],
                    sq[:, go:go + gw], start=True, stop=True), inc=(go + gw == TPC))
            wait("vector", MM)
            v = vop(lambda eng: eng.tensor_scalar(
                sb["rst"][0:2, 0:TPC], psA[0:2, 0:TPC], 1.0 / 64, 1e-6, ALU.mult, ALU.add))
            wait("scalar", VD, v)
            s2 = sop(lambda eng: eng.activation(out=sb["rst"][0:2, 0:TPC],
                                                in_=sb["rst"][0:2, 0:TPC], func=ACT.Sqrt))
            wait("vector", SD, s2)
            sig = sb["stage"][0:2, 2048:2048 + TPC]
            v = vop(lambda eng, sig=sig: eng.reciprocal(sig, sb["rst"][0:2, 0:TPC]))
            wait("tensor", VD, v)
            for go, gw in groups(TPC):
                top(lambda eng, go=go, gw=gw, sig=sig: eng.matmul(
                    psB[:, go:go + gw], sm[0:2, SM_REP64:SM_REP64 + 128], sig[:, go:go + gw],
                    start=True, stop=True), inc=(go + gw == TPC))
            wait("vector", MM)
            vop(lambda eng, j=j: eng.tensor_tensor(
                atT[:, j * TPC:(j + 1) * TPC], atT[:, j * TPC:(j + 1) * TPC],
                psB[:, 0:TPC], ALU.mult))
            wait("tensor", VD)
        So = chans("p2so", 2)
        for s in range(NT):
            wait("tensor", VD)
            for oh in range(D // 512):
                for c in range(NDC):
                    top(lambda eng, c=c, s=s, oh=oh: eng.matmul(
                        psA[:, oh * 512:(oh + 1) * 512],
                        atT[:, c * TPC + s * P: c * TPC + (s + 1) * P],
                        sb["kvc"][:, c * D + oh * 512: c * D + (oh + 1) * 512],
                        start=(c == 0), stop=(c == NDC - 1)),
                        inc=(c == NDC - 1 and oh == D // 512 - 1))
            wait("vector", MM)
            sl = s % 2
            out = sb["stage"][:, (2 + sl) * D:(3 + sl) * D]
            if s >= 2:
                wait("vector", So[sl])
            v = vop(lambda eng, out=out: eng.tensor_copy(out, psA[:, 0:D]))
            wait("sync", VD, v)
            dma_c("sync", A["att0_sp"][s * P:(s + 1) * P, :], out, So[sl])

    phase2_setup()
    phase2_proj()
    ctx.barrier("p2a")
    phase2_bounds()
    ctx.barrier("p2b")
    phase2_attn()
    ctx.barrier("p2c")
    phase2_rms_wout()
    ctx.barrier("p2")

    SC3 = float(64.0 ** -0.5)
    EPSR = 1.1920929e-07

    def phase3():
        sm = sb["small"]
        st = sb["stat"]
        EW = sb["xnT"]
        XL = psem("p3xl")
        wait("sync", SD)
        dma_c("sync", EW[:, 0:NDC * T], A["xnT_sp"][:], XL)
        wait("tensor", XL)
        C3 = psem("p3c")
        n1c = st[:, 1280:1288]
        for c in range(NDC):
            dma_c("sync", n1c[:, c:c + 1], A["f0_n1"][0:1, c * P:(c + 1) * P], C3)
        dma_c("sync", sb["rst"][0:1, 0:D], A["f0_kn_t"], C3)
        dma_c("sync", sb["rst"][32:33, 0:D], A["f0_qn_t"], C3)
        wait("tensor", C3)
        knrep = sb["rst"][:, 0:D]
        qnrep = sb["rst"][:, D:2 * D]
        for go, gw in groups(D):
            top(lambda eng, go=go, gw=gw: eng.matmul(
                psA[:, go:go + gw], sm[0:1, SM_ONESROW:SM_ONESROW + 128],
                sb["rst"][0:1, go:go + gw], start=True, stop=True), inc=(go + gw == D))
            top(lambda eng, go=go, gw=gw: eng.matmul(
                psB[:, go:go + gw], sm[32:33, SM_ONESROW:SM_ONESROW + 128],
                sb["rst"][32:33, go:go + gw], start=True, stop=True), inc=(go + gw == D))
        wait("vector", MM)
        vop(lambda eng: eng.tensor_copy(knrep, psA[:, 0:D]))
        vop(lambda eng: eng.tensor_copy(qnrep, psB[:, 0:D]))
        v_ = vop(lambda eng: eng.memset(st[:, 1288:1304], -1e30))
        wait("tensor", VD, v_)

        Wch = chans("p3w", 2)
        def proj_tiles(wbase, n_tiles, kind):
            KT = chans("p3kt" + kind, 2)
            SPL = chans("p3sp" + kind, 2)
            for chp in list(ctx.dma_seen):
                wait("vector", chp)
                wait("scalar", chp)
            nrep = knrep if kind == "k" else qnrep
            for tt in range(n_tiles):
                rcol = st[:, 448 + tt:448 + tt + 1]
                wait("tensor", VD)
                for oh in range(2):
                    for c in range(NDC):
                        top(lambda eng, c=c, tt=tt, oh=oh, wbase=wbase: eng.matmul(
                            psA[:, oh * 512:(oh + 1) * 512],
                            EW[:, c * T + tt * P:c * T + (tt + 1) * P],
                            sb["wbf"][:, wbase + c * D + oh * 512: wbase + c * D + (oh + 1) * 512],
                            start=(c == 0), stop=(c == NDC - 1)),
                            inc=(c == NDC - 1 and oh == 1))
                wait("vector", MM)
                sl = tt % 2
                if kind == "v":
                    vb = sb["bf"][:, (1 + sl) * D:(2 + sl) * D]
                    if tt >= 2:
                        wait("vector", SPL[sl])
                    v = vop(lambda eng, vb=vb, rcol=rcol: eng.tensor_scalar(
                        vb, psA[:, 0:D], rcol, None, ALU.mult))
                    wait("sync", VD, v)
                    dma_c("sync", A["v0f_sp"][:, tt * D:(tt + 1) * D], vb, SPL[sl])
                    continue
                kt = sb["stage"][:, 2 * D:3 * D]
                v = vop(lambda eng, kt=kt, rcol=rcol: eng.tensor_scalar(
                    kt, psA[:, 0:D], rcol, None, ALU.mult))
                wait("scalar", VD, v)
                sq = sb["stage"][:, 3 * D:4 * D]
                s_ = sop(lambda eng, sq=sq, kt=kt: eng.activation(out=sq, in_=kt, func=ACT.Square))
                wait("vector", SD, s_)
                msq = st[:, 0:16]
                vop(lambda eng, sq=sq, msq=msq: eng.reduce_sum(
                    msq, sq[:].rearrange("p (h d) -> p h d", d=64), AX))
                v2 = vop(lambda eng, msq=msq: eng.tensor_scalar(
                    msq, msq, 1.0 / 64, EPSR, ALU.mult, ALU.add))
                wait("scalar", VD, v2)
                s2 = sop(lambda eng, msq=msq: eng.activation(out=st[:, 16:32], in_=msq, func=ACT.Sqrt))
                wait("vector", SD, s2)
                vop(lambda eng: eng.reciprocal(st[:, 32:48], st[:, 16:32]))
                kb = sb["bf"][:, (1 + sl) * D:(2 + sl) * D]
                if tt >= 2:
                    wait("vector", KT[sl])
                vop(lambda eng, kt=kt: eng.tensor_tensor(
                    kt[:].rearrange("p (h d) -> p h d", d=64), kt[:].rearrange("p (h d) -> p h d", d=64),
                    bcast64(st[:, 32:48]), ALU.mult))
                v3 = vop(lambda eng, kb=kb, kt=kt, nrep=nrep: eng.tensor_tensor(kb, kt, nrep, ALU.mult))
                wait("scalar", VD, v3)
                s3 = sop(lambda eng, sq=sq, kb=kb: eng.activation(out=sq, in_=kb, func=ACT.Square))
                wait("vector", SD, s3)
                vop(lambda eng, sq=sq: eng.reduce_sum(
                    st[:, 48:64], sq[:].rearrange("p (h d) -> p h d", d=64), AX))
                if kind == "k":
                    vop(lambda eng: eng.tensor_tensor(
                        st[:, 1288:1304], st[:, 1288:1304], st[:, 48:64], ALU.max))
                else:
                    vop(lambda eng, tt=tt: eng.tensor_copy(st[:, 768 + 16 * tt:768 + 16 * (tt + 1)],
                                                           st[:, 48:64]))
                if kind == "k":
                    wait("sync", VD)
                    dmat_c(sb["bf"][:, 3 * D:4 * D].rearrange("p (c t) -> p c t", t=P), kb, KT[sl])
                    wait("sync", KT[sl])
                    dma_c("sync", A["k0f_sp"][:].rearrange("p (c t) -> p c t", t=T)[:, :, tt * P:(tt + 1) * P],
                          sb["bf"][:, 3 * D:4 * D].rearrange("p (c t) -> p c t", t=P), SPL[sl])
                    wait("sync", SPL[sl])
                else:
                    wait("sync", VD)
                    dmat_c(t_dst(sb["qt"], tt, tot=TPC), kb, KT[sl])

        stream_weight(A["f0_wk"], NDC, D, 0, Wch, fold=lambda c: n1c[:, c:c + 1])
        stream_weight(A["f0_wv"], NDC, D, 8192, Wch, fold=lambda c: n1c[:, c:c + 1])
        proj_tiles(0, NKC, "k")
        proj_tiles(8192, NKC, "v")
        wait("sync", MM)
        wait("sync", VD)
        stream_weight(A["f0_wq"], NDC, D, 0, Wch, fold=lambda c: n1c[:, c:c + 1])
        proj_tiles(0, NT, "q")
        ctx.barrier("p3proj")

    def phase3_bounds():
        sm = sb["small"]
        st = sb["stat"]
        pad = sb["stage"][:, 0:P]
        v = vop(lambda eng: eng.memset(pad, 0.0))
        v = vop(lambda eng, pad=pad: eng.tensor_copy(pad[:, 0:16], st[:, 1288:1304]))
        ID = psem("p3id")
        dma_c("sync", sb["rst"][:, 0:P], A["ident"], ID)
        wait("tensor", ID)
        wait("tensor", VD, v)
        m = top(lambda eng, pad=pad: eng.transpose(psA[:, 0:P], pad, sb["rst"][:, 0:P]))
        wait("vector", MM, m)
        v = vop(lambda eng: eng.reduce_max(st[0:16, 0:1], psA[0:16, 0:P], AX))
        v = vop(lambda eng, pad=pad: eng.memset(pad, 0.0))
        v = vop(lambda eng, pad=pad: eng.tensor_copy(pad[0:16, 0:1], st[0:16, 0:1]))
        wait("tensor", VD, v)
        m = top(lambda eng, pad=pad: eng.transpose(psB[:, 0:P], pad, sb["rst"][:, 0:P]))
        wait("vector", MM, m)
        kmrow = sb["rst"][0:1, 512:528]
        v = vop(lambda eng, kmrow=kmrow: eng.tensor_copy(kmrow, psB[0:1, 0:16]))
        wait("tensor", VD, v)
        m = top(lambda eng, kmrow=kmrow: eng.matmul(
            psA[:, 512:512 + 16], sm[0:1, SM_ONESROW:SM_ONESROW + 128],
            kmrow, start=True, stop=True))
        wait("vector", MM, m)
        kmrep = st[:, 0:16]
        v = vop(lambda eng, kmrep=kmrep: eng.tensor_copy(kmrep, psA[:, 512:512 + 16]))
        for tt in range(NT):
            v = vop(lambda eng, tt=tt, kmrep=kmrep: eng.tensor_tensor(
                st[:, 1024 + 16 * tt:1024 + 16 * (tt + 1)],
                st[:, 768 + 16 * tt:768 + 16 * (tt + 1)], kmrep, ALU.mult))
        wait("scalar", VD, v)
        s_ = sop(lambda eng: eng.activation(out=st[:, 1024:1024 + 16 * NT],
                                            in_=st[:, 1024:1024 + 16 * NT],
                                            func=ACT.Sqrt, scale=SC3 * SC3))
        wait("vector", SD, s_)
        vop(lambda eng: eng.tensor_scalar(st[:, 1024:1024 + 16 * NT],
                                          st[:, 1024:1024 + 16 * NT], -1.0, None, ALU.mult))

    def phase3_attn():
        st = sb["stat"]
        sm = sb["small"]
        EW = sb["xnT"]
        KSb, Eb, PTB, Vb = 0, T, 3 * T, 7 * T
        KL = chans("p3kl", 2)
        VL = chans("p3vl", 2)
        TpP = chans("p3pt", 2)
        NT2 = NT // 2
        wait("scalar", LD)
        wait("vector", LD)
        pend_pv = [None]
        bank_sd = {0: 0, 1: 0}
        pv_evac = [0]
        cmb_mark = [0]

        def pt_plane(nbm, hh, x):
            return PTB + (hh * 2 * nbm + x) * 256

        def flush_pv():
            if pend_pv[0] is None:
                return
            j, g, nbm = pend_pv[0]
            pend_pv[0] = None
            zm = 0
            for hh in range(2):
                base = pt_plane(nbm, hh, 2 * nbm - 2)
                zm = vop(lambda eng, base=base: eng.memset(
                    EW[:, base:base + 512].rearrange("p (x s q) -> p x s q", s=2, q=P)
                    [:, :, 0, :], 0.0))
            wait("tensor", VD, zm)
            wait("tensor", VD, pv_evac[0])
            wait("tensor", SD, bank_sd[1])
            wait("tensor", TpP[0])
            wait("tensor", TpP[1])
            wait("tensor", VL[0])
            for x in range(2 * nbm):
                bb, r = x >> 1, x & 1
                tile = bb if r == 0 else NT + bb
                for hh in range(2):
                    ps = psA if hh == 0 else psB
                    top(lambda eng, x=x, tile=tile, hh=hh, nbm=nbm, ps=ps:
                        eng.matmul(ps[64 * hh:64 * hh + 64, 1536:1536 + 256],
                                   EW[:, Vb + tile * P + 64 * hh: Vb + tile * P + 64 * hh + 64],
                                   EW[:, pt_plane(nbm, hh, x):pt_plane(nbm, hh, x) + 256],
                                   start=(x == 0), stop=(x == 2 * nbm - 1),
                                   tile_position=(0, 64 * hh), skip_group_check=True),
                        inc=(x == 2 * nbm - 1 and hh == 1))
            wait("vector", MM)
            vop(lambda eng, j=j, g=g: eng.tensor_copy(
                sb["wbf"][0:64, j * TPC + 2 * g * P: j * TPC + (2 * g + 2) * P],
                psA[0:64, 1536:1536 + 256]))
            pv_evac[0] = vop(lambda eng, j=j, g=g: eng.tensor_copy(
                sb["wbf"][64:128, j * TPC + 2 * g * P: j * TPC + (2 * g + 2) * P],
                psB[64:128, 1536:1536 + 256]))

        def do_slot(j, s, jq):
            nb = s + 1
            w = nb * P
            grps = groups(w, 1024)
            ngr = len(grps)
            segs = [(run, go, gw) for run in range(2) for (go, gw) in grps]
            first_exp = True
            for i, (run, go, gw) in enumerate(segs):
                bcol = (i % 2) * 1024
                tof = 0 if run == 0 else NT
                gi = grps.index((go, gw))
                if bank_sd[i % 2]:
                    wait("tensor", SD, bank_sd[i % 2])
                m = None
                for qo, qw in groups(gw):
                    top(lambda eng, go=go, qo=qo, qw=qw, s=s, tof=tof, jq=jq, bcol=bcol:
                        eng.matmul(psA[:, bcol + qo:bcol + qo + qw],
                                   sb["qt"][0:64, jq + s * P: jq + (s + 1) * P],
                                   EW[0:64, KSb + tof * P + go + qo:
                                      KSb + tof * P + go + qo + qw],
                                   start=True, stop=True, tile_position=(0, 0)),
                        inc=False)
                    m = top(lambda eng, go=go, qo=qo, qw=qw, s=s, tof=tof, jq=jq, bcol=bcol:
                            eng.matmul(psB[:, bcol + qo:bcol + qo + qw],
                                       sb["qt"][64:128, jq + s * P: jq + (s + 1) * P],
                                       EW[64:128, KSb + tof * P + go + qo:
                                          KSb + tof * P + go + qo + qw],
                                       start=True, stop=True, tile_position=(64, 0)),
                            inc=True)
                has_diag = go <= s * P < go + gw
                if has_diag:
                    dcol = bcol + s * P - go
                    dm = sm[:, SM_DM0:SM_DM0 + 128] if run == 0 else sm[:, SM_DM1:SM_DM1 + 128]
                    wait("vector", MM, m)
                    va = vop(lambda eng, dcol=dcol, dm=dm: eng.tensor_tensor(
                        psA[:, dcol:dcol + P], psA[:, dcol:dcol + P], dm, ALU.add))
                    vb_ = vop(lambda eng, dcol=dcol, dm=dm: eng.tensor_tensor(
                        psB[:, dcol:dcol + P], psB[:, dcol:dcol + P], dm, ALU.add))
                if first_exp:
                    wait("scalar", VD, cmb_mark[0])
                    wait("scalar", TpP[0])
                    wait("scalar", TpP[1])
                    first_exp = False
                if has_diag:
                    wait("scalar", VD, vb_)
                else:
                    wait("scalar", MM, m)
                for hh, ps in ((0, psA), (1, psB)):
                    bias = st[:, 1024 + s * 16 + 2 * j + hh: 1024 + s * 16 + 2 * j + hh + 1]
                    zcol = st[:, 48 + 4 * hh + run * ngr + gi:49 + 4 * hh + run * ngr + gi]
                    e_ = sop(lambda eng, go=go, gw=gw, w=w, run=run, hh=hh,
                             bias=bias, zcol=zcol, ps=ps, bcol=bcol:
                             eng.activation(
                                 out=EW[:, Eb + hh * T:Eb + hh * T + 2 * w]
                                 .rearrange("p (b r q) -> p b r q", r=2, q=P)
                                 [:, go // P:(go + gw) // P, run, :],
                                 in_=ps[:, bcol:bcol + gw], func=ACT.Exp,
                                 bias=bias, scale=SC3, accum_out=zcol))
                bank_sd[i % 2] = SD.n
            wait("vector", SD, bank_sd[(len(segs) - 1) % 2])
            for hh in range(2):
                vop(lambda eng, hh=hh, ngr=ngr: eng.reduce_sum(
                    st[:, 56 + hh:57 + hh], st[:, 48 + 4 * hh:48 + 4 * hh + 2 * ngr], AX))
            vop(lambda eng: eng.reciprocal(st[:, 58:60], st[:, 56:58]))
            for hh in range(2):
                ehh = EW[:, Eb + hh * T:Eb + hh * T + 2 * w]
                vf = vop(lambda eng, ehh=ehh, hh=hh: eng.tensor_scalar(
                    ehh, ehh, st[:, 58 + hh:59 + hh], None, ALU.mult))
                cmb_mark[0] = vf

        def do_dt(s, nbm, si):
            w2 = 2 * (s + 1) * P
            for hh in range(2):
                ehh = EW[:, Eb + hh * T:Eb + hh * T + w2]
                wait("sync", VD, cmb_mark[0])
                wait("sync", MM)
                dmat_c(EW[:, pt_plane(nbm, hh, 0):pt_plane(nbm, hh, 0) + 2 * nbm * 256]
                       .rearrange("p (x s q) -> p x s q", s=2, q=P)
                       [:, 0:2 * (s + 1), si, :], ehh, TpP[hh])

        for j in range(NDC):
            jq = j * TPC
            wait("sync", VD)
            wait("sync", SD)
            wait("sync", MM)
            wait("sync", KL[0])
            wait("sync", VL[0])
            dma_c("sync", EW[:, KSb:KSb + T],
                  A["k0f_sp"][:].rearrange("p (c t) -> p c t", t=T)[:, j, :], KL[0])
            dma_c("sync", EW[:, Vb:Vb + NKC * P],
                  A["v0f_sp"][:].rearrange("p (t d) -> p t d", d=D)[:, :, j * P:(j + 1) * P], VL[0])
            wait("tensor", KL[0])
            for g in range(NT2):
                s0, s1 = 2 * g, 2 * g + 1
                nbm = 2 * g + 2
                do_slot(j, s0, jq)
                flush_pv()
                do_dt(s0, nbm, 0)
                wait("tensor", VD, pv_evac[0])
                do_slot(j, s1, jq)
                do_dt(s1, nbm, 1)
                pend_pv[0] = (j, g, nbm)
            flush_pv()

    def phase3_wo_xmid():
        st = sb["stat"]
        atT = sb["wbf"]
        Wch = chans("p3wo", 2)
        XL2 = chans("p3xn", 2)
        SPm = chans("p3xm", 2)
        marks = []
        wait("sync", VD)
        for c in range(NDC):
            sl = c % 2
            stg = sb["stage"][:, sl * D:sl * D + 768]
            if c >= 2:
                wait("sync", VD, marks[c - 2])
            dma_c("sync", stg, A["f0_wo"][c * P:(c + 1) * P, 256:1024], Wch[sl])
            wait("vector", Wch[sl])
            v = vop(lambda eng, c=c, stg=stg: eng.tensor_copy(
                sb["kvc"][:, c * 768:(c + 1) * 768], stg))
            marks.append(v)
        wait("tensor", VD, marks[-1])
        wait("sync", VD, marks[-1])
        XL4 = chans("p3x4", 4)
        SP4 = chans("p3m4", 4)
        add_marks = []
        for s in range(NT):
            sl = s % 4
            ps = psA if s % 2 == 0 else psB
            xnl = sb["stage"][:, sl * D:(sl + 1) * D]
            if s >= 4:
                wait("sync", VD, add_marks[s - 4])
                wait("sync", SP4[sl])
            dma_c("sync", xnl, A["xn_sp"][s * P:(s + 1) * P, :], XL4[sl])
            if s >= 2:
                wait("tensor", VD, add_marks[s - 2])
            for oh, (oo, ow) in enumerate(groups(768)):
                for c in range(NDC):
                    top(lambda eng, c=c, s=s, oo=oo, ow=ow, ps=ps: eng.matmul(
                        ps[:, oo:oo + ow],
                        atT[:, c * TPC + s * P: c * TPC + (s + 1) * P],
                        sb["kvc"][:, c * 768 + oo: c * 768 + oo + ow],
                        start=(c == 0), stop=(c == NDC - 1)),
                        inc=(c == NDC - 1 and oo + ow == 768))
            wait("vector", XL4[sl])
            wait("vector", MM)
            v = vop(lambda eng, xnl=xnl, ps=ps: eng.tensor_tensor(
                xnl[:, 256:1024], xnl[:, 256:1024], ps[:, 0:768], ALU.add))
            add_marks.append(v)
            wait("sync", VD, v)
            dma_c("sync", A["xmid_sp"][s * P:(s + 1) * P, :], xnl, SP4[sl])

    xm_marks = []
    phase3()
    ctx.barrier("p3a")
    phase3_bounds()
    ctx.barrier("p3b")
    phase3_attn()
    ctx.barrier("p3c")
    phase3_wo_xmid()
    ctx.barrier("p3")

    EXPERT_W = [
        ("f0_e0_up", D, 2048, False), ("f0_e0_down", 2048, 768, True),
        ("f0_e1_w1", D, 2048, False), ("f0_e1_w2", D, 2048, False),
        ("f0_e1_down", 2048, 768, True),
        ("f0_e2_l1", D, 2048, False), ("f0_e2_l2", 2048, 2048, False),
        ("f0_e2_l3", 2048, 2048, False), ("f0_e2_l4", 2048, 768, True),
        ("f0_e3_down", D, D, False), ("f0_e3_up", D, 2048, False),
        ("f0_e3_out", 2048, 768, True),
    ]

    def phase4():
        st = sb["stat"]
        sm = sb["small"]
        C4 = psem("p4c")
        dma_c("sync", sb["rst"][0:1, 0:D], A["f0_n2"], C4)
        gstage = sb["stage"][:, 0:32]
        for c in range(NDC):
            dma_c("sync", gstage[:, 4 * c:4 * (c + 1)], A["f0_gate"][c * P:(c + 1) * P, :], C4)
        wait("vector", C4)
        gbf = sb["bf"][:, 3 * D:3 * D + 32]
        vop(lambda eng, gbf=gbf, gstage=gstage: eng.tensor_copy(gbf, gstage))
        wait("tensor", C4)
        n2rep = sb["rst"][:, D:2 * D]
        for go, gw in groups(D):
            top(lambda eng, go=go, gw=gw: eng.matmul(
                psB[:, go:go + gw], sm[0:1, SM_ONESROW:SM_ONESROW + 128],
                sb["rst"][0:1, go:go + gw], start=True, stop=True), inc=(go + gw == D))
        wait("vector", MM)
        vn2 = vop(lambda eng, n2rep=n2rep: eng.tensor_copy(n2rep, psB[:, 0:D]))
        wait("sync", VD)
        XM = chans("p4xm", 2)
        XT4 = chans("p4xt", 2)
        XS = chans("p4a", 2)
        XS2 = chans("p4b", 2)
        SPX = chans("p4sp", 2)
        for s_ in range(NT):
            sl = s_ % 2
            xm = sb["stage"][:, sl * D:(sl + 1) * D]
            if s_ >= 2:
                wait("sync", VD, p4x_marks[s_ - 2])
            dma_c("sync", xm, A["xmid_sp"][s_ * P:(s_ + 1) * P, :], XM[sl])
            wait("scalar", XM[sl])
            wait("scalar", VD)
            sq = sop(lambda eng, xm=xm, s_=s_, d=sb["bf"][:, (1 + sl) * D:(2 + sl) * D]:
                     eng.activation(out=d, in_=xm, func=ACT.Square,
                                    accum_out=st[:, 576 + s_:577 + s_]))
            wait("vector", SD, sq)
            rc = st[:, 592 + s_:593 + s_]
            vop(lambda eng, s_=s_, rc=rc: eng.tensor_scalar(
                rc, st[:, 576 + s_:577 + s_], 1.0 / D, EPSR, ALU.mult, ALU.add))
            v2 = vop(lambda eng, rc=rc: eng.tensor_scalar(rc, rc, 1.0, None, ALU.mult))
            wait("scalar", VD, v2)
            s2 = sop(lambda eng, rc=rc, s_=s_: eng.activation(
                out=st[:, 608 + s_:609 + s_], in_=rc, func=ACT.Sqrt))
            wait("vector", SD, s2)
            vop(lambda eng, rc=rc, s_=s_: eng.reciprocal(rc, st[:, 608 + s_:609 + s_]))
            xfb = sb["bf"][:, (1 + sl) * D:(2 + sl) * D]
            if s_ >= 2:
                wait("vector", XT4[sl])
            vop(lambda eng, xm=xm, rc=rc: eng.tensor_scalar(xm, xm, rc, None, ALU.mult))
            vf = vop(lambda eng, xfb=xfb, xm=xm, n2rep=n2rep: eng.tensor_tensor(xfb, xm, n2rep, ALU.mult))
            p4_marks.append(vf)
            wait("sync", VD, vf)
            dmat_c(t_dst(sb["qt"], s_, tot=TPC), xfb, XT4[sl])
            wait("tensor", XT4[sl])
            wait("tensor", VD, vf)
            if lg_marks:
                wait("tensor", VD, lg_marks[-1])
            for c in range(NDC):
                top(lambda eng, c=c, s_=s_: eng.matmul(
                    psA[:, 1024:1024 + 4],
                    sb["qt"][:, c * TPC + s_ * P:c * TPC + (s_ + 1) * P],
                    gbf[:, 4 * c:4 * (c + 1)],
                    start=(c == 0), stop=(c == NDC - 1)), inc=(c == NDC - 1))
            wait("vector", MM)
            vlg = vop(lambda eng, s_=s_: eng.tensor_copy(
                st[:, 128 + 4 * s_:132 + 4 * s_], psA[:, 1024:1028]))
            lg_marks.append(vlg)
            xs = sb["stage"][:, (2 + sl) * D:(3 + sl) * D]
            if s_ >= 2:
                wait("sync", VD, p4x_marks[s_ - 2])
                wait("sync", SPX[sl])
            dma_c("sync", xs, A["x_own"][s_ * P:(s_ + 1) * P, :], XS[sl])
            wait("sync", VD, vf)
            wait("sync", SD)
            dma_c("sync", xm, A["att0_sp"][s_ * P:(s_ + 1) * P, :], XS2[sl])
            wait("vector", XS[sl])
            wait("vector", XS2[sl])
            v9 = vop(lambda eng, xs=xs, xm=xm: eng.tensor_tensor(xs, xs, xm, ALU.add))
            wait("sync", VD, v9)
            dma_c("sync", xm, A["xmid_sp"][s_ * P:(s_ + 1) * P, :], XS2[sl])
            wait("vector", XS2[sl])
            if s_ >= 2:
                wait("vector", SPX[sl])
            va = vop(lambda eng, xs=xs, xm=xm: eng.tensor_tensor(xs, xs, xm, ALU.add))
            p4x_marks.append(va)
            wait("sync", VD, va)
            dma_c("sync", A["xsum_sp"][s_ * P:(s_ + 1) * P, :], xs, SPX[sl])
        lg = st[:, 128:128 + 4 * NT]
        l3 = lg.rearrange("p (s e) -> p s e", e=4)
        mx = st[:, 320:320 + NT]
        mxb = mx.unsqueeze(2).broadcast_to([P, NT, 4])
        vop(lambda eng: eng.reduce_max(mx, l3, AX))
        v = vop(lambda eng: eng.tensor_tensor(l3, l3, mxb, ALU.subtract))
        wait("scalar", VD, v)
        eb = st[:, 208:208 + 4 * NT]
        eb3 = eb.rearrange("p (s e) -> p s e", e=4)
        se = sop(lambda eng: eng.activation(out=eb, in_=lg, func=ACT.Exp))
        wait("vector", SD, se)
        zs = st[:, 336:336 + NT]
        rz = st[:, 352:352 + NT]
        rzb = rz.unsqueeze(2).broadcast_to([P, NT, 4])
        vop(lambda eng: eng.reduce_sum(zs, eb3, AX))
        vop(lambda eng: eng.reciprocal(rz, zs))
        vop(lambda eng: eng.tensor_tensor(eb3, eb3, rzb, ALU.mult))
        cnt = st[:, 368:368 + 4 * NT]
        cnt3 = cnt.rearrange("p (s e) -> p s e", e=4)
        tmp = st[:, 432:432 + NT]
        for e in range(4):
            js = [jj for jj in range(4) if jj != e]
            ce = cnt3[:, :, e:e + 1]
            pe_ = eb3[:, :, e:e + 1]
            vop(lambda eng, ce=ce, pe_=pe_, j0=js[0]: eng.tensor_tensor(
                ce, eb3[:, :, j0:j0 + 1], pe_, ALU.is_gt))
            for j2 in js[1:]:
                vop(lambda eng, pe_=pe_, j2=j2: eng.tensor_tensor(
                    tmp.unsqueeze(2), eb3[:, :, j2:j2 + 1], pe_, ALU.is_gt))
                vop(lambda eng, ce=ce: eng.tensor_tensor(
                    ce, ce, tmp.unsqueeze(2), ALU.add))
        vop(lambda eng: eng.tensor_scalar(cnt, cnt, 1.0, None, ALU.is_le))
        vop(lambda eng: eng.tensor_tensor(cnt3, cnt3, eb3, ALU.mult))
        sw = st[:, 336:336 + NT]
        rw = st[:, 352:352 + NT]
        rwb = rw.unsqueeze(2).broadcast_to([P, NT, 4])
        vop(lambda eng: eng.reduce_sum(sw, cnt3, AX))
        vop(lambda eng: eng.tensor_scalar(sw, sw, 1e-8, None, ALU.add))
        vop(lambda eng: eng.reciprocal(rw, sw))
        W3 = st[:, 512:512 + 4 * NT].rearrange("p (s e) -> p s e", e=4)
        vop(lambda eng: eng.tensor_tensor(W3, cnt3, rwb, ALU.mult))

    def phase5():
        st = sb["stat"]
        sm = sb["small"]
        EW = sb["xnT"]
        GW = min(512, TPC)
        ngrp = TPC // GW
        ntg = GW // P
        WL = chans("p5w", 4)
        AL = chans("p5a", 2)
        SP5 = chans("p5s", 2)
        PSP = [ctx.sem("p5psA"), ctx.sem("p5psB")]
        RLD = chans("p5rl", 2)
        W_INFO = {name: (din, dout, isdown) for name, din, dout, isdown in EXPERT_W}
        spill_rec = {}

        p5_cast_marks = []
        psp_hist = []

        def issue_load(name, co, cw, dst_base, grp):
            din, dout, isdown = W_INFO[name]
            nch = din // P
            wait("sync", MM)
            if grp == 0:
                wi = len(psp_hist)
                if wi >= 2:
                    wait("sync", PSP[0], psp_hist[wi - 2][0])
                    wait("sync", PSP[1], psp_hist[wi - 2][1])
            else:
                wait("sync", PSP[0])
                wait("sync", PSP[1])
            if grp == 0:
                pw = min(cw, 1024)
                npc = cw // pw
                for c in range(nch):
                    for pp in range(npc):
                        gi = len(p5_cast_marks)
                        sl = gi % 2
                        stg = sb["rst"][:, sl * 1024: sl * 1024 + pw]
                        if gi >= 2:
                            wait("sync", VD, p5_cast_marks[gi - 2])
                        src_co = (256 if isdown else 0) + co + pp * pw
                        dma_c("sync", stg, A[name][c * P:(c + 1) * P, src_co:src_co + pw],
                              RLD[sl])
                        wait("vector", RLD[sl])
                        v = vop(lambda eng, stg=stg, c=c, pp=pp, pw=pw,
                                dst_base=dst_base, cw=cw:
                                eng.tensor_copy(EW[:, dst_base + c * cw + pp * pw:
                                                   dst_base + c * cw + (pp + 1) * pw], stg))
                        p5_cast_marks.append(v)
                wait("gpsimd", VD, p5_cast_marks[-1])
                for c in range(nch):
                    wait("gpsimd", PSP[c % 2])
                    dma_c("gpsimd", A["ew_sp"][name][c * P:(c + 1) * P, co:co + cw],
                          EW[:, dst_base + c * cw: dst_base + (c + 1) * cw], PSP[c % 2])
                psp_hist.append((PSP[0].n, PSP[1].n))
                return ("vd", p5_cast_marks[-1])
            else:
                pr = 2 * ((dst_base != 0) + 0)
                for c in range(nch):
                    dma_c("sync", EW[:, dst_base + c * cw: dst_base + (c + 1) * cw],
                          A["ew_sp"][name][c * P:(c + 1) * P, co:co + cw], WL[pr + c % 2])
                return ("ch", pr, WL[pr].n, WL[pr + 1].n)

        def pe_wait(spec):
            if spec[0] == "vd":
                wait("tensor", VD, spec[1])
            else:
                pr = spec[1]
                wait("tensor", WL[pr], spec[2])
                wait("tensor", WL[pr + 1], spec[3])

        p5vd = [0]

        def link(src_buf, src_col, din, dout, wbase, act, dst_buf, dst_base):
            nch = din // P
            noc = dout // P
            wait("tensor", VD, p5vd[0])
            for oc in range(noc):
                wait("tensor", SD)
                ps = psA if (oc % 2 == 0) else psB
                for c in range(nch):
                    top(lambda eng, c=c, oc=oc, ps=ps, sc=src_col(c): eng.matmul(
                        ps[:, 0:GW],
                        EW[:, wbase + c * dout + oc * P: wbase + c * dout + (oc + 1) * P],
                        src_buf[:, sc:sc + GW],
                        start=(c == 0), stop=(c == nch - 1)), inc=(c == nch - 1))
                wait("scalar", MM)
                fn = {"gelu": ACT.Gelu, "silu": ACT.Silu, None: ACT.Copy}[act]
                sop(lambda eng, oc=oc, ps=ps, fn=fn: eng.activation(
                    out=dst_buf[:, dst_base + oc * GW: dst_base + (oc + 1) * GW],
                    in_=ps[:, 0:GW], func=fn))

        def down(src_buf, src_col, din, wbase, e_idx, grp, acc):
            nch = din // P
            wait("tensor", SD)
            for tg in range(ntg):
                wait("tensor", VD, p5vd[0])
                for oh, (oo, ow) in enumerate(groups(768)):
                    for c in range(nch):
                        top(lambda eng, c=c, tg=tg, oo=oo, ow=ow, sc=src_col(c): eng.matmul(
                            psA[:, oo:oo + ow],
                            src_buf[:, sc + tg * P: sc + (tg + 1) * P],
                            EW[:, wbase + c * 768 + oo: wbase + c * 768 + oo + ow],
                            start=(c == 0), stop=(c == nch - 1)),
                            inc=(c == nch - 1 and oo + ow == 768))
                wait("vector", MM)
                wcol = st[:, 512 + 4 * (grp * ntg + tg) + e_idx: 512 + 4 * (grp * ntg + tg) + e_idx + 1]
                p5vd[0] = vop(lambda eng, tg=tg, wcol=wcol, acc=acc: eng.scalar_tensor_tensor(
                    acc[:, tg * 768:(tg + 1) * 768], psA[:, 0:768], wcol,
                    acc[:, tg * 768:(tg + 1) * 768], ALU.mult, ALU.add))

        acc = sb["stage"][:, 0:ntg * 768]
        HB1, HB2 = 0, 8192
        BHALF = 16384
        wslot = [0]

        def mul_e1():
            wait("vector", SD)
            vm = 0
            for c in range(2048 // P):
                vm = vop(lambda eng, c=c: eng.tensor_tensor(
                    sb["wbf"][:, HB1 + c * GW:HB1 + (c + 1) * GW],
                    sb["wbf"][:, HB1 + c * GW:HB1 + (c + 1) * GW],
                    sb["wbf"][:, HB2 + c * GW:HB2 + (c + 1) * GW], ALU.mult))
            wait("tensor", VD, vm)

        def steps_for(grp, gq):
            xf_col = lambda c, gq=gq: c * TPC + gq
            hb1_col = lambda c: HB1 + c * GW
            hb2_col = lambda c: HB2 + c * GW
            kv_col = lambda c: c * GW
            return [
                ("f0_e0_up", 0, 2048, lambda b: link(
                    sb["qt"], xf_col, D, 2048, b, "gelu", sb["wbf"], HB1)),
                ("f0_e0_down", 0, 768, lambda b: down(
                    sb["wbf"], hb1_col, 2048, b, 0, grp, acc)),
                ("f0_e1_w1", 0, 2048, lambda b: link(
                    sb["qt"], xf_col, D, 2048, b, "silu", sb["wbf"], HB1)),
                ("f0_e1_w2", 0, 2048, lambda b: link(
                    sb["qt"], xf_col, D, 2048, b, None, sb["wbf"], HB2)),
                ("f0_e1_down", 0, 768, lambda b: (mul_e1(), down(
                    sb["wbf"], hb1_col, 2048, b, 1, grp, acc))),
                ("f0_e2_l1", 0, 2048, lambda b: link(
                    sb["qt"], xf_col, D, 2048, b, "gelu", sb["wbf"], HB1)),
                ("f0_e2_l2", 0, 1024, lambda b: link(
                    sb["wbf"], hb1_col, 2048, 1024, b, "gelu", sb["kvc"], 0)),
                ("f0_e2_l2", 1024, 1024, lambda b: link(
                    sb["wbf"], hb1_col, 2048, 1024, b, "gelu", sb["kvc"], 8 * GW)),
                ("f0_e2_l3", 0, 1024, lambda b: link(
                    sb["kvc"], kv_col, 2048, 1024, b, "gelu", sb["wbf"], HB1)),
                ("f0_e2_l3", 1024, 1024, lambda b: link(
                    sb["kvc"], kv_col, 2048, 1024, b, "gelu", sb["wbf"], HB1 + 8 * GW)),
                ("f0_e2_l4", 0, 768, lambda b: down(
                    sb["wbf"], hb1_col, 2048, b, 2, grp, acc)),
                ("f0_e3_down", 0, 1024, lambda b: link(
                    sb["qt"], xf_col, D, D, b, "gelu", sb["wbf"], HB1)),
                ("f0_e3_up", 0, 2048, lambda b: link(
                    sb["wbf"], hb1_col, D, 2048, b, "gelu", sb["kvc"], 0)),
                ("f0_e3_out", 0, 768, lambda b: down(
                    sb["kvc"], kv_col, 2048, b, 3, grp, acc)),
            ]

        pend = [None]

        def flush_pend():
            if pend[0] is not None:
                spec, comp, base = pend[0]
                pe_wait(spec)
                comp(base)
                pend[0] = None

        WQL = chans("p6wl", 2)
        WQS = chans("p6ws", 2)
        wq_state = {"gi": 0, "marks": []}
        wq_tasks = []

        def wq_guard_load(c, h, name, sl):
            gi = wq_state["gi"]
            if gi >= 2:
                wait("scalar", VD, wq_state["marks"][gi - 2])
            else:
                wait("scalar", VD, p5_cast_marks[-1])
            dma_c("scalar", sb["rst"][:, sl * 1024:(sl + 1) * 1024],
                  A[name][c * P:(c + 1) * P, h * 1024:(h + 1) * 1024], WQL[sl])

        def build_wq_tasks():
            for wi, (name, din, dout) in enumerate([("f1_wg", D, 4096),
                                                    ("f1_wu", D, 4096),
                                                    ("f1_wd", 4096, D)]):
                nch = din // P
                nh = dout // 1024
                pieces = [(c, h) for c in range(nch) for h in range(nh)]
                for pi, (c, h) in enumerate(pieces):
                    def t1(c=c, h=h, name=name, first=(pi == 0)):
                        sl = wq_state["gi"] % 2
                        wq_guard_load(c, h, name, sl)
                        stg = sb["rst"][:, sl * 1024:(sl + 1) * 1024]
                        wait("vector", WQL[sl])
                        vop(lambda eng, stg=stg: eng.reduce_sum(
                            st[:, 1:2], stg, AX, apply_absolute_value=True))
                        if first:
                            v = vop(lambda eng: eng.tensor_copy(st[:, 0:1], st[:, 1:2]))
                        else:
                            v = vop(lambda eng: eng.tensor_tensor(
                                st[:, 0:1], st[:, 0:1], st[:, 1:2], ALU.add))
                        wq_state["marks"].append(v)
                        wq_state["gi"] += 1
                    wq_tasks.append(t1)

                def tfin(wi=wi, name=name, n=float(din * dout)):
                    wait("tensor", VD, wq_state["marks"][-1])
                    m1 = top(lambda eng: eng.matmul(
                        psB[0:1, 1536:1537], st[:, 0:1], sm[:, SM_ONES:SM_ONES + 1],
                        start=True, stop=True), inc=True)
                    wait("vector", MM, m1)
                    vop(lambda eng: eng.tensor_copy(st[0:1, 2:3], psB[0:1, 1536:1537]))
                    vop(lambda eng, n=n: eng.tensor_scalar(
                        st[0:1, 2:3], st[0:1, 2:3], 1.0 / n, 1e-5, ALU.mult, ALU.max))
                    v = vop(lambda eng: eng.tensor_scalar(
                        st[0:1, 3:4], st[0:1, 2:3], 0.5, None, ALU.mult))
                    wait("tensor", VD, v)
                    m2 = top(lambda eng: eng.matmul(
                        psB[:, 1537:1539], sm[0:1, SM_ONESROW:SM_ONESROW + 128],
                        st[0:1, 2:4], start=True, stop=True), inc=True)
                    wait("vector", MM, m2)
                    beta = st[:, 1792 + 2 * wi:1793 + 2 * wi]
                    thr = st[:, 1793 + 2 * wi:1794 + 2 * wi]
                    vop(lambda eng, beta=beta: eng.tensor_copy(beta, psB[:, 1537:1538]))
                    vop(lambda eng, thr=thr: eng.tensor_copy(thr, psB[:, 1538:1539]))
                    vop(lambda eng, thr=thr: eng.tensor_scalar(
                        st[:, 4:5], thr, -1.0, None, ALU.mult))
                wq_tasks.append(tfin)

                for c, h in pieces:
                    def t2(c=c, h=h, name=name, wi=wi):
                        sl = wq_state["gi"] % 2
                        wq_guard_load(c, h, name, sl)
                        stg = sb["rst"][:, sl * 1024:(sl + 1) * 1024]
                        thr = st[:, 1793 + 2 * wi:1794 + 2 * wi]
                        wait("vector", WQL[sl])
                        wait("vector", WQS[sl])
                        msk = sb["bf"][:, 2048 + sl * 1024:2048 + (sl + 1) * 1024]
                        out = sb["bf"][:, sl * 1024:(sl + 1) * 1024]
                        vop(lambda eng, msk=msk, stg=stg: eng.tensor_scalar(
                            msk, stg, st[:, 4:5], None, ALU.is_le))
                        v = vop(lambda eng, out=out, stg=stg, thr=thr, msk=msk:
                                eng.scalar_tensor_tensor(
                                    out, stg, thr, msk, ALU.is_ge, ALU.subtract))
                        wq_state["marks"].append(v)
                        wq_state["gi"] += 1
                        wait("scalar", VD, v)
                        dma_c("scalar", A["wq_sp"][name][c * P:(c + 1) * P,
                                                         h * 1024:(h + 1) * 1024],
                              out, WQS[sl])
                    wq_tasks.append(t2)

        build_wq_tasks()

        for grp in range(ngrp):
            gq = grp * GW
            wait("sync", SP5[0])
            wait("sync", SP5[1])
            for tg in range(ntg):
                dma_c("sync", acc[:, tg * 768:(tg + 1) * 768],
                      A["xsum_sp"][gq + tg * P: gq + (tg + 1) * P, 256:1024], AL[tg % 2])
            wait("vector", AL[0])
            wait("vector", AL[1])
            for name, co, cw, comp in steps_for(grp, gq):
                base = (wslot[0] % 2) * BHALF
                wslot[0] += 1
                spec = issue_load(name, co, cw, base, grp)
                flush_pend()
                pend[0] = (spec, comp, base)
                if grp >= 1:
                    for _ in range(6):
                        if wq_tasks:
                            wq_tasks.pop(0)()
            flush_pend()
            wait("sync", VD)
            for tg in range(ntg):
                dma_c("sync", A["xsum_sp"][gq + tg * P: gq + (tg + 1) * P, 256:1024],
                      acc[:, tg * 768:(tg + 1) * 768], SP5[tg % 2])
            for c in (SP5[0], SP5[1]):
                wait("vector", c)
        while wq_tasks:
            wq_tasks.pop(0)()

    pw_marks, p4_marks, p4x_marks, lg_marks = [], [], [], []
    phase4()
    ctx.barrier("p4")
    phase5()
    ctx.barrier("p5")

    def phase6_ln_quant():
        st = sb["stat"]
        Lx = chans("p6lx", 2)
        Sp1 = chans("p6s1", 2)
        Sp2 = chans("p6s2", 2)
        Tq = chans("p6tq", 2)
        Tn = chans("p6tn", 2)
        vts = []
        for i in range(NT):
            sl = i % 2
            xa = sb["stage"][:, sl * D:(sl + 1) * D]
            xn = sb["stage"][:, (2 + sl) * D:(3 + sl) * D]
            if i >= 2:
                wait("sync", VD, vts[i - 2])
            dma_c("sync", xa, A["xsum_sp"][i * P:(i + 1) * P, :], Lx[sl])
            wait("vector", Lx[sl])
            wait("scalar", Lx[sl])
            if i >= 2:
                wait("vector", Sp1[sl])
            mu = st[:, 128 + i:128 + i + 1]
            r = st[:, 192 + i:192 + i + 1]
            vop(lambda eng, xa=xa, i=i: eng.reduce_sum(st[:, i:i + 1], xa, AX))
            if i >= 2:
                wait("scalar", VD, vts[i - 2])
                wait("scalar", Sp1[sl])
            sq_ = sop(lambda eng, xa=xa, i=i, xn=xn:
                      eng.activation(out=xn, in_=xa, func=ACT.Square,
                                     accum_out=st[:, 64 + i:65 + i]))
            vop(lambda eng, mu=mu, i=i: eng.tensor_scalar(
                mu, st[:, i:i + 1], 1.0 / D, None, ALU.mult))
            vop(lambda eng, mu=mu, i=i: eng.tensor_tensor(
                st[:, 256 + i:257 + i], mu, mu, ALU.mult))
            wait("vector", SD, sq_)
            vop(lambda eng, i=i: eng.tensor_scalar(
                st[:, 320 + i:321 + i], st[:, 64 + i:65 + i], 1.0 / D, None, ALU.mult))
            vop(lambda eng, i=i: eng.tensor_tensor(
                st[:, 320 + i:321 + i], st[:, 320 + i:321 + i],
                st[:, 256 + i:257 + i], ALU.subtract))
            v5 = vop(lambda eng, i=i: eng.tensor_scalar(
                st[:, 320 + i:321 + i], st[:, 320 + i:321 + i], 1e-5, None, ALU.add))
            wait("scalar", VD, v5)
            s6 = sop(lambda eng, i=i: eng.activation(
                out=st[:, 256 + i:257 + i], in_=st[:, 320 + i:321 + i],
                func=ACT.Sqrt))
            wait("vector", SD, s6)
            vop(lambda eng, r=r, i=i: eng.reciprocal(r, st[:, 256 + i:257 + i]))
            v1 = vop(lambda eng, xn=xn, xa=xa, mu=mu, r=r:
                     eng.tensor_scalar(xn, xa, mu, r, ALU.subtract, ALU.mult))
            wait("sync", VD, v1)
            dma_c("sync", A["xn1_sp"][i * P:(i + 1) * P, :], xn, Sp1[sl])
            am = st[:, 640 + i:641 + i]
            vop(lambda eng, xn=xn, am=am: eng.reduce_max(am, xn, AX, apply_absolute_value=True))
            vop(lambda eng, am=am: eng.tensor_scalar(am, am, 1e-5, None, ALU.max))
            vop(lambda eng, am=am, i=i: eng.tensor_scalar(
                st[:, 656 + i:657 + i], am, 1.0 / 127, None, ALU.mult))
            vop(lambda eng, am=am: eng.reciprocal(am, am))
            v2 = vop(lambda eng, am=am: eng.tensor_scalar(am, am, 127.0, None, ALU.mult))
            xb = sb["bf"][:, sl * D:(sl + 1) * D]
            if i >= 2:
                wait("vector", Tq[sl])
                wait("vector", Tn[sl])
            vop(lambda eng, xa=xa, xn=xn, am=am: eng.tensor_scalar(xa, xn, am, None, ALU.mult))
            vop(lambda eng, xa=xa: eng.tensor_scalar(xa, xa, 12582912.0, 12582912.0,
                                                     ALU.add, ALU.subtract))
            v3 = vop(lambda eng, xb=xb, xa=xa: eng.tensor_copy(xb, xa))
            wait("sync", VD, v3)
            dmat_c(t_dst(sb["qt"], i, tot=TPC), xb, Tq[sl])
            xb2 = sb["bf"][:, (2 + sl) * D:(3 + sl) * D]
            v4 = vop(lambda eng, xb2=xb2, xn=xn: eng.tensor_copy(xb2, xn))
            vts.append(v4)
            wait("sync", VD, v4)
            dmat_c(t_dst(sb["xnT"], i, tot=TPC), xb2, Tn[sl])
        wait("sync", Tn[0])
        wait("sync", Tn[1])
        dma_c("sync", A["xn1T_sp"][:], sb["xnT"][:, 0:NDC * TPC], LD)

    def phase6_kv_cc():
        st = sb["stat"]
        sm = sb["small"]
        C6 = psem("p6c")
        fk = st[:, 704:712]
        fv = st[:, 712:720]
        for c in range(NDC):
            dma_c("sync", fk[:, c:c + 1], A["a1_lnk_s"][0:1, c * P:(c + 1) * P], C6)
            dma_c("sync", fv[:, c:c + 1], A["a1_lnv_s"][0:1, c * P:(c + 1) * P], C6)
        bk_ = st[:, 720:728]
        bv_ = st[:, 728:736]
        for c in range(NDC):
            dma_c("sync", bk_[:, c:c + 1], A["a1_lnk_b"][0:1, c * P:(c + 1) * P], C6)
            dma_c("sync", bv_[:, c:c + 1], A["a1_lnv_b"][0:1, c * P:(c + 1) * P], C6)
        dma_c("sync", sb["rst"][0:1, 0:D], A["a1_bk"], C6)
        dma_c("sync", sb["rst"][32:33, 0:D], A["a1_bv"], C6)
        wait("vector", C6)
        wait("tensor", C6)
        Wch = chans("p6kv", 2)
        KM6 = st[:, 736:752]

        def kv_side(wname, fold, bcol, brow_part, out_region, do_norm):
            psem6_kt = psem("p6kt_" + wname)
            """stream W (fold lnX_s rows) -> wbf bf16; bvec = lnX_b @ W' + bX;
            produce token-major tiles + bias-rep + spill-free: write into
            cc staging then dma to cc_in; do_norm: collect kmax/qsq? (K only)"""
            marks = stream_weight(A[wname], NDC, D, 0, Wch)
            bb16 = sb["bf"][:, 0:8]
            vc_ = vop(lambda eng, bb16=bb16, bcol=bcol: eng.tensor_copy(bb16, bcol))
            wait("tensor", VD, marks[-1])
            wait("tensor", VD, vc_)
            for go, gw in groups(D):
                for c in range(NDC):
                    top(lambda eng, c=c, bb16=bb16, go=go, gw=gw: eng.matmul(
                        psB[0:1, go:go + gw], bb16[:, c:c + 1],
                        sb["wbf"][:, c * D + go:c * D + go + gw],
                        start=(c == 0), stop=(c == NDC - 1)),
                        inc=(c == NDC - 1 and go + gw == D))
            wait("vector", MM)
            for c in range(NDC):
                vop(lambda eng, c=c: eng.tensor_scalar(
                    sb["wbf"][:, c * D:(c + 1) * D], sb["wbf"][:, c * D:(c + 1) * D],
                    fold[:, c:c + 1], None, ALU.mult))
            wait("tensor", VD)
            wait("vector", MM)
            v = vop(lambda eng, brow_part=brow_part: eng.tensor_tensor(
                sb["rst"][brow_part:brow_part + 1, 0:D], sb["rst"][brow_part:brow_part + 1, 0:D],
                psB[0:1, 0:D], ALU.add))
            wait("tensor", VD, v)
            for go, gw in groups(D):
                top(lambda eng, go=go, gw=gw, brow_part=brow_part: eng.matmul(
                    psB[:, go:go + gw],
                    sm[brow_part:brow_part + 1, SM_ONESROW:SM_ONESROW + 128] if brow_part else sm[0:1, SM_ONESROW:SM_ONESROW + 128],
                    sb["rst"][brow_part:brow_part + 1, go:go + gw],
                    start=True, stop=True), inc=(go + gw == D))
            wait("vector", MM)
            brep = sb["rst"][:, D:2 * D]
            vb = vop(lambda eng, brep=brep: eng.tensor_copy(brep, psB[:, 0:D]))
            wait("tensor", VD, vb)
            CS = chans("p6cs", 2)
            for tt in range(NT):
                wait("tensor", VD)
                for oh in range(2):
                    for c in range(NDC):
                        top(lambda eng, c=c, tt=tt, oh=oh: eng.matmul(
                            psA[:, oh * 512:(oh + 1) * 512],
                            sb["xnT"][:, c * TPC + tt * P:c * TPC + (tt + 1) * P],
                            sb["wbf"][:, c * D + oh * 512:c * D + (oh + 1) * 512],
                            start=(c == 0), stop=(c == NDC - 1)),
                            inc=(c == NDC - 1 and oh == 1))
                wait("vector", MM)
                sl = tt % 2
                kb = sb["bf"][:, (1 + sl) * D:(2 + sl) * D]
                if tt >= 2:
                    wait("vector", CS[sl])
                v = vop(lambda eng, kb=kb, brep=brep: eng.tensor_tensor(
                    kb, psA[:, 0:D], brep, ALU.add))
                if do_norm:
                    wait("scalar", VD, v)
                    s_ = sop(lambda eng, kb=kb, sq=sb["stage"][:, 2 * D:3 * D]:
                             eng.activation(out=sq, in_=kb, func=ACT.Square))
                    wait("vector", SD, s_)
                    vop(lambda eng: eng.reduce_sum(
                        st[:, 48:64], sb["stage"][:, 2 * D:3 * D]
                        .rearrange("p (h d) -> p h d", d=64), AX))
                    v = vop(lambda eng: eng.tensor_tensor(KM6, KM6, st[:, 48:64], ALU.max))
                if not do_norm:
                    wait("sync", VD, v)
                    dma_c("sync", A["cc_in"][:, out_region + tt * D:out_region + (tt + 1) * D],
                          kb, CS[sl])
                else:
                    wait("sync", VD, v)
                    KT6 = psem("p6kt")
                    dmat_c(sb["bf"][:, 3 * D:4 * D].rearrange("p (c t) -> p c t", t=P), kb, KT6)
                    wait("sync", KT6)
                    dma_c("sync",
                          A["cc_in"][:, 0:NDC * TPC].rearrange("p (c t) -> p c t", t=TPC)[:, :, tt * P:(tt + 1) * P],
                          sb["bf"][:, 3 * D:4 * D].rearrange("p (c t) -> p c t", t=P), CS[sl])
                    wait("scalar", CS[sl])

        v0 = vop(lambda eng: eng.memset(KM6, 0.0))
        kv_side("a1_Wk", fk, bk_, 0, 0, True)
        kv_side("a1_Wv", fv, bv_, 32, NDC * TPC, False)
        pad = sb["stage"][:, 0:P]
        v = vop(lambda eng, pad=pad: eng.memset(pad, 0.0))
        v = vop(lambda eng, pad=pad: eng.tensor_copy(pad[:, 0:16], KM6))
        ID6 = psem("p6id")
        dma_c("sync", sb["rst"][:, 0:P], A["ident"], ID6)
        wait("tensor", ID6)
        wait("tensor", VD, v)
        m = top(lambda eng, pad=pad: eng.transpose(psA[:, 0:P], pad, sb["rst"][:, 0:P]))
        wait("vector", MM, m)
        v = vop(lambda eng: eng.reduce_max(st[0:16, 0:1], psA[0:16, 0:P], AX))
        v = vop(lambda eng, pad=pad: eng.memset(pad, 0.0))
        v = vop(lambda eng, pad=pad: eng.tensor_copy(pad[0:16, 0:1], st[0:16, 0:1]))
        wait("tensor", VD, v)
        m = top(lambda eng, pad=pad: eng.transpose(psB[:, 0:P], pad, sb["rst"][:, 0:P]))
        wait("vector", MM, m)
        kb = sb["bf"][:, 0:128]
        v = vop(lambda eng, kb=kb: eng.memset(kb, 0.0))
        v = vop(lambda eng, kb=kb: eng.tensor_copy(kb[0:1, 0:16], psB[0:1, 0:16]))
        KX = psem("p6kx")
        wait("sync", VD, v)
        dma_c("sync", A["cc_in"][:, 2 * NDC * TPC:2 * NDC * TPC + 128], kb, KX)
        CCS = ctx.sem("ccs")
        wait("gpsimd", KX)
        for ch in list(ctx.dma_seen):
            wait("gpsimd", ch)
        E("gpsimd", lambda eng: eng.collective_compute(
            "AllGather", ALU.bypass,
            replica_groups=[[0, 1], [2, 3], [4, 5], [6, 7]],
            ins=[A["cc_in"][:]], outs=[A["cc_out"][:]]).then_inc(CCS.h, 1))
        CCS.n += 1
        return CCS

    def phase6_ffn():
        st = sb["stat"]
        EW = sb["xnT"]
        for chp in list(ctx.dma_seen):
            wait("vector", chp)
            wait("scalar", chp)
            wait("sync", chp)
        WL = chans("p6fw", 2)
        GS = chans("p6gs", 2)
        XL = chans("p6x1", 2)
        SPA = chans("p6sa", 2)
        wait("sync", MM)
        for c in range(NDC):
            dma_c("sync", EW[:, c * 4096:(c + 1) * 4096],
                  A["wq_sp"]["f1_wg"][c * P:(c + 1) * P, :], WL[c % 2])
        for c in range(NDC):
            dst = sb["wbf"] if c < 4 else sb["kvc"]
            dma_c("sync", dst[:, (c % 4) * 4096:(c % 4 + 1) * 4096],
                  A["wq_sp"]["f1_wu"][c * P:(c + 1) * P, :], WL[c % 2])
        wait("tensor", WL[0])
        wait("tensor", WL[1])

        def wu_ap(c, col, wd):
            buf = sb["wbf"] if c < 4 else sb["kvc"]
            return buf[:, (c % 4) * 4096 + col:(c % 4) * 4096 + col + wd]

        bg = st[:, 1792:1793]
        bu = st[:, 1794:1795]
        bd = st[:, 1796:1797]
        for i in range(NT):
            inv = st[:, 656 + i:657 + i]
            sg = st[:, 8:9]
            su = st[:, 9:10]
            wait("vector", SD)
            vop(lambda eng, sg=sg, inv=inv, bg=bg: eng.tensor_tensor(sg, inv, bg, ALU.mult))
            v0 = vop(lambda eng, su=su, inv=inv, bu=bu: eng.tensor_tensor(su, inv, bu, ALU.mult))
            wait("scalar", VD, v0)
            am2 = st[:, 672 + i:673 + i]
            vop(lambda eng, am2=am2: eng.memset(am2, 0.0))
            wait("vector", GS[0])
            wait("vector", GS[1])
            for hc in range(8):
                wait("tensor", SD)
                wait("tensor", VD)
                for c in range(NDC):
                    top(lambda eng, c=c, i=i, hc=hc: eng.matmul(
                        psA[:, 0:512],
                        sb["qt"][:, c * TPC + i * P:c * TPC + (i + 1) * P],
                        EW[:, c * 4096 + hc * 512:c * 4096 + (hc + 1) * 512],
                        start=(c == 0), stop=(c == NDC - 1)), inc=(c == NDC - 1))
                for c in range(NDC):
                    top(lambda eng, c=c, i=i, hc=hc: eng.matmul(
                        psB[:, 0:512],
                        sb["qt"][:, c * TPC + i * P:c * TPC + (i + 1) * P],
                        wu_ap(c, hc * 512, 512),
                        start=(c == 0), stop=(c == NDC - 1)), inc=(c == NDC - 1))
                wait("scalar", MM)
                G = sb["stage"][:, 0:512]
                U = sb["stage"][:, 512:1024]
                sop(lambda eng, G=G, sg=sg: eng.activation(
                    out=G, in_=psA[:, 0:512], func=ACT.Silu, scale=sg))
                s1 = sop(lambda eng, U=U, su=su: eng.activation(
                    out=U, in_=psB[:, 0:512], func=ACT.Copy, scale=su))
                wait("vector", SD, s1)
                gu = sb["stage"][:, 1024 + (hc % 2) * 512:1024 + (hc % 2) * 512 + 512]
                if hc >= 2:
                    wait("vector", GS[hc % 2])
                vg = vop(lambda eng, gu=gu, G=G, U=U: eng.tensor_tensor(gu, G, U, ALU.mult))
                vop(lambda eng, gu=gu: eng.reduce_max(st[:, 10:11], gu, AX,
                                                      apply_absolute_value=True))
                vop(lambda eng, am2=am2: eng.tensor_tensor(am2, am2, st[:, 10:11], ALU.max))
                wait("sync", VD, vg)
                dma_c("sync", A["gu_sp"][i * P:(i + 1) * P, hc * 512:(hc + 1) * 512], gu, GS[hc % 2])
            vop(lambda eng, am2=am2: eng.tensor_scalar(am2, am2, 1e-5, None, ALU.max))
            vop(lambda eng, am2=am2, i=i: eng.tensor_scalar(
                st[:, 688 + i:689 + i], am2, 1.0 / 127, None, ALU.mult))
            vop(lambda eng, am2=am2: eng.reciprocal(am2, am2))
            vop(lambda eng, am2=am2: eng.tensor_scalar(am2, am2, 127.0, None, ALU.mult))
        wait("sync", MM)
        wait("sync", SD)
        for c in range(4096 // P):
            dma_c("sync", EW[:, c * D:(c + 1) * D],
                  A["wq_sp"]["f1_wd"][c * P:(c + 1) * P, :], WL[c % 2])
        wait("tensor", WL[0])
        wait("tensor", WL[1])
        GT = chans("p6gt", 2)
        wait("sync", GS[0])
        wait("sync", GS[1])
        for i in range(NT):
            gu = sb["stage"][:, 0:4096]
            wait("sync", VD)
            wait("sync", SD)
            dma_c("sync", gu, A["gu_sp"][i * P:(i + 1) * P, :], GS[0])
            wait("vector", GS[0])
            s2 = st[:, 672 + i:673 + i]
            vop(lambda eng, gu=gu, s2=s2: eng.tensor_scalar(gu, gu, s2, None, ALU.mult))
            vop(lambda eng, gu=gu: eng.tensor_scalar(gu, gu, 12582912.0, 12582912.0,
                                                     ALU.add, ALU.subtract))
            vop(lambda eng, gu=gu: eng.tensor_copy(sb["bf"][:, 0:4096], gu))
            v2 = VD.n
            wait("sync", VD, v2)
            dmat_c(sb["qt"][:, 0:4096].rearrange("p (c t) -> p c t", t=P),
                   sb["bf"][:, 0:4096], GT[0])
            x1t = sb["rst"][:, 0:D]
            dma_c("sync", x1t, A["xsum_sp"][i * P:(i + 1) * P, :], XL[0])
            c2 = st[:, 11:12]
            vop(lambda eng, c2=c2, i=i, bd=bd: eng.tensor_tensor(
                c2, st[:, 688 + i:689 + i], bd, ALU.mult))
            wait("tensor", GT[0])
            wait("tensor", VD)
            for oh in range(2):
                for c in range(4096 // P):
                    top(lambda eng, c=c, oh=oh: eng.matmul(
                        psA[:, oh * 512:(oh + 1) * 512],
                        sb["qt"][:, c * P: (c + 1) * P],
                        EW[:, c * D + oh * 512:c * D + (oh + 1) * 512],
                        start=(c == 0), stop=(c == 4096 // P - 1)),
                        inc=(c == 4096 // P - 1 and oh == 1))
            wait("vector", MM)
            wait("vector", XL[0])
            out = sb["rst"][:, D:2 * D]
            if i >= 1:
                wait("vector", SPA[0])
            va = vop(lambda eng, out=out, x1t=x1t, c2=c2: eng.scalar_tensor_tensor(
                out, psA[:, 0:D], c2, x1t, ALU.mult, ALU.add))
            wait("sync", VD, va)
            dma_c("sync", A["xacc_sp"][i * P:(i + 1) * P, :], out, SPA[0])

    def phase7(CCS):
        st = sb["stat"]
        sm = sb["small"]
        EW = sb["xnT"]
        SC1 = float(D) ** -0.5
        XT7 = psem("p7xt")
        wait("sync", SD)
        wait("sync", VD)
        wait("sync", MM)
        dma_c("sync", EW[:, 0:NDC * TPC], A["xn1T_sp"][:], XT7)
        wait("tensor", XT7)
        C7 = psem("p7c")
        fq = st[:, 704:712]
        bq_ = st[:, 720:728]
        for c in range(NDC):
            dma_c("sync", fq[:, c:c + 1], A["a1_lnq_s"][0:1, c * P:(c + 1) * P], C7)
            dma_c("sync", bq_[:, c:c + 1], A["a1_lnq_b"][0:1, c * P:(c + 1) * P], C7)
        dma_c("sync", sb["rst"][0:1, 0:D], A["a1_bq"], C7)
        wait("vector", C7)
        wait("tensor", C7)
        Wch = chans("p7w", 2)
        marks = stream_weight(A["a1_Wq"], NDC, D, 0, Wch)
        bb16 = sb["bf"][:, 0:8]
        vc_ = vop(lambda eng, bb16=bb16: eng.tensor_copy(bb16, bq_))
        wait("tensor", VD, marks[-1])
        wait("tensor", VD, vc_)
        for go, gw in groups(D):
            for c in range(NDC):
                top(lambda eng, c=c, bb16=bb16, go=go, gw=gw: eng.matmul(
                    psB[0:1, go:go + gw], bb16[:, c:c + 1],
                    sb["wbf"][:, c * D + go:c * D + go + gw],
                    start=(c == 0), stop=(c == NDC - 1)),
                    inc=(c == NDC - 1 and go + gw == D))
        wait("vector", MM)
        for c in range(NDC):
            vop(lambda eng, c=c: eng.tensor_scalar(
                sb["wbf"][:, c * D:(c + 1) * D], sb["wbf"][:, c * D:(c + 1) * D],
                fq[:, c:c + 1], None, ALU.mult))
        wait("tensor", VD)
        wait("vector", MM)
        v = vop(lambda eng: eng.tensor_tensor(sb["rst"][0:1, 0:D], sb["rst"][0:1, 0:D],
                                              psB[0:1, 0:D], ALU.add))
        wait("tensor", VD, v)
        for go, gw in groups(D):
            top(lambda eng, go=go, gw=gw: eng.matmul(
                psB[:, go:go + gw], sm[0:1, SM_ONESROW:SM_ONESROW + 128],
                sb["rst"][0:1, go:go + gw], start=True, stop=True), inc=(go + gw == D))
        wait("vector", MM)
        brep = sb["rst"][:, D:2 * D]
        vb = vop(lambda eng, brep=brep: eng.tensor_copy(brep, psB[:, 0:D]))
        wait("tensor", VD, vb)
        TQ7 = chans("p7tq", 2)
        for tt in range(NT):
            wait("tensor", VD)
            for oh in range(2):
                for c in range(NDC):
                    top(lambda eng, c=c, tt=tt, oh=oh: eng.matmul(
                        psA[:, oh * 512:(oh + 1) * 512],
                        EW[:, c * TPC + tt * P:c * TPC + (tt + 1) * P],
                        sb["wbf"][:, c * D + oh * 512:c * D + (oh + 1) * 512],
                        start=(c == 0), stop=(c == NDC - 1)),
                        inc=(c == NDC - 1 and oh == 1))
            wait("vector", MM)
            sl = tt % 2
            kb = sb["bf"][:, (1 + sl) * D:(2 + sl) * D]
            if tt >= 2:
                wait("vector", TQ7[sl])
            v = vop(lambda eng, kb=kb, brep=brep: eng.tensor_tensor(kb, psA[:, 0:D], brep, ALU.add))
            wait("scalar", VD, v)
            s_ = sop(lambda eng, kb=kb, sq=sb["stage"][:, 2 * D:3 * D]:
                     eng.activation(out=sq, in_=kb, func=ACT.Square))
            wait("vector", SD, s_)
            vop(lambda eng, tt=tt: eng.reduce_sum(
                st[:, 768 + 16 * tt:768 + 16 * (tt + 1)],
                sb["stage"][:, 2 * D:3 * D].rearrange("p (h d) -> p h d", d=64), AX))
            wait("sync", VD)
            dmat_c(t_dst(sb["qt"], tt, tot=TPC), kb, TQ7[sl])
        wait("tensor", TQ7[0])
        wait("tensor", TQ7[1])
        for e in ("vector", "scalar", "tensor", "sync", "gpsimd"):
            wait(e, CCS)
        KX7 = [ctx.sem("kx7a"), ctx.sem("kx7b")]
        kmx = sm[0:1, 600:616]
        kmx2 = sm[0:1, 616:632]
        dma_c("gpsimd", kmx,
              A["cc_out"][0][0:1, 2 * NDC * TPC:2 * NDC * TPC + 16], KX7[0])
        dma_c("gpsimd", kmx2,
              A["cc_out"][1][0:1, 2 * NDC * TPC:2 * NDC * TPC + 16], KX7[1])
        wait("vector", KX7[0])
        wait("vector", KX7[1])
        v = vop(lambda eng, kmx=kmx, kmx2=kmx2: eng.tensor_tensor(kmx, kmx, kmx2, ALU.max))
        wait("tensor", VD, v)
        m = top(lambda eng, kmx=kmx: eng.matmul(psA[:, 0:16], sm[0:1, SM_ONESROW:SM_ONESROW + 128],
                                                kmx, start=True, stop=True))
        wait("vector", MM, m)
        kmrep = st[:, 32:48]
        v = vop(lambda eng, kmrep=kmrep: eng.tensor_copy(kmrep, psA[:, 0:16]))
        for tt in range(NT):
            v = vop(lambda eng, tt=tt, kmrep=kmrep: eng.tensor_tensor(
                st[:, 1024 + 16 * tt:1024 + 16 * (tt + 1)],
                st[:, 768 + 16 * tt:768 + 16 * (tt + 1)], kmrep, ALU.mult))
        wait("scalar", VD, v)
        s_ = sop(lambda eng: eng.activation(out=st[:, 1024:1024 + 16 * NT],
                                            in_=st[:, 1024:1024 + 16 * NT],
                                            func=ACT.Sqrt, scale=SC1 * SC1))
        wait("vector", SD, s_)
        vop(lambda eng: eng.tensor_scalar(st[:, 1024:1024 + 16 * NT],
                                          st[:, 1024:1024 + 16 * NT], -1.0, None, ALU.mult))
        CR = psem("p7cr")
        dma_c("sync", sm[0:1, 632:633], A["corner"], CR)
        wait("vector", CR)
        KSb, Eb, PTb, Vb = 0, T, 3 * T, 7 * T
        KL = chans("p7kl", 2)
        VL = chans("p7vl", 2)
        TpP = chans("p7pt", 2)
        NT2 = NT // 2
        rw_ = T // 2
        wait("scalar", LD)
        wait("vector", LD)
        PTALL = EW[:, PTb:PTb + 4 * T].rearrange("p (x s q) -> p x s q", s=2, q=P)
        for j in range(NDC):
            jq = j * TPC
            wait("sync", VD)
            wait("sync", SD)
            wait("sync", MM)
            for r in range(2):
                dma_c("sync", EW[:, KSb + r * TPC:KSb + (r + 1) * TPC],
                      A["cc_out"][r][:, 0:NDC * TPC]
                      .rearrange("p (c t) -> p c t", t=TPC)[:, j, :], KL[0])
                dma_c("sync", EW[:, Vb + r * NT * P:Vb + (r + 1) * NT * P],
                      A["cc_out"][r][:, NDC * TPC:2 * NDC * TPC]
                      .rearrange("p (t d) -> p t d", d=D)[:, :, j * P:(j + 1) * P], VL[0])

            def p7_slot(s, j=j, jq=jq):
                zc = st[:, 48:56]
                wait("scalar", VD)
                first_exp = True
                exps = []
                for run in range(2):
                    wait("tensor", SD)
                    wait("tensor", VD)
                    wait("tensor", KL[0])
                    for go, gw in groups(rw_):
                        top(lambda eng, go=go, gw=gw, s=s, run=run, jq=jq:
                            eng.matmul(psA[:, go:go + gw],
                                       sb["qt"][0:64, jq + s * P: jq + (s + 1) * P],
                                       EW[0:64, KSb + run * rw_ + go: KSb + run * rw_ + go + gw],
                                       start=True, stop=True, tile_position=(0, 0)),
                            inc=False)
                        top(lambda eng, go=go, gw=gw, s=s, run=run, jq=jq:
                            eng.matmul(psB[:, go:go + gw],
                                       sb["qt"][64:128, jq + s * P: jq + (s + 1) * P],
                                       EW[64:128, KSb + run * rw_ + go: KSb + run * rw_ + go + gw],
                                       start=True, stop=True, tile_position=(64, 0)),
                            inc=(go + gw == rw_))
                    m = MM.n
                    vv = None
                    if s == 0 and run == 1:
                        wait("vector", MM, m)
                        vop(lambda eng: eng.tensor_scalar(
                            psA[0:1, rw_ - 1:rw_], psA[0:1, rw_ - 1:rw_],
                            sm[0:1, 632:633], None, ALU.add))
                        vv = vop(lambda eng: eng.tensor_scalar(
                            psB[0:1, rw_ - 1:rw_], psB[0:1, rw_ - 1:rw_],
                            sm[0:1, 632:633], None, ALU.add))
                    if first_exp:
                        wait("scalar", TpP[0])
                        wait("scalar", TpP[1])
                        first_exp = False
                    if vv is not None:
                        wait("scalar", VD, vv)
                    else:
                        wait("scalar", MM, m)
                    for hh, ps in ((0, psA), (1, psB)):
                        bias = st[:, 1024 + s * 16 + 2 * j + hh: 1024 + s * 16 + 2 * j + hh + 1]
                        zcol = zc[:, run * 2 + hh: run * 2 + hh + 1]
                        e_ = sop(lambda eng, run=run, hh=hh, bias=bias, zcol=zcol, ps=ps:
                                 eng.activation(out=EW[:, Eb + (hh * 2 + run) * rw_:
                                                       Eb + (hh * 2 + run + 1) * rw_],
                                                in_=ps[:, 0:rw_], func=ACT.Exp, bias=bias,
                                                scale=SC1, accum_out=zcol))
                        exps.append(e_)
                wait("vector", SD, exps[-1])
                vop(lambda eng, zc=zc: eng.tensor_tensor(zc[:, 4:6], zc[:, 0:2], zc[:, 2:4], ALU.add))
                wait_rc = vop(lambda eng, zc=zc: eng.reciprocal(zc[:, 6:8], zc[:, 4:6]))
                for hh in range(2):
                    ehh = EW[:, Eb + hh * T:Eb + (hh + 1) * T]
                    vf = vop(lambda eng, ehh=ehh, zc=zc, hh=hh: eng.tensor_scalar(
                        ehh, ehh, zc[:, 6 + hh:7 + hh], None, ALU.mult))
                    wait("sync", VD, vf)
                    wait("sync", MM)
                    dmat_c(PTALL[:, hh * 2 * NT:(hh + 1) * 2 * NT, s % 2, :], ehh, TpP[hh])

            for g in range(NT2):
                s0, s1 = 2 * g, 2 * g + 1
                p7_slot(s0)
                p7_slot(s1)
                wait("tensor", TpP[0])
                wait("tensor", TpP[1])
                wait("tensor", VL[0])
                for x in range(2 * NT):
                    for hh in range(2):
                        ps = psA if hh == 0 else psB
                        top(lambda eng, x=x, hh=hh, ps=ps:
                            eng.matmul(ps[64 * hh:64 * hh + 64, 0:256],
                                       EW[:, Vb + x * P + 64 * hh: Vb + x * P + 64 * hh + 64],
                                       EW[:, PTb + (hh * 2 * NT + x) * 256:
                                             PTb + (hh * 2 * NT + x + 1) * 256],
                                       start=(x == 0), stop=(x == 2 * NT - 1),
                                       tile_position=(0, 64 * hh), skip_group_check=True),
                            inc=(x == 2 * NT - 1 and hh == 1))
                wait("vector", MM)
                vop(lambda eng, j=j, g=g: eng.tensor_copy(
                    sb["wbf"][0:64, j * TPC + 2 * g * P: j * TPC + (2 * g + 2) * P],
                    psA[0:64, 0:256]))
                vop(lambda eng, j=j, g=g: eng.tensor_copy(
                    sb["wbf"][64:128, j * TPC + 2 * g * P: j * TPC + (2 * g + 2) * P],
                    psB[64:128, 0:256]))
                wait("tensor", VD)
        ctx.barrier("p7att")
        atT = sb["wbf"]
        marks = []
        wait("sync", VD)
        for c in range(NDC):
            sl = c % 2
            stg = sb["stage"][:, sl * D:(sl + 1) * D]
            if c >= 2:
                wait("sync", VD, marks[c - 2])
            dma_c("sync", stg, A["a1_Wo"][c * P:(c + 1) * P, :], Wch[sl])
            wait("vector", Wch[sl])
            v = vop(lambda eng, c=c, stg=stg: eng.tensor_copy(
                sb["kvc"][:, c * D:(c + 1) * D], stg))
            marks.append(v)
        BO = psem("p7bo")
        dma_c("sync", sb["rst"][0:1, 0:D], A["a1_bo"], BO)
        wait("tensor", BO)
        wait("tensor", VD, marks[-1])
        for go, gw in groups(D):
            top(lambda eng, go=go, gw=gw: eng.matmul(
                psB[:, go:go + gw], sm[0:1, SM_ONESROW:SM_ONESROW + 128],
                sb["rst"][0:1, go:go + gw], start=True, stop=True), inc=(go + gw == D))
        wait("vector", MM)
        borep = sb["rst"][:, D:2 * D]
        vb = vop(lambda eng, borep=borep: eng.tensor_copy(borep, psB[:, 0:D]))
        wait("tensor", VD, vb)
        XN7 = chans("p7xn", 2)
        SPH = chans("p7h", 2)
        hm = []
        for s in range(NT):
            wait("tensor", VD)
            for oh in range(2):
                for c in range(NDC):
                    top(lambda eng, c=c, s=s, oh=oh: eng.matmul(
                        psA[:, oh * 512:(oh + 1) * 512],
                        atT[:, c * TPC + s * P: c * TPC + (s + 1) * P],
                        sb["kvc"][:, c * D + oh * 512: c * D + (oh + 1) * 512],
                        start=(c == 0), stop=(c == NDC - 1)),
                        inc=(c == NDC - 1 and oh == 1))
            sl = s % 2
            xn1t_ = sb["stage"][:, (2 + sl) * D:(3 + sl) * D]
            if s >= 2:
                wait("sync", VD, hm[s - 2])
                wait("sync", SD)
                wait("sync", SPH[sl])
            dma_c("sync", xn1t_, A["xn1_sp"][s * P:(s + 1) * P, :], XN7[sl])
            wait("vector", XN7[sl])
            wait("vector", MM)
            if s >= 2:
                wait("vector", SPH[sl])
            vop(lambda eng, xn1t_=xn1t_, borep=borep: eng.tensor_tensor(
                xn1t_, xn1t_, borep, ALU.add))
            vh = vop(lambda eng, xn1t_=xn1t_: eng.tensor_tensor(
                xn1t_, xn1t_, psA[:, 0:D], ALU.add))
            hm.append(vh)
            vop(lambda eng, xn1t_=xn1t_, s=s: eng.reduce_sum(st[:, 768 + s:769 + s], xn1t_, AX))
            wait("scalar", VD, vh)
            sop(lambda eng, xn1t_=xn1t_, s=s, sq=sb["bf"][:, 2 * D:3 * D]:
                eng.activation(out=sq, in_=xn1t_, func=ACT.Square,
                               accum_out=st[:, 832 + s:833 + s]))
            wait("sync", VD, vh)
            wait("sync", SD)
            dma_c("sync", A["h_sp"][s * P:(s + 1) * P, :], xn1t_, SPH[sl])
        ln_finalize(NT, 1e-5, base=768)
        ctx.barrier("p7h")
        fm = st[:, 704:712]
        bm_ = st[:, 720:728]
        C7b = psem("p7c2")
        for c in range(NDC):
            dma_c("sync", fm[:, c:c + 1], A["a1_lnm_s"][0:1, c * P:(c + 1) * P], C7b)
            dma_c("sync", bm_[:, c:c + 1], A["a1_lnm_b"][0:1, c * P:(c + 1) * P], C7b)
        dma_c("sync", sb["rst"][0:1, 0:2048], A["a1_b1"], C7b)
        wait("vector", C7b)
        wait("tensor", C7b)
        W1L = chans("p7w1", 2)
        wait("sync", VD)
        w1marks = []
        for c in range(NDC):
            sl = c % 2
            stg = sb["stage"][:, sl * 2048:(sl + 1) * 2048]
            if c >= 2:
                wait("sync", VD, w1marks[c - 2])
            dma_c("sync", stg, A["a1_W1"][c * P:(c + 1) * P, :], W1L[sl])
            wait("vector", W1L[sl])
            v = vop(lambda eng, c=c, stg=stg: eng.tensor_copy(
                EW[:, c * 2048:(c + 1) * 2048], stg))
            w1marks.append(v)
        bb16b = sb["bf"][:, 8:16]
        vc2 = vop(lambda eng, bb16b=bb16b: eng.tensor_copy(bb16b, bm_))
        wait("tensor", VD, w1marks[-1])
        wait("tensor", VD, vc2)
        for go, gw in groups(2048):
            for c in range(NDC):
                top(lambda eng, c=c, bb16b=bb16b, go=go, gw=gw: eng.matmul(
                    psB[0:1, go:go + gw], bb16b[:, c:c + 1],
                    EW[:, c * 2048 + go:c * 2048 + go + gw],
                    start=(c == 0), stop=(c == NDC - 1)),
                    inc=(c == NDC - 1 and go + gw == 2048))
        wait("vector", MM)
        for c in range(NDC):
            vop(lambda eng, c=c: eng.tensor_scalar(
                EW[:, c * 2048:(c + 1) * 2048], EW[:, c * 2048:(c + 1) * 2048],
                fm[:, c:c + 1], None, ALU.mult))
        wait("tensor", VD)
        wait("vector", MM)
        v = vop(lambda eng: eng.tensor_tensor(sb["rst"][0:1, 0:2048], sb["rst"][0:1, 0:2048],
                                              psB[0:1, 0:2048], ALU.add))
        BV1 = psem("p7bv")
        wait("sync", VD, v)
        dma_c("sync", A["bv1_sp"][:], sb["rst"][0:1, 0:2048], BV1)
        wait("sync", BV1)
        b1c = st[:, 1536:1552]
        dma_c("sync", b1c, A["bv1_sp"][0:1, :].rearrange("o (c p) -> p (o c)", p=P), BV1, slow=True)
        wait("scalar", BV1)
        ZL = chans("p7z", 2)
        TZ = chans("p7tz", 2)
        zm = []
        for s in range(NT):
            sl = s % 2
            ht = sb["stage"][:, sl * D:(sl + 1) * D]
            if s >= 2:
                wait("sync", VD, zm[s - 2])
            dma_c("sync", ht, A["h_sp"][s * P:(s + 1) * P, :], ZL[sl])
            wait("vector", ZL[sl])
            mu = st[:, 768 + 128 + s:768 + 128 + s + 1]
            r = st[:, 768 + 192 + s:768 + 192 + s + 1]
            zb = sb["bf"][:, sl * D:(sl + 1) * D]
            if s >= 2:
                wait("vector", TZ[sl])
            vz = vop(lambda eng, zb=zb, ht=ht, mu=mu, r=r:
                     eng.tensor_scalar(zb, ht, mu, r, ALU.subtract, ALU.mult))
            zm.append(vz)
            wait("sync", VD, vz)
            dmat_c(t_dst(sb["qt"], s, tot=TPC), zb, TZ[sl])
        W2L = chans("p7w2", 2)
        wait("sync", VD)
        wait("sync", MM)
        w2marks = []
        for c in range(2048 // P):
            sl = c % 2
            stg = sb["stage"][:, sl * D:(sl + 1) * D]
            if c >= 2:
                wait("sync", VD, w2marks[c - 2])
            dma_c("sync", stg, A["a1_W2"][c * P:(c + 1) * P, :], W2L[sl])
            wait("vector", W2L[sl])
            v = vop(lambda eng, c=c, stg=stg: eng.tensor_copy(
                sb["kvc"][:, c * D:(c + 1) * D], stg))
            w2marks.append(v)
        BV2 = psem("p7b2")
        dma_c("sync", sb["rst"][0:1, 0:D], A["a1_b2"], BV2)
        wait("tensor", BV2)
        wait("tensor", VD, w2marks[-1])
        for go, gw in groups(D):
            top(lambda eng, go=go, gw=gw: eng.matmul(
                psB[:, go:go + gw], sm[0:1, SM_ONESROW:SM_ONESROW + 128],
                sb["rst"][0:1, go:go + gw], start=True, stop=True), inc=(go + gw == D))
        wait("vector", MM)
        b2rep = sb["rst"][:, D:2 * D]
        vb2 = vop(lambda eng, b2rep=b2rep: eng.tensor_copy(b2rep, psB[:, 0:D]))
        wait("tensor", VD, vb2)
        wait("tensor", TZ[0])
        wait("tensor", TZ[1])
        wait("sync", VD)
        HL = chans("p7hl", 2)
        AL7 = chans("p7al", 2)
        YS = chans("p7y", 2)
        for grp, (gq, gw) in enumerate(groups(TPC, 512)):
            wait("tensor", VD)
            for oc in range(16):
                wait("tensor", SD)
                ps = psA if oc % 2 == 0 else psB
                for c in range(NDC):
                    top(lambda eng, c=c, oc=oc, gq=gq, gw=gw, ps=ps: eng.matmul(
                        ps[:, 0:gw],
                        EW[:, c * 2048 + oc * P:c * 2048 + (oc + 1) * P],
                        sb["qt"][:, c * TPC + gq:c * TPC + gq + gw],
                        start=(c == 0), stop=(c == NDC - 1)), inc=(c == NDC - 1))
                wait("scalar", MM)
                sop(lambda eng, oc=oc, ps=ps, gw=gw: eng.activation(
                    out=sb["wbf"][:, oc * gw:(oc + 1) * gw], in_=ps[:, 0:gw],
                    func=ACT.Gelu_apprx_tanh, bias=b1c[:, oc:oc + 1]))
            for tg in range(gw // P):
                s = gq // P + tg
                wait("tensor", SD)
                wait("tensor", VD)
                for oh in range(2):
                    for c in range(2048 // P):
                        top(lambda eng, c=c, tg=tg, oh=oh, gw=gw: eng.matmul(
                            psA[:, oh * 512:(oh + 1) * 512],
                            sb["wbf"][:, c * gw + tg * P:  c * gw + (tg + 1) * P],
                            sb["kvc"][:, c * D + oh * 512:c * D + (oh + 1) * 512],
                            start=(c == 0), stop=(c == 2048 // P - 1)),
                            inc=(c == 2048 // P - 1 and oh == 1))
                sl = s % 2
                ht = sb["stage"][:, sl * D:(sl + 1) * D]
                xat = sb["stage"][:, (2 + sl) * D:(3 + sl) * D]
                if s >= 2:
                    wait("sync", VD, ym[s - 2])
                    wait("sync", YS[sl])
                dma_c("sync", ht, A["h_sp"][s * P:(s + 1) * P, :], HL[sl])
                dma_c("sync", xat, A["xacc_sp"][s * P:(s + 1) * P, :], AL7[sl])
                wait("vector", HL[sl])
                wait("vector", AL7[sl])
                wait("vector", MM)
                if s >= 2:
                    wait("vector", YS[sl])
                vop(lambda eng, ht=ht, xat=xat: eng.tensor_tensor(ht, ht, xat, ALU.add))
                vop(lambda eng, ht=ht, b2rep=b2rep: eng.tensor_tensor(ht, ht, b2rep, ALU.add))
                vy = vop(lambda eng, ht=ht: eng.tensor_tensor(ht, ht, psA[:, 0:D], ALU.add))
                ym.append(vy)
                wait("sync", VD, vy)
                dma_c("sync", A["y"][s * P:(s + 1) * P, :], ht, YS[sl])
        wait("sync", YS[0])
        wait("sync", YS[1])

    ym = []
    CCS6 = None
    phase6_ln_quant()
    ctx.barrier("p6l")
    CCS6 = phase6_kv_cc()
    phase6_ffn()
    ctx.barrier("p6f")
    phase7(CCS6)


    if "y_done" not in globals():
        pass
    ctx.finalize()
    return nc, dbg



def _in_maps(inputs, T):
    g = _build_host(inputs, T)
    maps = []
    for core in range(8):
        d = _percore_host(inputs, g, T, core)
        m = {}
        for k, v in g.items():
            if k.startswith("_"):
                continue
            if k in ("rope_ck", "rope_sk"):
                continue
            m[k] = np.asarray(v, np.float32)
        for k, v in d.items():
            m[k] = np.asarray(v, np.float32)
        maps.append(m)
    return maps


def kernel(**inputs):
    T = int(np.asarray(inputs["L0_attn_x"]).shape[1])
    nc, _ = build_program(T)
    maps = _in_maps(inputs, T)
    res = run_bass_kernel_spmd(nc, maps, core_ids=list(range(8)))
    TPC = T // 2
    x = np.asarray(inputs["L0_attn_x"])
    out = np.zeros((4, T, D), np.float32)
    for core in range(8):
        b, par = core // 2, core % 2
        pos = _own_positions(T, par)
        out[b][pos] = res.results[core]["y"]
    return out


# version 10
# speedup vs baseline: 1.0673x; 1.0673x over previous
import numpy as np
import ml_dtypes
import concourse.bass as bass
import concourse.mybir as mybir
from concourse.bass_utils import run_bass_kernel_spmd

F32 = mybir.dt.float32
BF16 = mybir.dt.bfloat16
I32 = mybir.dt.int32
ALU = mybir.AluOpType
ACT = mybir.ActivationFunctionType
AX = mybir.AxisListType.X
BF = ml_dtypes.bfloat16

P = 128
D = 1024
NDC = D // P
NEG = -1.0e30

ENGINES = ("sync", "gpsimd", "scalar", "vector", "tensor")



def _rope_tables(T):
    i = np.arange(0, 32, 2, dtype=np.float32) / np.float32(32.0)
    inv_freq = (1.0 / (10000.0 ** i.astype(np.float64)))
    t = np.arange(T, dtype=np.float64)
    fr = np.outer(t, inv_freq)
    cos = np.cos(fr).astype(np.float32)
    sin = np.sin(fr).astype(np.float32)
    rows = (np.arange(64) % 32) // 2
    C = cos[:, rows].T.copy()
    S = sin[:, rows].T.copy()
    return np.ascontiguousarray(C), np.ascontiguousarray(S)


def _j_mat(n):
    idx = np.arange(n)
    src = np.where(idx % 2 == 0, idx + 1, idx - 1)
    sgn = np.where(idx % 2 == 0, -1.0, 1.0).astype(np.float32)
    return src, sgn


def _wj(wt):
    d_in, d_out = wt.shape
    src, sgn = _j_mat(d_out)
    return np.ascontiguousarray(wt[:, src] * sgn[None, :])


def _own_positions(T, parity):
    nblk = T // P
    blocks = [b for b in range(nblk) if b % 2 == parity]
    return np.concatenate([np.arange(b * P, (b + 1) * P) for b in blocks])


def _build_host(inputs, T):
    g = {}
    t = lambda a: np.ascontiguousarray(np.asarray(a).T)

    wq_t = t(inputs["L0_attn_Wq"])
    wkv_t = t(inputs["L0_attn_Wkv"])
    wk_t = np.ascontiguousarray(wkv_t[:, 0:256])
    wv_t = np.ascontiguousarray(wkv_t[:, 256:512])
    HORDER = [0, 4, 1, 5, 2, 6, 3, 7, 8, 12, 9, 13, 10, 14, 11, 15]
    qperm = np.concatenate([np.arange(h * 64, (h + 1) * 64) for h in HORDER])
    wq_p = np.ascontiguousarray(wq_t[:, qperm])
    g["a0_wq"] = wq_p
    g["a0_wqj"] = _wj(wq_p)
    g["a0_wk"] = wk_t
    g["a0_wkj"] = _wj(wk_t)
    g["a0_wv"] = wv_t
    wout_t = t(inputs["L0_attn_Wout"])
    g["a0_wout"] = np.ascontiguousarray(wout_t[qperm, :])
    lam = np.asarray(inputs["L0_attn_lambda_init"], np.float32)
    g["a0_lambda"] = np.ascontiguousarray(lam[HORDER].reshape(1, 16))
    rw = np.asarray(inputs["L0_attn_rms_weight"], np.float32)
    g["a0_rmsw2"] = np.ascontiguousarray(np.tile(rw, 2).reshape(P, 1))

    for k in ["wq", "wk", "wv", "wo"]:
        g["f0_" + k] = t(inputs["L0_ffn_" + k])
    g["f0_gate"] = t(inputs["L0_ffn_gate_w"])
    for k in ["e0_up", "e0_down", "e1_w1", "e1_w2", "e1_down",
              "e2_l1", "e2_l2", "e2_l3", "e2_l4", "e3_down", "e3_up", "e3_out"]:
        g["f0_" + k] = t(inputs["L0_ffn_" + k])
    g["f0_n1"] = np.asarray(inputs["L0_ffn_n1_w"]).reshape(1, D)
    g["f0_n2"] = np.asarray(inputs["L0_ffn_n2_w"]).reshape(1, D)
    g["f0_qn_t"] = np.ascontiguousarray(np.tile(np.asarray(inputs["L0_ffn_qn_w"], np.float32), 16).reshape(1, D))
    g["f0_kn_t"] = np.ascontiguousarray(np.tile(np.asarray(inputs["L0_ffn_kn_w"], np.float32), 16).reshape(1, D))
    g["ident"] = np.eye(P, dtype=np.float32)

    for k in ["Wq", "Wk", "Wv", "Wo", "W1", "W2"]:
        g["a1_" + k] = np.ascontiguousarray(np.asarray(inputs["L1_attn_" + k], dtype=np.float32))
    for k in ["bq", "bk", "bv", "bo", "b1", "b2",
              "lnq_s", "lnq_b", "lnk_s", "lnk_b", "lnv_s", "lnv_b", "lnm_s", "lnm_b"]:
        v = np.asarray(inputs["L1_attn_" + k], dtype=np.float32)
        g["a1_" + k] = v.reshape(1, -1)

    g["f1_wg"] = t(inputs["L1_ffn_w_gate"])
    g["f1_wu"] = t(inputs["L1_ffn_w_up"])
    g["f1_wd"] = t(inputs["L1_ffn_w_down"])

    C, S = _rope_tables(T)
    g["rope_ck"] = C
    g["rope_sk"] = S
    g["onesrow"] = np.ones((1, P), np.float32)
    sel64 = np.zeros((P, 2), np.float32)
    sel64[0:64, 0] = 1.0
    sel64[64:128, 1] = 1.0
    g["sel64"] = sel64
    sel32 = np.zeros((P, 4), np.float32)
    for j in range(4):
        sel32[32 * j:32 * (j + 1), j] = 1.0
    g["sel32"] = sel32
    rep64 = np.zeros((2, P), np.float32)
    rep64[0, 0:64] = 1.0
    rep64[1, 64:128] = 1.0
    g["rep64"] = rep64
    rep32 = np.zeros((4, P), np.float32)
    for j in range(4):
        rep32[j, 32 * j:32 * (j + 1)] = 1.0
    g["rep32"] = rep32
    tri = np.triu(np.full((P, P), NEG, np.float32), k=1)
    g["_tri"] = tri
    zap1 = np.zeros((P, 1), np.float32)
    zap1[0:32] = 1.0
    zap1[64:96] = 1.0
    g["zap1"] = zap1
    g["zap2"] = (1.0 - zap1).astype(np.float32)
    return g


def _percore_host(inputs, g, T, core):
    b = core // 2
    par = core % 2
    pos = _own_positions(T, par)
    x = np.asarray(inputs["L0_attn_x"], dtype=np.float32)
    d = {}
    d["x_own"] = np.ascontiguousarray(x[b][pos])
    ppos = _own_positions(T, 1 - par)
    d["x_par"] = np.ascontiguousarray(x[b][ppos])
    loc = np.concatenate([pos, ppos])
    d["rope_ckl"] = np.ascontiguousarray(g["rope_ck"][:, loc])
    d["rope_skl"] = np.ascontiguousarray(g["rope_sk"][:, loc])
    tri = g["_tri"]
    zero = np.zeros((P, P), np.float32)
    full = np.full((P, P), NEG, np.float32)
    dm = np.stack([tri, full] if par == 0 else [tri, zero])
    d["diagmask"] = np.ascontiguousarray(dm)
    d["corner"] = np.asarray([[-999.0 * np.sqrt(float(D)) if par == 0 else 0.0]], np.float32)
    return d



class Sem:
    __slots__ = ("h", "n", "name")

    def __init__(self, h, name):
        self.h = h
        self.n = 0
        self.name = name

    def inc(self, instr, k=1):
        instr.then_inc(self.h, k)
        self.n += k
        return instr

    def wait(self, eng, val=None):
        eng.wait_ge(self.h, self.n if val is None else val)


class Ctx:

    def __init__(self, nc):
        self.nc = nc
        self.q = {e: [] for e in ENGINES}
        self._stack = []
        self.sems = {}
        self.dma_seen = []
        self.sbuf_bytes = 0

    def sem(self, name):
        if name not in self.sems:
            h = self.nc.semaphore(name)
            hh = h.__enter__()
            self._stack.append(h)
            self.sems[name] = Sem(hh, name)
        return self.sems[name]

    def init_pool(self, k=24):
        self.pool = [self.sem(f"ch{i}") for i in range(k)]
        self.pool_ptr = 0

    def take(self, k):
        assert self.pool_ptr + k <= len(self.pool), "chan pool exhausted in phase"
        out = self.pool[self.pool_ptr:self.pool_ptr + k]
        self.pool_ptr += k
        return out

    def sbuf(self, name, shape, dt):
        cm = self.nc.sbuf_tensor(name, shape, dt)
        t = cm.__enter__()
        self._stack.append(cm)
        self.sbuf_bytes += shape[1] * mybir.dt.size(dt)
        return t

    def psum(self, name, shape=(P, 512), dt=F32):
        cm = self.nc.psum_tensor(name, list(shape), dt)
        t = cm.__enter__()
        self._stack.append(cm)
        return t

    def emit(self, eng, fn):
        self.q[eng].append(fn)

    def barrier(self, tag):
        for ch in self.dma_seen:
            v = ch.n
            self.emit("sync", lambda eng, h=ch.h, v=v: eng.wait_ge(h, v))
        s = self.sem("bar")
        for e in ENGINES:
            def go(eng, s=s):
                eng.nop().then_inc(s.h, 1)
            self.emit(e, go)
        s.n += len(ENGINES)
        tgt = s.n
        for e in ENGINES:
            def wt(eng, s=s, tgt=tgt):
                eng.wait_ge(s.h, tgt)
            self.emit(e, wt)
        self.pool_ptr = 0

    def finalize(self):
        with self.nc.Block() as block:
            for ename in ENGINES:
                dec = getattr(block, ename)
                fns = self.q[ename]

                def body(eng, fns=fns):
                    for f in fns:
                        f(eng)

                dec(body)
        for cm in reversed(self._stack):
            cm.__exit__(None, None, None)



def _cfg(T):
    c = {}
    c["T"] = T
    c["TPC"] = T // 2
    c["NT"] = c["TPC"] // P
    c["NKC"] = T // P
    return c


def build_program(T, debug=()):
    cfg = _cfg(T)
    TPC, NT, NKC = cfg["TPC"], cfg["NT"], cfg["NKC"]
    nc = bass.Bass("TRN2", target_bir_lowering=False, debug=False)
    ctx = Ctx(nc)
    dbg = {}

    def dram_in(name, shape, dt=F32):
        return nc.dram_tensor(name, list(shape), dt, kind="ExternalInput").ap()

    def dram_int(name, shape, dt=F32):
        return nc.dram_tensor(name, list(shape), dt).ap()

    def dram_out(name, shape, dt=F32):
        return nc.dram_tensor(name, list(shape), dt, kind="ExternalOutput").ap()

    def dbg_out(name, shape, dt=F32):
        if name in debug:
            dbg[name] = dram_out("dbg_" + name, shape, dt)
            return dbg[name]
        return None

    A = {}
    A["x_own"] = dram_in("x_own", [TPC, D])
    A["x_par"] = dram_in("x_par", [TPC, D])
    for n, sh in [
        ("a0_wq", [D, D]), ("a0_wqj", [D, D]), ("a0_wk", [D, 256]), ("a0_wkj", [D, 256]),
        ("a0_wv", [D, 256]), ("a0_wout", [D, D]), ("a0_lambda", [1, 16]), ("a0_rmsw2", [P, 1]),
        ("f0_wq", [D, D]), ("f0_wk", [D, D]), ("f0_wv", [D, D]), ("f0_wo", [D, D]),
        ("f0_gate", [D, 4]),
        ("f0_e0_up", [D, 2048]), ("f0_e0_down", [2048, D]),
        ("f0_e1_w1", [D, 2048]), ("f0_e1_w2", [D, 2048]), ("f0_e1_down", [2048, D]),
        ("f0_e2_l1", [D, 2048]), ("f0_e2_l2", [2048, 2048]), ("f0_e2_l3", [2048, 2048]),
        ("f0_e2_l4", [2048, D]),
        ("f0_e3_down", [D, D]), ("f0_e3_up", [D, 2048]), ("f0_e3_out", [2048, D]),
        ("f0_n1", [1, D]), ("f0_n2", [1, D]), ("f0_qn_t", [1, D]), ("f0_kn_t", [1, D]),
        ("ident", [P, P]),
        ("a1_Wq", [D, D]), ("a1_Wk", [D, D]), ("a1_Wv", [D, D]), ("a1_Wo", [D, D]),
        ("a1_W1", [D, 2048]), ("a1_W2", [2048, D]),
        ("a1_bq", [1, D]), ("a1_bk", [1, D]), ("a1_bv", [1, D]), ("a1_bo", [1, D]),
        ("a1_b1", [1, 2048]), ("a1_b2", [1, D]),
        ("a1_lnq_s", [1, D]), ("a1_lnq_b", [1, D]), ("a1_lnk_s", [1, D]), ("a1_lnk_b", [1, D]),
        ("a1_lnv_s", [1, D]), ("a1_lnv_b", [1, D]), ("a1_lnm_s", [1, D]), ("a1_lnm_b", [1, D]),
        ("f1_wg", [D, 4096]), ("f1_wu", [D, 4096]), ("f1_wd", [4096, D]),
        ("rope_ckl", [64, T]), ("rope_skl", [64, T]),
        ("onesrow", [1, P]), ("sel64", [P, 2]), ("sel32", [P, 4]),
        ("rep64", [2, P]), ("rep32", [4, P]),
        ("diagmask", [2, P, P]), ("corner", [1, 1]), ("zap1", [P, 1]), ("zap2", [P, 1]),
    ]:
        A[n] = dram_in(n, sh)

    A["xn_sp"] = dram_int("xn_sp", [TPC, D])
    A["xnT_sp"] = dram_int("xnT_sp", [P, NDC * T], BF16)
    A["att0_sp"] = dram_int("att0_sp", [TPC, D])
    A["xmid_sp"] = dram_int("xmid_sp", [TPC, D])
    A["xsum_sp"] = dram_int("xsum_sp", [TPC, D])
    A["xn1_sp"] = dram_int("xn1_sp", [TPC, D])
    A["xn1T_sp"] = dram_int("xn1T_sp", [P, NDC * TPC], BF16)
    A["gu_sp"] = dram_int("gu_sp", [TPC, 4096])
    A["xacc_sp"] = dram_int("xacc_sp", [TPC, D])
    A["h_sp"] = dram_int("h_sp", [TPC, D])
    A["bv1_sp"] = dram_int("bv1_sp", [1, 2048])
    A["cc_in"] = dram_int("cc_in", [P, 2 * NDC * TPC + 128], BF16)
    A["cc_out"] = dram_int("cc_out", [2, P, 2 * NDC * TPC + 128], BF16)
    A["wq_sp"] = {}
    for _n, _di, _do in [("f1_wg", D, 4096), ("f1_wu", D, 4096), ("f1_wd", 4096, D)]:
        A["wq_sp"][_n] = dram_int(_n + "_q", [_di, _do], BF16)
    A["ew_sp"] = {}
    for _n, _di, _do, _down in [
        ("f0_e0_up", D, 2048, False), ("f0_e0_down", 2048, 768, True),
        ("f0_e1_w1", D, 2048, False), ("f0_e1_w2", D, 2048, False),
        ("f0_e1_down", 2048, 768, True),
        ("f0_e2_l1", D, 2048, False), ("f0_e2_l2", 2048, 2048, False),
        ("f0_e2_l3", 2048, 2048, False), ("f0_e2_l4", 2048, 768, True),
        ("f0_e3_down", D, D, False), ("f0_e3_up", D, 2048, False),
        ("f0_e3_out", 2048, 768, True),
    ]:
        A["ew_sp"][_n] = dram_int(_n + "_bq", [_di, _do], BF16)
    A["k0f_sp"] = dram_int("k0f_sp", [P, NDC * T], BF16)
    A["v0f_sp"] = dram_int("v0f_sp", [P, (T // P) * D], BF16)
    A["y"] = dram_out("y", [TPC, D])

    sb = {}
    sb["stage"] = ctx.sbuf("stage", [P, 4 * D], F32)
    sb["stat"] = ctx.sbuf("stat", [P, 1800], F32)
    sb["bf"] = ctx.sbuf("bfs", [P, 4096], BF16)
    sb["xnT"] = ctx.sbuf("xnT", [P, max(NDC * T, 32768)], BF16)
    sb["wbf"] = ctx.sbuf("wbf", [P, 16384], BF16)
    sb["qt"] = ctx.sbuf("qt", [P, max(NDC * TPC, 4096)], BF16)
    sb["kvc"] = ctx.sbuf("kvc", [P, 16384], BF16)
    sb["rst"] = ctx.sbuf("rst", [P, 2048], F32)
    sb["bwork"] = ctx.sbuf("bwork", [P, 512], F32)
    sb["small"] = ctx.sbuf("small", [P, 768], F32)

    psA = ctx.psum("psA", (P, 2048))
    psB = ctx.psum("psB", (P, 2048))

    LD = ctx.sem("ld")
    G = ctx.sem("gd")
    TP = ctx.sem("tp")
    VD = ctx.sem("vd")
    SD = ctx.sem("sd")
    MM = ctx.sem("mm")
    ST = ctx.sem("st")

    E = ctx.emit

    def dma(qeng, dst, src, sem):
        def go(eng, dst=dst, src=src, h=sem.h):
            eng.dma_start(out=dst, in_=src).then_inc(h, 16)
        E(qeng, go)
        sem.n += 16
        return sem.n

    def dmat(dst, src):
        def go(eng, dst=dst, src=src, h=TP.h):
            eng.dma_start_transpose(dst, src).then_inc(h, 16)
        E("scalar", go)
        TP.n += 16
        return TP.n

    def wait(eng_name, sem, val=None):
        v = sem.n if val is None else val
        E(eng_name, lambda eng, h=sem.h, v=v: eng.wait_ge(h, v))

    def vop(fn):
        v = VD.n
        E("vector", lambda eng, fn=fn, h=VD.h, v=v: (eng.wait_ge(h, v), fn(eng).then_inc(h, 1)))
        VD.n += 1
        return VD.n

    def sop(fn):
        v = SD.n
        E("scalar", lambda eng, fn=fn, h=SD.h, v=v: (eng.wait_ge(h, v), fn(eng).then_inc(h, 1)))
        SD.n += 1
        return SD.n

    def top(fn, inc=True):
        if inc:
            E("tensor", lambda eng, fn=fn, h=MM.h: fn(eng).then_inc(h, 1))
            MM.n += 1
            return MM.n
        E("tensor", lambda eng, fn=fn: fn(eng))
        return MM.n

    ctx.init_pool(40)

    def chans(name, k):
        return ctx.take(k)

    def psem(name):
        return ctx.take(1)[0]

    def t_dst(buf, blk, tot=None):
        tot = T if tot is None else tot
        return buf[:, 0:NDC * tot].rearrange("p (c t) -> p c t", t=tot)[:, :, blk * P:(blk + 1) * P]

    def groups(total, w=512):
        out = []
        o = 0
        while o < total:
            out.append((o, min(w, total - o)))
            o += min(w, total - o)
        return out

    def bcast64(ap2d):
        return ap2d.unsqueeze(2).broadcast_to([P, ap2d.shape[1], 64])

    def dma_c(qeng, dst, src, ch, slow=False):
        if ch not in ctx.dma_seen:
            ctx.dma_seen.append(ch)
        def go(eng, dst=dst, src=src, h=ch.h, slow=slow):
            eng.dma_start(out=dst, in_=src,
                          allow_slow_non_contiguous=slow).then_inc(h, 16)
        E(qeng, go)
        ch.n += 16
        return ch.n

    def dmat_c(dst, src, ch):
        if ch not in ctx.dma_seen:
            ctx.dma_seen.append(ch)
        def go(eng, dst=dst, src=src, h=ch.h):
            eng.dma_start_transpose(dst, src).then_inc(h, 16)
        E("sync", go)
        ch.n += 16
        return ch.n

    def ln_finalize(nt2, eps, base=0):
        st = sb["stat"]
        o = base
        mu_raw, ss = st[:, o:o + nt2], st[:, o + 64:o + 64 + nt2]
        mu, rstd = st[:, o + 128:o + 128 + nt2], st[:, o + 192:o + 192 + nt2]
        var = st[:, o + 256:o + 256 + nt2]
        tmp = st[:, o + 320:o + 320 + nt2]
        invd = 1.0 / D
        wait("vector", SD)
        vop(lambda eng: eng.tensor_scalar(mu, mu_raw, invd, None, ALU.mult))
        vop(lambda eng: eng.tensor_scalar(var, ss, invd, None, ALU.mult))
        vop(lambda eng: eng.tensor_tensor(tmp, mu, mu, ALU.mult))
        vop(lambda eng: eng.tensor_tensor(var, var, tmp, ALU.subtract))
        vop(lambda eng: eng.tensor_scalar(var, var, eps, None, ALU.add))
        wait("scalar", VD)
        sop(lambda eng: eng.activation(out=tmp, in_=var, func=ACT.Sqrt))
        wait("vector", SD)
        vop(lambda eng: eng.reciprocal(rstd, tmp))
        return mu, rstd

    def phase1():
        st = sb["stat"]
        nt2 = 2 * NT
        assert nt2 <= 64
        Lx = chans("p1lx", 2)
        Sp = chans("p1sp", 2)
        Tp = chans("p1tp", 2)
        vmark, smark = [], []
        for i in range(nt2):
            src = A["x_own"] if i < NT else A["x_par"]
            row = (i % NT) * P
            sl = i % 2
            xa = sb["stage"][:, sl * D:(sl + 1) * D]
            if i >= 2:
                wait("sync", VD, vmark[i - 2])
                wait("sync", SD, smark[i - 2])
            dma_c("sync", xa, src[row:row + P, :], Lx[sl])
            wait("vector", Lx[sl])
            wait("scalar", Lx[sl])
            v = vop(lambda eng, xa=xa, c=st[:, i:i + 1]: eng.reduce_sum(c, xa, AX))
            s_ = sop(lambda eng, xa=xa, c=st[:, 64 + i:64 + i + 1],
                     sq=sb["stage"][:, (2 + i % 2) * D:(3 + i % 2) * D]:
                     eng.activation(out=sq, in_=xa, func=ACT.Square, accum_out=c))
            vmark.append(v)
            smark.append(s_)
        ln_finalize(nt2, 1e-5)
        wait("sync", VD)
        wait("sync", SD)
        wait("vector", SD)
        vts, vcast, sqm = [], [], []
        for i in range(nt2):
            own = i < NT
            row = (i % NT) * P
            src = A["x_own"] if own else A["x_par"]
            sl = i % 2
            xa = sb["stage"][:, sl * D:(sl + 1) * D]
            xn = sb["stage"][:, (2 + sl) * D:(3 + sl) * D]
            xb = sb["bf"][:, sl * D:(sl + 1) * D]
            if i >= 2:
                wait("sync", VD, vts[i - 2])
            dma_c("sync", xa, src[row:row + P, :], Lx[sl])
            wait("vector", Lx[sl])
            if i >= 2:
                wait("vector", Sp[sl])
                wait("vector", SD, sqm[i - 2])
            mu = st[:, 128 + i:128 + i + 1]
            r = st[:, 192 + i:192 + i + 1]
            v1 = vop(lambda eng, xn=xn, xa=xa, mu=mu, r=r:
                     eng.tensor_scalar(xn, xa, mu, r, ALU.subtract, ALU.mult))
            vts.append(v1)
            if i >= 2:
                wait("vector", Tp[sl])
            v2 = vop(lambda eng, xb=xb, xn=xn: eng.tensor_copy(xb, xn))
            vcast.append(v2)
            wait("scalar", VD, v1)
            sqm.append(sop(lambda eng, xn=xn, i=i, sq=sb["bf"][:, (2 + i % 2) * D:(3 + i % 2) * D]:
                eng.activation(out=sq, in_=xn, func=ACT.Square, accum_out=st[:, 384 + i:385 + i])))
            if own:
                wait("sync", VD, v1)
                dma_c("sync", A["xn_sp"][row:row + P, :], xn, Sp[sl])
            blk = (i % NT) if own else NT + (i % NT)
            wait("sync", VD, v2)
            dmat_c(t_dst(sb["xnT"], blk), xb, Tp[sl])
        wait("vector", SD)
        vop(lambda eng: eng.tensor_scalar(st[:, 448:448 + nt2], st[:, 384:384 + nt2],
                                          1.0 / D, 1.1920929e-07, ALU.mult, ALU.add))
        v = vop(lambda eng: eng.tensor_scalar(st[:, 448:448 + nt2], st[:, 448:448 + nt2],
                                              1.0, None, ALU.mult))
        wait("scalar", VD, v)
        sr = sop(lambda eng: eng.activation(out=st[:, 384:384 + nt2], in_=st[:, 448:448 + nt2],
                                            func=ACT.Sqrt))
        wait("vector", SD, sr)
        vop(lambda eng: eng.reciprocal(st[:, 448:448 + nt2], st[:, 384:384 + nt2]))
        for c in (Tp[0], Tp[1]):
            wait("sync", c)
        dma_c("sync", A["xnT_sp"][:], sb["xnT"][:, 0:NDC * T], LD)

    phase1()
    ctx.barrier("p1")

    SC0 = float(32.0 ** -0.5)

    SM_DM0, SM_DM1 = 0, 128
    SM_LREP = 256
    SM_ONES = 272
    SM_RMSW = 273
    SM_SEL32 = 274
    SM_SEL64 = 278
    SM_KMAX = 280
    SM_ONESROW = 283
    SM_REP64 = 411
    SM_LSIG = 539
    SM_ZAP1 = 555
    SM_ZAP2 = 556

    def phase2_setup():
        sm = sb["small"]
        C = psem("p2c")
        dma_c("sync", sm[:, SM_DM0:SM_DM0 + 128], A["diagmask"][0], C)
        dma_c("sync", sm[:, SM_DM1:SM_DM1 + 128], A["diagmask"][1], C)
        dma_c("sync", sm[:, SM_SEL32:SM_SEL32 + 4], A["sel32"], C)
        dma_c("sync", sm[:, SM_SEL64:SM_SEL64 + 2], A["sel64"], C)
        dma_c("sync", sm[0:2, SM_REP64:SM_REP64 + 128], A["rep64"], C)
        dma_c("sync", sm[0:1, SM_ONESROW:SM_ONESROW + 128], A["onesrow"], C)
        dma_c("sync", sm[32:33, SM_ONESROW:SM_ONESROW + 128], A["onesrow"], C)
        dma_c("sync", sm[0:1, SM_LSIG:SM_LSIG + 16], A["a0_lambda"], C)
        dma_c("sync", sm[:, SM_RMSW:SM_RMSW + 1], A["a0_rmsw2"], C)
        dma_c("sync", sm[:, SM_ZAP1:SM_ZAP1 + 1], A["zap1"], C)
        dma_c("sync", sm[:, SM_ZAP2:SM_ZAP2 + 1], A["zap2"], C)
        wait("vector", C)
        wait("scalar", C)
        vop(lambda eng: eng.memset(sm[:, SM_ONES:SM_ONES + 1], 1.0))
        vop(lambda eng: eng.tensor_scalar(sm[:, SM_RMSW:SM_RMSW + 1],
                                          sm[:, SM_RMSW:SM_RMSW + 1], -1.0, None, ALU.mult))
        s1 = sop(lambda eng: eng.activation(out=sm[0:1, SM_LSIG:SM_LSIG + 16],
                                            in_=sm[0:1, SM_LSIG:SM_LSIG + 16], func=ACT.Sigmoid))
        wait("tensor", SD, s1)
        wait("tensor", VD)
        m = top(lambda eng: eng.matmul(psA[0:P, 0:16], sm[0:1, SM_ONESROW:SM_ONESROW + 128],
                                       sm[0:1, SM_LSIG:SM_LSIG + 16], start=True, stop=True))
        wait("vector", MM, m)
        vop(lambda eng: eng.tensor_copy(sm[:, SM_LREP:SM_LREP + 16], psA[:, 0:16]))

    def stream_weight(src_ap, nchunks, ncols, dst_base, Wch, fold=None):
        wait("sync", VD)
        marks = []
        for c in range(nchunks):
            sl = c % 2
            stg = sb["stage"][:, sl * D:sl * D + ncols]
            if c >= 2:
                wait("sync", VD, marks[c - 2])
            dma_c("sync", stg, src_ap[c * P:(c + 1) * P, :], Wch[sl])
            wait("vector", Wch[sl])
            dst = sb["wbf"][:, dst_base + c * ncols: dst_base + (c + 1) * ncols]
            if fold is None:
                v = vop(lambda eng, d=dst, s=stg: eng.tensor_copy(d, s))
            else:
                v = vop(lambda eng, d=dst, s=stg, f=fold(c):
                        eng.tensor_scalar(d, s, f, None, ALU.mult))
            marks.append(v)
        return marks

    def rope_proj(n_oc, wbase, wcols, TT_, out_buf, out_base, out_stride, Rch):
        wbase2 = wbase + NDC * wcols
        for oc in range(n_oc):
            for gi, (go, gw) in enumerate(groups(TT_)):
                sl = gi % 2
                cs = sb["rst"][:, sl * 1024: sl * 1024 + gw]
                sn = sb["rst"][:, sl * 1024 + 512: sl * 1024 + 512 + gw]
                colr = slice(go, go + gw)
                wait("sync", VD)
                dma_c("sync", cs[0:64, :], A["rope_ckl"][0:64, colr], Rch[sl])
                dma_c("sync", cs[64:128, :], A["rope_ckl"][0:64, colr], Rch[sl])
                dma_c("sync", sn[0:64, :], A["rope_skl"][0:64, colr], Rch[sl])
                dma_c("sync", sn[64:128, :], A["rope_skl"][0:64, colr], Rch[sl])
                wait("tensor", VD)
                for c in range(NDC):
                    top(lambda eng, c=c, oc=oc, go=go, gw=gw:
                        eng.matmul(psA[:, 0:gw],
                                   sb["wbf"][:, wbase + c * wcols + oc * P: wbase + c * wcols + (oc + 1) * P],
                                   sb["xnT"][:, c * T + go: c * T + go + gw],
                                   start=(c == 0), stop=(c == NDC - 1)), inc=(c == NDC - 1))
                for c in range(NDC):
                    top(lambda eng, c=c, oc=oc, go=go, gw=gw:
                        eng.matmul(psB[:, 0:gw],
                                   sb["wbf"][:, wbase2 + c * wcols + oc * P: wbase2 + c * wcols + (oc + 1) * P],
                                   sb["xnT"][:, c * T + go: c * T + go + gw],
                                   start=(c == 0), stop=(c == NDC - 1)), inc=(c == NDC - 1))
                wait("vector", MM)
                wait("vector", Rch[sl])
                t1 = sb["stage"][:, 2 * D:2 * D + gw]
                t2 = sb["stage"][:, 2 * D + 512:2 * D + 512 + gw]
                vop(lambda eng, t1=t1, cs=cs, gw=gw: eng.tensor_tensor(t1, psA[:, 0:gw], cs, ALU.mult))
                vop(lambda eng, t2=t2, sn=sn, gw=gw: eng.tensor_tensor(t2, psB[:, 0:gw], sn, ALU.mult))
                dst = out_buf[:, out_base + oc * out_stride + go:
                              out_base + oc * out_stride + go + gw]
                vop(lambda eng, d=dst, t1=t1, t2=t2: eng.tensor_tensor(d, t1, t2, ALU.add))

    def phase2_proj():
        Wch = chans("p2w", 2)
        Rch = chans("p2r", 2)
        stream_weight(A["a0_wk"], NDC, 256, 0, Wch)
        stream_weight(A["a0_wkj"], NDC, 256, 2048, Wch)
        stream_weight(A["a0_wv"], NDC, 256, 4096, Wch)
        rope_proj(2, 0, 256, T, sb["kvc"], 0, T, Rch)
        mk = []
        for tt in range(NKC):
            wait("tensor", VD)
            for c in range(NDC):
                top(lambda eng, c=c, tt=tt:
                    eng.matmul(psA[:, 512:768],
                               sb["xnT"][:, c * T + tt * P:c * T + (tt + 1) * P],
                               sb["wbf"][:, 4096 + c * 256: 4096 + (c + 1) * 256],
                               start=(c == 0), stop=(c == NDC - 1)), inc=(c == NDC - 1))
            wait("vector", MM)
            v = vop(lambda eng, tt=tt: eng.tensor_copy(
                sb["kvc"][:, 2 * T + tt * 256: 2 * T + (tt + 1) * 256], psA[:, 512:768]))
            mk.append(v)
        wait("sync", MM)
        wait("sync", VD)
        stream_weight(A["a0_wq"], NDC, D, 0, Wch)
        stream_weight(A["a0_wqj"], NDC, D, 8192, Wch)
        rope_proj(NDC, 0, D, TPC, sb["qt"], 0, TPC, Rch)

    def phase2_bounds():
        sm = sb["small"]
        for cc in range(2):
            for hi, (ho, hw) in enumerate(groups(T, 2048)):
                sq = sb["stage"][:, 0:hw]
                s_ = sop(lambda eng, cc=cc, ho=ho, hw=hw, sq=sq: eng.activation(
                    out=sq, in_=sb["kvc"][:, cc * T + ho: cc * T + ho + hw], func=ACT.Square))
                wait("tensor", SD, s_)
                wait("tensor", VD)
                for gi, (go, gw) in enumerate(groups(hw)):
                    top(lambda eng, go=go, gw=gw, sq=sq:
                        eng.matmul(psB[0:4, go:go + gw], sm[:, SM_SEL32:SM_SEL32 + 4],
                                   sq[:, go:go + gw], start=True, stop=True),
                        inc=(go + gw == hw))
                wait("vector", MM)
                vop(lambda eng, hi=hi, hw=hw: eng.reduce_max(
                    sb["rst"][0:4, hi:hi + 1], psB[0:4, 0:hw], AX))
                wait("scalar", VD)
            nh = len(groups(T, 2048))
            v = vop(lambda eng, cc=cc, nh=nh: eng.reduce_max(
                sm[0:4, SM_KMAX + cc:SM_KMAX + cc + 1], sb["rst"][0:4, 0:nh], AX))
        BT = psem("p2bt")
        vop(lambda eng: eng.memset(sb["bf"][0:16, 0:TPC], 0.0))
        for j in range(NDC):
            sq = sb["stage"][:, 0:TPC]
            s_ = sop(lambda eng, j=j, sq=sq: eng.activation(
                out=sq, in_=sb["qt"][:, j * TPC:(j + 1) * TPC], func=ACT.Square))
            wait("tensor", SD, s_)
            wait("tensor", VD)
            for gi, (go, gw) in enumerate(groups(TPC)):
                top(lambda eng, go=go, gw=gw, sq=sq: eng.matmul(
                    psA[0:4, go:go + gw], sm[:, SM_SEL32:SM_SEL32 + 4],
                    sq[:, go:go + gw], start=True, stop=True), inc=(go + gw == TPC))
            wait("vector", MM)
            cc = 0 if j < NDC // 2 else 1
            v = vop(lambda eng, cc=cc: eng.tensor_scalar(
                sb["rst"][0:4, 0:TPC], psA[0:4, 0:TPC],
                sm[0:4, SM_KMAX + cc:SM_KMAX + cc + 1], None, ALU.mult))
            wait("scalar", VD, v)
            s2 = sop(lambda eng: eng.activation(
                out=sb["rst"][0:4, 0:TPC], in_=sb["rst"][0:4, 0:TPC],
                func=ACT.Sqrt, scale=SC0 * SC0))
            wait("vector", SD, s2)
            wait("vector", BT)
            v = vop(lambda eng: eng.tensor_scalar(
                sb["bf"][0:4, 0:TPC], sb["rst"][0:4, 0:TPC], -1.0, None, ALU.mult))
            wait("sync", VD, v)
            dmat_c(sb["bf"][:, 2048 + j * NT * 16: 2048 + (j + 1) * NT * 16]
                   .rearrange("p (t s) -> p t s", s=16),
                   sb["bf"][0:16, 0:TPC], BT)
        wait("vector", BT)
        vop(lambda eng: eng.tensor_copy(
            sb["bwork"][:, 0:NT * 4 * NDC],
            sb["bf"][:, 2048:2048 + NT * 16 * NDC]
            .rearrange("p (g s) -> p g s", s=16)[:, :, 0:4]))

    def phase2_attn():
        EW = sb["xnT"]
        E = {}
        E[(0, 0)], E[(0, 1)], E[(1, 0)], E[(1, 1)] = 0, T, 2 * T, 3 * T
        PTB = 4 * T
        st = sb["stat"]
        sm = sb["small"]
        TpP = chans("p2pt", 2)
        NT2 = NT // 2
        wait("scalar", LD)
        wait("vector", LD)
        pend_pv = [None]
        bank_sd = {0: 0, 1: 0}
        pv_evac = [0]
        cmb_mark = [0]

        def pt_plane(nbm, hh, x):
            return PTB + (hh * 2 * nbm + x) * 256

        def flush_pv():
            if pend_pv[0] is None:
                return
            j, g, nbm, gA = pend_pv[0]
            pend_pv[0] = None
            zm = 0
            for hh in range(2):
                base = pt_plane(nbm, hh, 2 * nbm - 2)
                zm = vop(lambda eng, base=base: eng.memset(
                    EW[:, base:base + 512].rearrange("p (x s q) -> p x s q", s=2, q=P)
                    [:, :, 0, :], 0.0))
            wait("tensor", VD, zm)
            wait("tensor", VD, pv_evac[0])
            wait("tensor", SD, bank_sd[1])
            wait("tensor", TpP[0])
            wait("tensor", TpP[1])
            for x in range(2 * nbm):
                bb, r = x >> 1, x & 1
                tile = bb if r == 0 else NT + bb
                for hh in range(2):
                    ps = psA if hh == 0 else psB
                    top(lambda eng, x=x, tile=tile, hh=hh, gA=gA, nbm=nbm, ps=ps:
                        eng.matmul(ps[64 * hh:64 * hh + 64, 1536:1536 + 256],
                                   sb["kvc"][:, 2 * T + tile * 256 + (gA + hh) * 64:
                                             2 * T + tile * 256 + (gA + hh + 1) * 64],
                                   EW[:, pt_plane(nbm, hh, x):pt_plane(nbm, hh, x) + 256],
                                   start=(x == 0), stop=(x == 2 * nbm - 1),
                                   tile_position=(0, 64 * hh), skip_group_check=True),
                        inc=(x == 2 * nbm - 1 and hh == 1))
            wait("vector", MM)
            vop(lambda eng, j=j, g=g: eng.tensor_copy(
                sb["wbf"][0:64, j * TPC + 2 * g * P: j * TPC + (2 * g + 2) * P],
                psA[0:64, 1536:1536 + 256]))
            pv_evac[0] = vop(lambda eng, j=j, g=g: eng.tensor_copy(
                sb["wbf"][64:128, j * TPC + 2 * g * P: j * TPC + (2 * g + 2) * P],
                psB[64:128, 1536:1536 + 256]))

        def do_slot(j, s, kcb, nbm):
            nb = s + 1
            w = nb * P
            grps = groups(w, 1024)
            ngr = len(grps)
            segs = [(half, run, go, gw) for half in (1, 0)
                    for run in range(2) for (go, gw) in grps]
            first_exp = True
            first_h0 = True
            for i, (half, run, go, gw) in enumerate(segs):
                bcol = (i % 2) * 1024
                qzb = half * TPC
                tof = 0 if run == 0 else NT
                gi = grps.index((go, gw))
                if bank_sd[i % 2]:
                    wait("tensor", SD, bank_sd[i % 2])
                m = None
                for qo, qw in groups(gw):
                    top(lambda eng, go=go, qo=qo, qw=qw, qzb=qzb, s=s, kcb=kcb,
                        tof=tof, bcol=bcol:
                        eng.matmul(psA[:, bcol + qo:bcol + qo + qw],
                                   sb["bf"][0:64, qzb + s * P: qzb + (s + 1) * P],
                                   sb["kvc"][0:64, kcb + tof * P + go + qo:
                                             kcb + tof * P + go + qo + qw],
                                   start=True, stop=True, tile_position=(0, 0)),
                        inc=False)
                    m = top(lambda eng, go=go, qo=qo, qw=qw, qzb=qzb, s=s, kcb=kcb,
                            tof=tof, bcol=bcol:
                            eng.matmul(psB[:, bcol + qo:bcol + qo + qw],
                                       sb["bf"][64:128, qzb + s * P: qzb + (s + 1) * P],
                                       sb["kvc"][64:128, kcb + tof * P + go + qo:
                                                 kcb + tof * P + go + qo + qw],
                                       start=True, stop=True, tile_position=(64, 0)),
                            inc=True)
                has_diag = go <= s * P < go + gw
                if has_diag:
                    dcol = bcol + s * P - go
                    dm = sm[:, SM_DM0:SM_DM0 + 128] if run == 0 else sm[:, SM_DM1:SM_DM1 + 128]
                    wait("vector", MM, m)
                    va = vop(lambda eng, dcol=dcol, dm=dm: eng.tensor_tensor(
                        psA[:, dcol:dcol + P], psA[:, dcol:dcol + P], dm, ALU.add))
                    vb = vop(lambda eng, dcol=dcol, dm=dm: eng.tensor_tensor(
                        psB[:, dcol:dcol + P], psB[:, dcol:dcol + P], dm, ALU.add))
                if first_exp:
                    wait("scalar", VD, cmb_mark[0])
                    first_exp = False
                if half == 0 and first_h0:
                    wait("scalar", TpP[0])
                    wait("scalar", TpP[1])
                    first_h0 = False
                if has_diag:
                    wait("scalar", VD, vb)
                else:
                    wait("scalar", MM, m)
                for hh, ps in ((0, psA), (1, psB)):
                    strm = j * NT * 4 + s * 4 + 2 * hh + half
                    bias = sb["bwork"][:, strm:strm + 1]
                    zcol = st[:, 16 * hh + half * 2 * ngr + run * ngr + gi:
                              16 * hh + half * 2 * ngr + run * ngr + gi + 1]
                    e_ = sop(lambda eng, go=go, gw=gw, w=w, run=run, hh=hh, half=half,
                             bias=bias, zcol=zcol, ps=ps, bcol=bcol:
                             eng.activation(
                                 out=EW[:, E[(hh, half)]:E[(hh, half)] + 2 * w]
                                 .rearrange("p (b r q) -> p b r q", r=2, q=P)
                                 [:, go // P:(go + gw) // P, run, :],
                                 in_=ps[:, bcol:bcol + gw], func=ACT.Exp,
                                 bias=bias, scale=SC0, accum_out=zcol))
                bank_sd[i % 2] = SD.n
            wait("vector", SD, bank_sd[(len(segs) - 1) % 2])
            for hh in range(2):
                for half in range(2):
                    cbase = 16 * hh + half * 2 * ngr
                    vop(lambda eng, hh=hh, half=half, cbase=cbase, ngr=ngr:
                        eng.reduce_sum(st[:, 32 + 2 * hh + half:33 + 2 * hh + half],
                                       st[:, cbase:cbase + 2 * ngr], AX))
            vop(lambda eng: eng.reciprocal(st[:, 36:40], st[:, 32:36]))
            vop(lambda eng, j=j: eng.tensor_tensor(
                st[:, 37:38], st[:, 37:38], sm[:, SM_LREP + 2 * j:SM_LREP + 2 * j + 1],
                ALU.mult))
            vop(lambda eng, j=j: eng.tensor_tensor(
                st[:, 39:40], st[:, 39:40], sm[:, SM_LREP + 2 * j + 1:SM_LREP + 2 * j + 2],
                ALU.mult))
            w2 = 2 * w
            for hh in range(2):
                e1 = EW[:, E[(hh, 0)]:E[(hh, 0)] + w2]
                e2 = EW[:, E[(hh, 1)]:E[(hh, 1)] + w2]
                vop(lambda eng, e1=e1, hh=hh: eng.tensor_scalar(
                    e1, e1, st[:, 36 + 2 * hh:37 + 2 * hh], None, ALU.mult))
                vf = vop(lambda eng, e1=e1, e2=e2, hh=hh: eng.scalar_tensor_tensor(
                    e1, e2, st[:, 37 + 2 * hh:38 + 2 * hh], e1, ALU.mult, ALU.subtract))
                cmb_mark[0] = vf

        def do_dt(j, s, nbm, si):
            w2 = 2 * (s + 1) * P
            for hh in range(2):
                e1 = EW[:, E[(hh, 0)]:E[(hh, 0)] + w2]
                wait("sync", VD, cmb_mark[0])
                dmat_c(EW[:, pt_plane(nbm, hh, 0):pt_plane(nbm, hh, 0) + 2 * nbm * 256]
                       .rearrange("p (x s q) -> p x s q", s=2, q=P)
                       [:, 0:2 * (s + 1), si, :], e1, TpP[hh])

        for j in range(NDC):
            kcb = (0 if j < NDC // 2 else 1) * T
            gA = 2 * (j // (NDC // 2))
            wait("vector", MM)
            v1 = vop(lambda eng, j=j: eng.tensor_scalar(
                sb["bf"][:, 0:TPC], sb["qt"][:, j * TPC:(j + 1) * TPC],
                sm[:, SM_ZAP1:SM_ZAP1 + 1], None, ALU.mult))
            v2 = vop(lambda eng, j=j: eng.tensor_scalar(
                sb["bf"][:, TPC:2 * TPC], sb["qt"][:, j * TPC:(j + 1) * TPC],
                sm[:, SM_ZAP2:SM_ZAP2 + 1], None, ALU.mult))
            wait("tensor", VD, v2)
            for g in range(NT2):
                s0, s1 = 2 * g, 2 * g + 1
                nbm = 2 * g + 2
                do_slot(j, s0, kcb, nbm)
                flush_pv()
                wait("sync", MM)
                do_dt(j, s0, nbm, 0)
                wait("tensor", VD, pv_evac[0])
                do_slot(j, s1, kcb, nbm)
                do_dt(j, s1, nbm, 1)
                pend_pv[0] = (j, g, nbm, gA)
        flush_pv()

    def phase2_rms_wout():
        sm = sb["small"]
        atT = sb["wbf"]
        Wch = chans("p2wo", 2)
        marks = []
        for c in range(NDC):
            sl = c % 2
            stg = sb["stage"][:, sl * D:(sl + 1) * D]
            if c >= 2:
                wait("sync", VD, marks[c - 2])
            dma_c("sync", stg, A["a0_wout"][c * P:(c + 1) * P, :], Wch[sl])
            wait("vector", Wch[sl])
            v = vop(lambda eng, c=c, stg=stg: eng.tensor_scalar(
                sb["kvc"][:, c * D:(c + 1) * D], stg, sm[:, SM_RMSW:SM_RMSW + 1], None, ALU.mult))
            marks.append(v)
        wait("scalar", VD, marks[-1])
        wait("tensor", VD, marks[-1])
        for j in range(NDC):
            sq = sb["stage"][:, 0:TPC]
            s_ = sop(lambda eng, j=j, sq=sq: eng.activation(
                out=sq, in_=atT[:, j * TPC:(j + 1) * TPC], func=ACT.Square))
            wait("tensor", SD, s_)
            wait("tensor", VD)
            for go, gw in groups(TPC):
                top(lambda eng, go=go, gw=gw, sq=sq: eng.matmul(
                    psA[0:2, go:go + gw], sm[:, SM_SEL64:SM_SEL64 + 2],
                    sq[:, go:go + gw], start=True, stop=True), inc=(go + gw == TPC))
            wait("vector", MM)
            v = vop(lambda eng: eng.tensor_scalar(
                sb["rst"][0:2, 0:TPC], psA[0:2, 0:TPC], 1.0 / 64, 1e-6, ALU.mult, ALU.add))
            wait("scalar", VD, v)
            s2 = sop(lambda eng: eng.activation(out=sb["rst"][0:2, 0:TPC],
                                                in_=sb["rst"][0:2, 0:TPC], func=ACT.Sqrt))
            wait("vector", SD, s2)
            sig = sb["stage"][0:2, 2048:2048 + TPC]
            v = vop(lambda eng, sig=sig: eng.reciprocal(sig, sb["rst"][0:2, 0:TPC]))
            wait("tensor", VD, v)
            for go, gw in groups(TPC):
                top(lambda eng, go=go, gw=gw, sig=sig: eng.matmul(
                    psB[:, go:go + gw], sm[0:2, SM_REP64:SM_REP64 + 128], sig[:, go:go + gw],
                    start=True, stop=True), inc=(go + gw == TPC))
            wait("vector", MM)
            vop(lambda eng, j=j: eng.tensor_tensor(
                atT[:, j * TPC:(j + 1) * TPC], atT[:, j * TPC:(j + 1) * TPC],
                psB[:, 0:TPC], ALU.mult))
            wait("tensor", VD)
        So = chans("p2so", 2)
        for s in range(NT):
            wait("tensor", VD)
            for oh in range(D // 512):
                for c in range(NDC):
                    top(lambda eng, c=c, s=s, oh=oh: eng.matmul(
                        psA[:, oh * 512:(oh + 1) * 512],
                        atT[:, c * TPC + s * P: c * TPC + (s + 1) * P],
                        sb["kvc"][:, c * D + oh * 512: c * D + (oh + 1) * 512],
                        start=(c == 0), stop=(c == NDC - 1)),
                        inc=(c == NDC - 1 and oh == D // 512 - 1))
            wait("vector", MM)
            sl = s % 2
            out = sb["stage"][:, (2 + sl) * D:(3 + sl) * D]
            if s >= 2:
                wait("vector", So[sl])
            v = vop(lambda eng, out=out: eng.tensor_copy(out, psA[:, 0:D]))
            wait("sync", VD, v)
            dma_c("sync", A["att0_sp"][s * P:(s + 1) * P, :], out, So[sl])

    phase2_setup()
    phase2_proj()
    ctx.barrier("p2a")
    phase2_bounds()
    ctx.barrier("p2b")
    phase2_attn()
    ctx.barrier("p2c")
    phase2_rms_wout()
    ctx.barrier("p2")

    SC3 = float(64.0 ** -0.5)
    EPSR = 1.1920929e-07

    def phase3():
        sm = sb["small"]
        st = sb["stat"]
        EW = sb["xnT"]
        XL = psem("p3xl")
        wait("sync", SD)
        dma_c("sync", EW[:, 0:NDC * T], A["xnT_sp"][:], XL)
        wait("tensor", XL)
        C3 = psem("p3c")
        n1c = st[:, 1280:1288]
        for c in range(NDC):
            dma_c("sync", n1c[:, c:c + 1], A["f0_n1"][0:1, c * P:(c + 1) * P], C3)
        dma_c("sync", sb["rst"][0:1, 0:D], A["f0_kn_t"], C3)
        dma_c("sync", sb["rst"][32:33, 0:D], A["f0_qn_t"], C3)
        wait("tensor", C3)
        knrep = sb["rst"][:, 0:D]
        qnrep = sb["rst"][:, D:2 * D]
        for go, gw in groups(D):
            top(lambda eng, go=go, gw=gw: eng.matmul(
                psA[:, go:go + gw], sm[0:1, SM_ONESROW:SM_ONESROW + 128],
                sb["rst"][0:1, go:go + gw], start=True, stop=True), inc=(go + gw == D))
            top(lambda eng, go=go, gw=gw: eng.matmul(
                psB[:, go:go + gw], sm[32:33, SM_ONESROW:SM_ONESROW + 128],
                sb["rst"][32:33, go:go + gw], start=True, stop=True), inc=(go + gw == D))
        wait("vector", MM)
        vop(lambda eng: eng.tensor_copy(knrep, psA[:, 0:D]))
        vop(lambda eng: eng.tensor_copy(qnrep, psB[:, 0:D]))
        v_ = vop(lambda eng: eng.memset(st[:, 1288:1304], -1e30))
        wait("tensor", VD, v_)

        Wch = chans("p3w", 2)
        def proj_tiles(wbase, n_tiles, kind):
            KT = chans("p3kt" + kind, 2)
            SPL = chans("p3sp" + kind, 2)
            for chp in list(ctx.dma_seen):
                wait("vector", chp)
                wait("scalar", chp)
            nrep = knrep if kind == "k" else qnrep
            for tt in range(n_tiles):
                rcol = st[:, 448 + tt:448 + tt + 1]
                wait("tensor", VD)
                for oh in range(2):
                    for c in range(NDC):
                        top(lambda eng, c=c, tt=tt, oh=oh, wbase=wbase: eng.matmul(
                            psA[:, oh * 512:(oh + 1) * 512],
                            EW[:, c * T + tt * P:c * T + (tt + 1) * P],
                            sb["wbf"][:, wbase + c * D + oh * 512: wbase + c * D + (oh + 1) * 512],
                            start=(c == 0), stop=(c == NDC - 1)),
                            inc=(c == NDC - 1 and oh == 1))
                wait("vector", MM)
                sl = tt % 2
                if kind == "v":
                    vb = sb["bf"][:, (1 + sl) * D:(2 + sl) * D]
                    if tt >= 2:
                        wait("vector", SPL[sl])
                    v = vop(lambda eng, vb=vb, rcol=rcol: eng.tensor_scalar(
                        vb, psA[:, 0:D], rcol, None, ALU.mult))
                    wait("sync", VD, v)
                    dma_c("sync", A["v0f_sp"][:, tt * D:(tt + 1) * D], vb, SPL[sl])
                    continue
                kt = sb["stage"][:, 2 * D:3 * D]
                v = vop(lambda eng, kt=kt, rcol=rcol: eng.tensor_scalar(
                    kt, psA[:, 0:D], rcol, None, ALU.mult))
                wait("scalar", VD, v)
                sq = sb["stage"][:, 3 * D:4 * D]
                s_ = sop(lambda eng, sq=sq, kt=kt: eng.activation(out=sq, in_=kt, func=ACT.Square))
                wait("vector", SD, s_)
                msq = st[:, 0:16]
                vop(lambda eng, sq=sq, msq=msq: eng.reduce_sum(
                    msq, sq[:].rearrange("p (h d) -> p h d", d=64), AX))
                v2 = vop(lambda eng, msq=msq: eng.tensor_scalar(
                    msq, msq, 1.0 / 64, EPSR, ALU.mult, ALU.add))
                wait("scalar", VD, v2)
                s2 = sop(lambda eng, msq=msq: eng.activation(out=st[:, 16:32], in_=msq, func=ACT.Sqrt))
                wait("vector", SD, s2)
                vop(lambda eng: eng.reciprocal(st[:, 32:48], st[:, 16:32]))
                kb = sb["bf"][:, (1 + sl) * D:(2 + sl) * D]
                if tt >= 2:
                    wait("vector", KT[sl])
                vop(lambda eng, kt=kt: eng.tensor_tensor(
                    kt[:].rearrange("p (h d) -> p h d", d=64), kt[:].rearrange("p (h d) -> p h d", d=64),
                    bcast64(st[:, 32:48]), ALU.mult))
                v3 = vop(lambda eng, kb=kb, kt=kt, nrep=nrep: eng.tensor_tensor(kb, kt, nrep, ALU.mult))
                wait("scalar", VD, v3)
                s3 = sop(lambda eng, sq=sq, kb=kb: eng.activation(out=sq, in_=kb, func=ACT.Square))
                wait("vector", SD, s3)
                vop(lambda eng, sq=sq: eng.reduce_sum(
                    st[:, 48:64], sq[:].rearrange("p (h d) -> p h d", d=64), AX))
                if kind == "k":
                    vop(lambda eng: eng.tensor_tensor(
                        st[:, 1288:1304], st[:, 1288:1304], st[:, 48:64], ALU.max))
                else:
                    vop(lambda eng, tt=tt: eng.tensor_copy(st[:, 768 + 16 * tt:768 + 16 * (tt + 1)],
                                                           st[:, 48:64]))
                if kind == "k":
                    wait("sync", VD)
                    dmat_c(sb["bf"][:, 3 * D:4 * D].rearrange("p (c t) -> p c t", t=P), kb, KT[sl])
                    wait("sync", KT[sl])
                    dma_c("sync", A["k0f_sp"][:].rearrange("p (c t) -> p c t", t=T)[:, :, tt * P:(tt + 1) * P],
                          sb["bf"][:, 3 * D:4 * D].rearrange("p (c t) -> p c t", t=P), SPL[sl])
                    wait("sync", SPL[sl])
                else:
                    wait("sync", VD)
                    dmat_c(t_dst(sb["qt"], tt, tot=TPC), kb, KT[sl])

        stream_weight(A["f0_wk"], NDC, D, 0, Wch, fold=lambda c: n1c[:, c:c + 1])
        stream_weight(A["f0_wv"], NDC, D, 8192, Wch, fold=lambda c: n1c[:, c:c + 1])
        proj_tiles(0, NKC, "k")
        proj_tiles(8192, NKC, "v")
        wait("sync", MM)
        wait("sync", VD)
        stream_weight(A["f0_wq"], NDC, D, 0, Wch, fold=lambda c: n1c[:, c:c + 1])
        proj_tiles(0, NT, "q")
        ctx.barrier("p3proj")

    def phase3_bounds():
        sm = sb["small"]
        st = sb["stat"]
        pad = sb["stage"][:, 0:P]
        v = vop(lambda eng: eng.memset(pad, 0.0))
        v = vop(lambda eng, pad=pad: eng.tensor_copy(pad[:, 0:16], st[:, 1288:1304]))
        ID = psem("p3id")
        dma_c("sync", sb["rst"][:, 0:P], A["ident"], ID)
        wait("tensor", ID)
        wait("tensor", VD, v)
        m = top(lambda eng, pad=pad: eng.transpose(psA[:, 0:P], pad, sb["rst"][:, 0:P]))
        wait("vector", MM, m)
        v = vop(lambda eng: eng.reduce_max(st[0:16, 0:1], psA[0:16, 0:P], AX))
        v = vop(lambda eng, pad=pad: eng.memset(pad, 0.0))
        v = vop(lambda eng, pad=pad: eng.tensor_copy(pad[0:16, 0:1], st[0:16, 0:1]))
        wait("tensor", VD, v)
        m = top(lambda eng, pad=pad: eng.transpose(psB[:, 0:P], pad, sb["rst"][:, 0:P]))
        wait("vector", MM, m)
        kmrow = sb["rst"][0:1, 512:528]
        v = vop(lambda eng, kmrow=kmrow: eng.tensor_copy(kmrow, psB[0:1, 0:16]))
        wait("tensor", VD, v)
        m = top(lambda eng, kmrow=kmrow: eng.matmul(
            psA[:, 512:512 + 16], sm[0:1, SM_ONESROW:SM_ONESROW + 128],
            kmrow, start=True, stop=True))
        wait("vector", MM, m)
        kmrep = st[:, 0:16]
        v = vop(lambda eng, kmrep=kmrep: eng.tensor_copy(kmrep, psA[:, 512:512 + 16]))
        for tt in range(NT):
            v = vop(lambda eng, tt=tt, kmrep=kmrep: eng.tensor_tensor(
                st[:, 1024 + 16 * tt:1024 + 16 * (tt + 1)],
                st[:, 768 + 16 * tt:768 + 16 * (tt + 1)], kmrep, ALU.mult))
        wait("scalar", VD, v)
        s_ = sop(lambda eng: eng.activation(out=st[:, 1024:1024 + 16 * NT],
                                            in_=st[:, 1024:1024 + 16 * NT],
                                            func=ACT.Sqrt, scale=SC3 * SC3))
        wait("vector", SD, s_)
        vop(lambda eng: eng.tensor_scalar(st[:, 1024:1024 + 16 * NT],
                                          st[:, 1024:1024 + 16 * NT], -1.0, None, ALU.mult))

    def phase3_attn():
        st = sb["stat"]
        sm = sb["small"]
        EW = sb["xnT"]
        KSb, Eb, PTB, Vb = 0, T, 3 * T, 7 * T
        KL = chans("p3kl", 2)
        VL = chans("p3vl", 2)
        TpP = chans("p3pt", 2)
        NT2 = NT // 2
        wait("scalar", LD)
        wait("vector", LD)
        pend_pv = [None]
        bank_sd = {0: 0, 1: 0}
        pv_evac = [0]
        cmb_mark = [0]

        def pt_plane(nbm, hh, x):
            return PTB + (hh * 2 * nbm + x) * 256

        def flush_pv():
            if pend_pv[0] is None:
                return
            j, g, nbm = pend_pv[0]
            pend_pv[0] = None
            zm = 0
            for hh in range(2):
                base = pt_plane(nbm, hh, 2 * nbm - 2)
                zm = vop(lambda eng, base=base: eng.memset(
                    EW[:, base:base + 512].rearrange("p (x s q) -> p x s q", s=2, q=P)
                    [:, :, 0, :], 0.0))
            wait("tensor", VD, zm)
            wait("tensor", VD, pv_evac[0])
            wait("tensor", SD, bank_sd[1])
            wait("tensor", TpP[0])
            wait("tensor", TpP[1])
            wait("tensor", VL[0])
            for x in range(2 * nbm):
                bb, r = x >> 1, x & 1
                tile = bb if r == 0 else NT + bb
                for hh in range(2):
                    ps = psA if hh == 0 else psB
                    top(lambda eng, x=x, tile=tile, hh=hh, nbm=nbm, ps=ps:
                        eng.matmul(ps[64 * hh:64 * hh + 64, 1536:1536 + 256],
                                   EW[:, Vb + tile * P + 64 * hh: Vb + tile * P + 64 * hh + 64],
                                   EW[:, pt_plane(nbm, hh, x):pt_plane(nbm, hh, x) + 256],
                                   start=(x == 0), stop=(x == 2 * nbm - 1),
                                   tile_position=(0, 64 * hh), skip_group_check=True),
                        inc=(x == 2 * nbm - 1 and hh == 1))
            wait("vector", MM)
            vop(lambda eng, j=j, g=g: eng.tensor_copy(
                sb["wbf"][0:64, j * TPC + 2 * g * P: j * TPC + (2 * g + 2) * P],
                psA[0:64, 1536:1536 + 256]))
            pv_evac[0] = vop(lambda eng, j=j, g=g: eng.tensor_copy(
                sb["wbf"][64:128, j * TPC + 2 * g * P: j * TPC + (2 * g + 2) * P],
                psB[64:128, 1536:1536 + 256]))

        def do_slot(j, s, jq):
            nb = s + 1
            w = nb * P
            grps = groups(w, 1024)
            ngr = len(grps)
            segs = [(run, go, gw) for run in range(2) for (go, gw) in grps]
            first_exp = True
            for i, (run, go, gw) in enumerate(segs):
                bcol = (i % 2) * 1024
                tof = 0 if run == 0 else NT
                gi = grps.index((go, gw))
                if bank_sd[i % 2]:
                    wait("tensor", SD, bank_sd[i % 2])
                m = None
                for qo, qw in groups(gw):
                    top(lambda eng, go=go, qo=qo, qw=qw, s=s, tof=tof, jq=jq, bcol=bcol:
                        eng.matmul(psA[:, bcol + qo:bcol + qo + qw],
                                   sb["qt"][0:64, jq + s * P: jq + (s + 1) * P],
                                   EW[0:64, KSb + tof * P + go + qo:
                                      KSb + tof * P + go + qo + qw],
                                   start=True, stop=True, tile_position=(0, 0)),
                        inc=False)
                    m = top(lambda eng, go=go, qo=qo, qw=qw, s=s, tof=tof, jq=jq, bcol=bcol:
                            eng.matmul(psB[:, bcol + qo:bcol + qo + qw],
                                       sb["qt"][64:128, jq + s * P: jq + (s + 1) * P],
                                       EW[64:128, KSb + tof * P + go + qo:
                                          KSb + tof * P + go + qo + qw],
                                       start=True, stop=True, tile_position=(64, 0)),
                            inc=True)
                has_diag = go <= s * P < go + gw
                if has_diag:
                    dcol = bcol + s * P - go
                    dm = sm[:, SM_DM0:SM_DM0 + 128] if run == 0 else sm[:, SM_DM1:SM_DM1 + 128]
                    wait("vector", MM, m)
                    va = vop(lambda eng, dcol=dcol, dm=dm: eng.tensor_tensor(
                        psA[:, dcol:dcol + P], psA[:, dcol:dcol + P], dm, ALU.add))
                    vb_ = vop(lambda eng, dcol=dcol, dm=dm: eng.tensor_tensor(
                        psB[:, dcol:dcol + P], psB[:, dcol:dcol + P], dm, ALU.add))
                if first_exp:
                    wait("scalar", VD, cmb_mark[0])
                    wait("scalar", TpP[0])
                    wait("scalar", TpP[1])
                    first_exp = False
                if has_diag:
                    wait("scalar", VD, vb_)
                else:
                    wait("scalar", MM, m)
                for hh, ps in ((0, psA), (1, psB)):
                    bias = st[:, 1024 + s * 16 + 2 * j + hh: 1024 + s * 16 + 2 * j + hh + 1]
                    zcol = st[:, 48 + 4 * hh + run * ngr + gi:49 + 4 * hh + run * ngr + gi]
                    e_ = sop(lambda eng, go=go, gw=gw, w=w, run=run, hh=hh,
                             bias=bias, zcol=zcol, ps=ps, bcol=bcol:
                             eng.activation(
                                 out=EW[:, Eb + hh * T:Eb + hh * T + 2 * w]
                                 .rearrange("p (b r q) -> p b r q", r=2, q=P)
                                 [:, go // P:(go + gw) // P, run, :],
                                 in_=ps[:, bcol:bcol + gw], func=ACT.Exp,
                                 bias=bias, scale=SC3, accum_out=zcol))
                bank_sd[i % 2] = SD.n
            wait("vector", SD, bank_sd[(len(segs) - 1) % 2])
            for hh in range(2):
                vop(lambda eng, hh=hh, ngr=ngr: eng.reduce_sum(
                    st[:, 56 + hh:57 + hh], st[:, 48 + 4 * hh:48 + 4 * hh + 2 * ngr], AX))
            vop(lambda eng: eng.reciprocal(st[:, 58:60], st[:, 56:58]))
            for hh in range(2):
                ehh = EW[:, Eb + hh * T:Eb + hh * T + 2 * w]
                vf = vop(lambda eng, ehh=ehh, hh=hh: eng.tensor_scalar(
                    ehh, ehh, st[:, 58 + hh:59 + hh], None, ALU.mult))
                cmb_mark[0] = vf

        def do_dt(s, nbm, si):
            w2 = 2 * (s + 1) * P
            for hh in range(2):
                ehh = EW[:, Eb + hh * T:Eb + hh * T + w2]
                wait("sync", VD, cmb_mark[0])
                wait("sync", MM)
                dmat_c(EW[:, pt_plane(nbm, hh, 0):pt_plane(nbm, hh, 0) + 2 * nbm * 256]
                       .rearrange("p (x s q) -> p x s q", s=2, q=P)
                       [:, 0:2 * (s + 1), si, :], ehh, TpP[hh])

        for j in range(NDC):
            jq = j * TPC
            wait("sync", VD)
            wait("sync", SD)
            wait("sync", MM)
            wait("sync", KL[0])
            wait("sync", VL[0])
            dma_c("sync", EW[:, KSb:KSb + T],
                  A["k0f_sp"][:].rearrange("p (c t) -> p c t", t=T)[:, j, :], KL[0])
            dma_c("sync", EW[:, Vb:Vb + NKC * P],
                  A["v0f_sp"][:].rearrange("p (t d) -> p t d", d=D)[:, :, j * P:(j + 1) * P], VL[0])
            wait("tensor", KL[0])
            for g in range(NT2):
                s0, s1 = 2 * g, 2 * g + 1
                nbm = 2 * g + 2
                do_slot(j, s0, jq)
                flush_pv()
                do_dt(s0, nbm, 0)
                wait("tensor", VD, pv_evac[0])
                do_slot(j, s1, jq)
                do_dt(s1, nbm, 1)
                pend_pv[0] = (j, g, nbm)
            flush_pv()

    def phase3_wo_xmid():
        st = sb["stat"]
        atT = sb["wbf"]
        Wch = chans("p3wo", 2)
        XL2 = chans("p3xn", 2)
        SPm = chans("p3xm", 2)
        marks = []
        wait("sync", VD)
        for c in range(NDC):
            sl = c % 2
            stg = sb["stage"][:, sl * D:sl * D + 768]
            if c >= 2:
                wait("sync", VD, marks[c - 2])
            dma_c("sync", stg, A["f0_wo"][c * P:(c + 1) * P, 256:1024], Wch[sl])
            wait("vector", Wch[sl])
            v = vop(lambda eng, c=c, stg=stg: eng.tensor_copy(
                sb["kvc"][:, c * 768:(c + 1) * 768], stg))
            marks.append(v)
        wait("tensor", VD, marks[-1])
        wait("sync", VD, marks[-1])
        XL4 = chans("p3x4", 4)
        SP4 = chans("p3m4", 4)
        add_marks = []
        for s in range(NT):
            sl = s % 4
            ps = psA if s % 2 == 0 else psB
            xnl = sb["stage"][:, sl * D:(sl + 1) * D]
            if s >= 4:
                wait("sync", VD, add_marks[s - 4])
                wait("sync", SP4[sl])
            dma_c("sync", xnl, A["xn_sp"][s * P:(s + 1) * P, :], XL4[sl])
            if s >= 2:
                wait("tensor", VD, add_marks[s - 2])
            for oh, (oo, ow) in enumerate(groups(768)):
                for c in range(NDC):
                    top(lambda eng, c=c, s=s, oo=oo, ow=ow, ps=ps: eng.matmul(
                        ps[:, oo:oo + ow],
                        atT[:, c * TPC + s * P: c * TPC + (s + 1) * P],
                        sb["kvc"][:, c * 768 + oo: c * 768 + oo + ow],
                        start=(c == 0), stop=(c == NDC - 1)),
                        inc=(c == NDC - 1 and oo + ow == 768))
            wait("vector", XL4[sl])
            wait("vector", MM)
            v = vop(lambda eng, xnl=xnl, ps=ps: eng.tensor_tensor(
                xnl[:, 256:1024], xnl[:, 256:1024], ps[:, 0:768], ALU.add))
            add_marks.append(v)
            wait("sync", VD, v)
            dma_c("sync", A["xmid_sp"][s * P:(s + 1) * P, :], xnl, SP4[sl])

    xm_marks = []
    phase3()
    ctx.barrier("p3a")
    phase3_bounds()
    ctx.barrier("p3b")
    phase3_attn()
    ctx.barrier("p3c")
    phase3_wo_xmid()
    ctx.barrier("p3")

    EXPERT_W = [
        ("f0_e0_up", D, 2048, False), ("f0_e0_down", 2048, 768, True),
        ("f0_e1_w1", D, 2048, False), ("f0_e1_w2", D, 2048, False),
        ("f0_e1_down", 2048, 768, True),
        ("f0_e2_l1", D, 2048, False), ("f0_e2_l2", 2048, 2048, False),
        ("f0_e2_l3", 2048, 2048, False), ("f0_e2_l4", 2048, 768, True),
        ("f0_e3_down", D, D, False), ("f0_e3_up", D, 2048, False),
        ("f0_e3_out", 2048, 768, True),
    ]

    def phase4():
        st = sb["stat"]
        sm = sb["small"]
        C4 = psem("p4c")
        dma_c("sync", sb["rst"][0:1, 0:D], A["f0_n2"], C4)
        gstage = sb["stage"][:, 0:32]
        for c in range(NDC):
            dma_c("sync", gstage[:, 4 * c:4 * (c + 1)], A["f0_gate"][c * P:(c + 1) * P, :], C4)
        wait("vector", C4)
        gbf = sb["bf"][:, 3 * D:3 * D + 32]
        vop(lambda eng, gbf=gbf, gstage=gstage: eng.tensor_copy(gbf, gstage))
        wait("tensor", C4)
        n2rep = sb["rst"][:, D:2 * D]
        for go, gw in groups(D):
            top(lambda eng, go=go, gw=gw: eng.matmul(
                psB[:, go:go + gw], sm[0:1, SM_ONESROW:SM_ONESROW + 128],
                sb["rst"][0:1, go:go + gw], start=True, stop=True), inc=(go + gw == D))
        wait("vector", MM)
        vn2 = vop(lambda eng, n2rep=n2rep: eng.tensor_copy(n2rep, psB[:, 0:D]))
        wait("sync", VD)
        XM = chans("p4xm", 2)
        XT4 = chans("p4xt", 2)
        XS = chans("p4a", 2)
        XS2 = chans("p4b", 2)
        SPX = chans("p4sp", 2)
        for s_ in range(NT):
            sl = s_ % 2
            xm = sb["stage"][:, sl * D:(sl + 1) * D]
            if s_ >= 2:
                wait("sync", VD, p4x_marks[s_ - 2])
            dma_c("sync", xm, A["xmid_sp"][s_ * P:(s_ + 1) * P, :], XM[sl])
            wait("scalar", XM[sl])
            wait("scalar", VD)
            sq = sop(lambda eng, xm=xm, s_=s_, d=sb["bf"][:, (1 + sl) * D:(2 + sl) * D]:
                     eng.activation(out=d, in_=xm, func=ACT.Square,
                                    accum_out=st[:, 576 + s_:577 + s_]))
            wait("vector", SD, sq)
            rc = st[:, 592 + s_:593 + s_]
            vop(lambda eng, s_=s_, rc=rc: eng.tensor_scalar(
                rc, st[:, 576 + s_:577 + s_], 1.0 / D, EPSR, ALU.mult, ALU.add))
            v2 = vop(lambda eng, rc=rc: eng.tensor_scalar(rc, rc, 1.0, None, ALU.mult))
            wait("scalar", VD, v2)
            s2 = sop(lambda eng, rc=rc, s_=s_: eng.activation(
                out=st[:, 608 + s_:609 + s_], in_=rc, func=ACT.Sqrt))
            wait("vector", SD, s2)
            vop(lambda eng, rc=rc, s_=s_: eng.reciprocal(rc, st[:, 608 + s_:609 + s_]))
            xfb = sb["bf"][:, (1 + sl) * D:(2 + sl) * D]
            if s_ >= 2:
                wait("vector", XT4[sl])
            vop(lambda eng, xm=xm, rc=rc: eng.tensor_scalar(xm, xm, rc, None, ALU.mult))
            vf = vop(lambda eng, xfb=xfb, xm=xm, n2rep=n2rep: eng.tensor_tensor(xfb, xm, n2rep, ALU.mult))
            p4_marks.append(vf)
            wait("sync", VD, vf)
            dmat_c(t_dst(sb["qt"], s_, tot=TPC), xfb, XT4[sl])
            wait("tensor", XT4[sl])
            wait("tensor", VD, vf)
            if lg_marks:
                wait("tensor", VD, lg_marks[-1])
            for c in range(NDC):
                top(lambda eng, c=c, s_=s_: eng.matmul(
                    psA[:, 1024:1024 + 4],
                    sb["qt"][:, c * TPC + s_ * P:c * TPC + (s_ + 1) * P],
                    gbf[:, 4 * c:4 * (c + 1)],
                    start=(c == 0), stop=(c == NDC - 1)), inc=(c == NDC - 1))
            wait("vector", MM)
            vlg = vop(lambda eng, s_=s_: eng.tensor_copy(
                st[:, 128 + 4 * s_:132 + 4 * s_], psA[:, 1024:1028]))
            lg_marks.append(vlg)
            xs = sb["stage"][:, (2 + sl) * D:(3 + sl) * D]
            if s_ >= 2:
                wait("sync", VD, p4x_marks[s_ - 2])
                wait("sync", SPX[sl])
            dma_c("sync", xs, A["x_own"][s_ * P:(s_ + 1) * P, :], XS[sl])
            wait("sync", VD, vf)
            wait("sync", SD)
            dma_c("sync", xm, A["att0_sp"][s_ * P:(s_ + 1) * P, :], XS2[sl])
            wait("vector", XS[sl])
            wait("vector", XS2[sl])
            v9 = vop(lambda eng, xs=xs, xm=xm: eng.tensor_tensor(xs, xs, xm, ALU.add))
            wait("sync", VD, v9)
            dma_c("sync", xm, A["xmid_sp"][s_ * P:(s_ + 1) * P, :], XS2[sl])
            wait("vector", XS2[sl])
            if s_ >= 2:
                wait("vector", SPX[sl])
            va = vop(lambda eng, xs=xs, xm=xm: eng.tensor_tensor(xs, xs, xm, ALU.add))
            p4x_marks.append(va)
            wait("sync", VD, va)
            dma_c("sync", A["xsum_sp"][s_ * P:(s_ + 1) * P, :], xs, SPX[sl])
        lg = st[:, 128:128 + 4 * NT]
        l3 = lg.rearrange("p (s e) -> p s e", e=4)
        mx = st[:, 320:320 + NT]
        mxb = mx.unsqueeze(2).broadcast_to([P, NT, 4])
        vop(lambda eng: eng.reduce_max(mx, l3, AX))
        v = vop(lambda eng: eng.tensor_tensor(l3, l3, mxb, ALU.subtract))
        wait("scalar", VD, v)
        eb = st[:, 208:208 + 4 * NT]
        eb3 = eb.rearrange("p (s e) -> p s e", e=4)
        se = sop(lambda eng: eng.activation(out=eb, in_=lg, func=ACT.Exp))
        wait("vector", SD, se)
        zs = st[:, 336:336 + NT]
        rz = st[:, 352:352 + NT]
        rzb = rz.unsqueeze(2).broadcast_to([P, NT, 4])
        vop(lambda eng: eng.reduce_sum(zs, eb3, AX))
        vop(lambda eng: eng.reciprocal(rz, zs))
        vop(lambda eng: eng.tensor_tensor(eb3, eb3, rzb, ALU.mult))
        cnt = st[:, 368:368 + 4 * NT]
        cnt3 = cnt.rearrange("p (s e) -> p s e", e=4)
        tmp = st[:, 432:432 + NT]
        for e in range(4):
            js = [jj for jj in range(4) if jj != e]
            ce = cnt3[:, :, e:e + 1]
            pe_ = eb3[:, :, e:e + 1]
            vop(lambda eng, ce=ce, pe_=pe_, j0=js[0]: eng.tensor_tensor(
                ce, eb3[:, :, j0:j0 + 1], pe_, ALU.is_gt))
            for j2 in js[1:]:
                vop(lambda eng, pe_=pe_, j2=j2: eng.tensor_tensor(
                    tmp.unsqueeze(2), eb3[:, :, j2:j2 + 1], pe_, ALU.is_gt))
                vop(lambda eng, ce=ce: eng.tensor_tensor(
                    ce, ce, tmp.unsqueeze(2), ALU.add))
        vop(lambda eng: eng.tensor_scalar(cnt, cnt, 1.0, None, ALU.is_le))
        vop(lambda eng: eng.tensor_tensor(cnt3, cnt3, eb3, ALU.mult))
        sw = st[:, 336:336 + NT]
        rw = st[:, 352:352 + NT]
        rwb = rw.unsqueeze(2).broadcast_to([P, NT, 4])
        vop(lambda eng: eng.reduce_sum(sw, cnt3, AX))
        vop(lambda eng: eng.tensor_scalar(sw, sw, 1e-8, None, ALU.add))
        vop(lambda eng: eng.reciprocal(rw, sw))
        W3 = st[:, 512:512 + 4 * NT].rearrange("p (s e) -> p s e", e=4)
        vop(lambda eng: eng.tensor_tensor(W3, cnt3, rwb, ALU.mult))

    def phase5():
        st = sb["stat"]
        sm = sb["small"]
        EW = sb["xnT"]
        GW = min(512, TPC)
        ngrp = TPC // GW
        ntg = GW // P
        WL = chans("p5w", 4)
        AL = chans("p5a", 2)
        SP5 = chans("p5s", 2)
        PSP = [ctx.sem("p5psA"), ctx.sem("p5psB")]
        RLD = chans("p5rl", 2)
        W_INFO = {name: (din, dout, isdown) for name, din, dout, isdown in EXPERT_W}
        spill_rec = {}

        p5_cast_marks = []
        psp_hist = []

        def issue_load(name, co, cw, dst_base, grp):
            din, dout, isdown = W_INFO[name]
            nch = din // P
            wait("sync", MM)
            if grp == 0:
                wi = len(psp_hist)
                if wi >= 2:
                    wait("sync", PSP[0], psp_hist[wi - 2][0])
                    wait("sync", PSP[1], psp_hist[wi - 2][1])
            else:
                wait("sync", PSP[0])
                wait("sync", PSP[1])
            if grp == 0:
                pw = min(cw, 1024)
                npc = cw // pw
                for c in range(nch):
                    for pp in range(npc):
                        gi = len(p5_cast_marks)
                        sl = gi % 2
                        stg = sb["rst"][:, sl * 1024: sl * 1024 + pw]
                        if gi >= 2:
                            wait("sync", VD, p5_cast_marks[gi - 2])
                        src_co = (256 if isdown else 0) + co + pp * pw
                        dma_c("sync", stg, A[name][c * P:(c + 1) * P, src_co:src_co + pw],
                              RLD[sl])
                        wait("vector", RLD[sl])
                        v = vop(lambda eng, stg=stg, c=c, pp=pp, pw=pw,
                                dst_base=dst_base, cw=cw:
                                eng.tensor_copy(EW[:, dst_base + c * cw + pp * pw:
                                                   dst_base + c * cw + (pp + 1) * pw], stg))
                        p5_cast_marks.append(v)
                wait("gpsimd", VD, p5_cast_marks[-1])
                for c in range(nch):
                    wait("gpsimd", PSP[c % 2])
                    dma_c("gpsimd", A["ew_sp"][name][c * P:(c + 1) * P, co:co + cw],
                          EW[:, dst_base + c * cw: dst_base + (c + 1) * cw], PSP[c % 2])
                psp_hist.append((PSP[0].n, PSP[1].n))
                return ("vd", p5_cast_marks[-1])
            else:
                pr = 2 * ((dst_base != 0) + 0)
                for c in range(nch):
                    dma_c("sync", EW[:, dst_base + c * cw: dst_base + (c + 1) * cw],
                          A["ew_sp"][name][c * P:(c + 1) * P, co:co + cw], WL[pr + c % 2])
                return ("ch", pr, WL[pr].n, WL[pr + 1].n)

        def pe_wait(spec):
            if spec[0] == "vd":
                wait("tensor", VD, spec[1])
            else:
                pr = spec[1]
                wait("tensor", WL[pr], spec[2])
                wait("tensor", WL[pr + 1], spec[3])

        p5vd = [0]

        def link(src_buf, src_col, din, dout, wbase, act, dst_buf, dst_base):
            nch = din // P
            noc = dout // P
            wait("tensor", VD, p5vd[0])
            for oc in range(noc):
                wait("tensor", SD)
                ps = psA if (oc % 2 == 0) else psB
                for c in range(nch):
                    top(lambda eng, c=c, oc=oc, ps=ps, sc=src_col(c): eng.matmul(
                        ps[:, 0:GW],
                        EW[:, wbase + c * dout + oc * P: wbase + c * dout + (oc + 1) * P],
                        src_buf[:, sc:sc + GW],
                        start=(c == 0), stop=(c == nch - 1)), inc=(c == nch - 1))
                wait("scalar", MM)
                fn = {"gelu": ACT.Gelu, "silu": ACT.Silu, None: ACT.Copy}[act]
                sop(lambda eng, oc=oc, ps=ps, fn=fn: eng.activation(
                    out=dst_buf[:, dst_base + oc * GW: dst_base + (oc + 1) * GW],
                    in_=ps[:, 0:GW], func=fn))

        def down(src_buf, src_col, din, wbase, e_idx, grp, acc):
            nch = din // P
            wait("tensor", SD)
            for tg in range(ntg):
                wait("tensor", VD, p5vd[0])
                for oh, (oo, ow) in enumerate(groups(768)):
                    for c in range(nch):
                        top(lambda eng, c=c, tg=tg, oo=oo, ow=ow, sc=src_col(c): eng.matmul(
                            psA[:, oo:oo + ow],
                            src_buf[:, sc + tg * P: sc + (tg + 1) * P],
                            EW[:, wbase + c * 768 + oo: wbase + c * 768 + oo + ow],
                            start=(c == 0), stop=(c == nch - 1)),
                            inc=(c == nch - 1 and oo + ow == 768))
                wait("vector", MM)
                wcol = st[:, 512 + 4 * (grp * ntg + tg) + e_idx: 512 + 4 * (grp * ntg + tg) + e_idx + 1]
                p5vd[0] = vop(lambda eng, tg=tg, wcol=wcol, acc=acc: eng.scalar_tensor_tensor(
                    acc[:, tg * 768:(tg + 1) * 768], psA[:, 0:768], wcol,
                    acc[:, tg * 768:(tg + 1) * 768], ALU.mult, ALU.add))

        acc = sb["stage"][:, 0:ntg * 768]
        HB1, HB2 = 0, 8192
        BHALF = 16384
        wslot = [0]

        def mul_e1():
            wait("vector", SD)
            vm = 0
            for c in range(2048 // P):
                vm = vop(lambda eng, c=c: eng.tensor_tensor(
                    sb["wbf"][:, HB1 + c * GW:HB1 + (c + 1) * GW],
                    sb["wbf"][:, HB1 + c * GW:HB1 + (c + 1) * GW],
                    sb["wbf"][:, HB2 + c * GW:HB2 + (c + 1) * GW], ALU.mult))
            wait("tensor", VD, vm)

        def steps_for(grp, gq):
            xf_col = lambda c, gq=gq: c * TPC + gq
            hb1_col = lambda c: HB1 + c * GW
            hb2_col = lambda c: HB2 + c * GW
            kv_col = lambda c: c * GW
            return [
                ("f0_e0_up", 0, 2048, lambda b: link(
                    sb["qt"], xf_col, D, 2048, b, "gelu", sb["wbf"], HB1)),
                ("f0_e0_down", 0, 768, lambda b: down(
                    sb["wbf"], hb1_col, 2048, b, 0, grp, acc)),
                ("f0_e1_w1", 0, 2048, lambda b: link(
                    sb["qt"], xf_col, D, 2048, b, "silu", sb["wbf"], HB1)),
                ("f0_e1_w2", 0, 2048, lambda b: link(
                    sb["qt"], xf_col, D, 2048, b, None, sb["wbf"], HB2)),
                ("f0_e1_down", 0, 768, lambda b: (mul_e1(), down(
                    sb["wbf"], hb1_col, 2048, b, 1, grp, acc))),
                ("f0_e2_l1", 0, 2048, lambda b: link(
                    sb["qt"], xf_col, D, 2048, b, "gelu", sb["wbf"], HB1)),
                ("f0_e2_l2", 0, 1024, lambda b: link(
                    sb["wbf"], hb1_col, 2048, 1024, b, "gelu", sb["kvc"], 0)),
                ("f0_e2_l2", 1024, 1024, lambda b: link(
                    sb["wbf"], hb1_col, 2048, 1024, b, "gelu", sb["kvc"], 8 * GW)),
                ("f0_e2_l3", 0, 1024, lambda b: link(
                    sb["kvc"], kv_col, 2048, 1024, b, "gelu", sb["wbf"], HB1)),
                ("f0_e2_l3", 1024, 1024, lambda b: link(
                    sb["kvc"], kv_col, 2048, 1024, b, "gelu", sb["wbf"], HB1 + 8 * GW)),
                ("f0_e2_l4", 0, 768, lambda b: down(
                    sb["wbf"], hb1_col, 2048, b, 2, grp, acc)),
                ("f0_e3_down", 0, 1024, lambda b: link(
                    sb["qt"], xf_col, D, D, b, "gelu", sb["wbf"], HB1)),
                ("f0_e3_up", 0, 2048, lambda b: link(
                    sb["wbf"], hb1_col, D, 2048, b, "gelu", sb["kvc"], 0)),
                ("f0_e3_out", 0, 768, lambda b: down(
                    sb["kvc"], kv_col, 2048, b, 3, grp, acc)),
            ]

        pend = [None]

        def flush_pend():
            if pend[0] is not None:
                spec, comp, base = pend[0]
                pe_wait(spec)
                comp(base)
                pend[0] = None

        WQL = chans("p6wl", 2)
        WQS = chans("p6ws", 2)
        wq_state = {"gi": 0, "marks": []}
        wq_tasks = []

        def wq_guard_load(c, h, name, sl):
            gi = wq_state["gi"]
            if gi >= 2:
                wait("scalar", VD, wq_state["marks"][gi - 2])
            else:
                wait("scalar", VD, p5_cast_marks[-1])
            dma_c("scalar", sb["rst"][:, sl * 1024:(sl + 1) * 1024],
                  A[name][c * P:(c + 1) * P, h * 1024:(h + 1) * 1024], WQL[sl])

        def build_wq_tasks():
            for wi, (name, din, dout) in enumerate([("f1_wg", D, 4096),
                                                    ("f1_wu", D, 4096),
                                                    ("f1_wd", 4096, D)]):
                nch = din // P
                nh = dout // 1024
                pieces = [(c, h) for c in range(nch) for h in range(nh)]
                for pi, (c, h) in enumerate(pieces):
                    def t1(c=c, h=h, name=name, first=(pi == 0)):
                        sl = wq_state["gi"] % 2
                        wq_guard_load(c, h, name, sl)
                        stg = sb["rst"][:, sl * 1024:(sl + 1) * 1024]
                        wait("vector", WQL[sl])
                        vop(lambda eng, stg=stg: eng.reduce_sum(
                            st[:, 1:2], stg, AX, apply_absolute_value=True))
                        if first:
                            v = vop(lambda eng: eng.tensor_copy(st[:, 0:1], st[:, 1:2]))
                        else:
                            v = vop(lambda eng: eng.tensor_tensor(
                                st[:, 0:1], st[:, 0:1], st[:, 1:2], ALU.add))
                        wq_state["marks"].append(v)
                        wq_state["gi"] += 1
                    wq_tasks.append(t1)

                def tfin(wi=wi, name=name, n=float(din * dout)):
                    wait("tensor", VD, wq_state["marks"][-1])
                    m1 = top(lambda eng: eng.matmul(
                        psB[0:1, 1536:1537], st[:, 0:1], sm[:, SM_ONES:SM_ONES + 1],
                        start=True, stop=True), inc=True)
                    wait("vector", MM, m1)
                    vop(lambda eng: eng.tensor_copy(st[0:1, 2:3], psB[0:1, 1536:1537]))
                    vop(lambda eng, n=n: eng.tensor_scalar(
                        st[0:1, 2:3], st[0:1, 2:3], 1.0 / n, 1e-5, ALU.mult, ALU.max))
                    v = vop(lambda eng: eng.tensor_scalar(
                        st[0:1, 3:4], st[0:1, 2:3], 0.5, None, ALU.mult))
                    wait("tensor", VD, v)
                    m2 = top(lambda eng: eng.matmul(
                        psB[:, 1537:1539], sm[0:1, SM_ONESROW:SM_ONESROW + 128],
                        st[0:1, 2:4], start=True, stop=True), inc=True)
                    wait("vector", MM, m2)
                    beta = st[:, 1792 + 2 * wi:1793 + 2 * wi]
                    thr = st[:, 1793 + 2 * wi:1794 + 2 * wi]
                    vop(lambda eng, beta=beta: eng.tensor_copy(beta, psB[:, 1537:1538]))
                    vop(lambda eng, thr=thr: eng.tensor_copy(thr, psB[:, 1538:1539]))
                    vop(lambda eng, thr=thr: eng.tensor_scalar(
                        st[:, 4:5], thr, -1.0, None, ALU.mult))
                wq_tasks.append(tfin)

                for c, h in pieces:
                    def t2(c=c, h=h, name=name, wi=wi):
                        sl = wq_state["gi"] % 2
                        wq_guard_load(c, h, name, sl)
                        stg = sb["rst"][:, sl * 1024:(sl + 1) * 1024]
                        thr = st[:, 1793 + 2 * wi:1794 + 2 * wi]
                        wait("vector", WQL[sl])
                        wait("vector", WQS[sl])
                        msk = sb["bf"][:, 2048 + sl * 1024:2048 + (sl + 1) * 1024]
                        out = sb["bf"][:, sl * 1024:(sl + 1) * 1024]
                        vop(lambda eng, msk=msk, stg=stg: eng.tensor_scalar(
                            msk, stg, st[:, 4:5], None, ALU.is_le))
                        v = vop(lambda eng, out=out, stg=stg, thr=thr, msk=msk:
                                eng.scalar_tensor_tensor(
                                    out, stg, thr, msk, ALU.is_ge, ALU.subtract))
                        wq_state["marks"].append(v)
                        wq_state["gi"] += 1
                        wait("scalar", VD, v)
                        dma_c("scalar", A["wq_sp"][name][c * P:(c + 1) * P,
                                                         h * 1024:(h + 1) * 1024],
                              out, WQS[sl])
                    wq_tasks.append(t2)

        build_wq_tasks()

        for grp in range(ngrp):
            gq = grp * GW
            wait("sync", SP5[0])
            wait("sync", SP5[1])
            for tg in range(ntg):
                dma_c("sync", acc[:, tg * 768:(tg + 1) * 768],
                      A["xsum_sp"][gq + tg * P: gq + (tg + 1) * P, 256:1024], AL[tg % 2])
            wait("vector", AL[0])
            wait("vector", AL[1])
            for name, co, cw, comp in steps_for(grp, gq):
                base = (wslot[0] % 2) * BHALF
                wslot[0] += 1
                spec = issue_load(name, co, cw, base, grp)
                flush_pend()
                pend[0] = (spec, comp, base)
                if grp >= 1:
                    for _ in range(6):
                        if wq_tasks:
                            wq_tasks.pop(0)()
            flush_pend()
            wait("sync", VD)
            for tg in range(ntg):
                dma_c("sync", A["xsum_sp"][gq + tg * P: gq + (tg + 1) * P, 256:1024],
                      acc[:, tg * 768:(tg + 1) * 768], SP5[tg % 2])
            for c in (SP5[0], SP5[1]):
                wait("vector", c)
        while wq_tasks:
            wq_tasks.pop(0)()

    pw_marks, p4_marks, p4x_marks, lg_marks = [], [], [], []
    phase4()
    ctx.barrier("p4")
    phase5()
    ctx.barrier("p5")

    def phase6_ln_quant():
        st = sb["stat"]
        Lx = chans("p6lx", 2)
        Sp1 = chans("p6s1", 2)
        Sp2 = chans("p6s2", 2)
        Tq = chans("p6tq", 2)
        Tn = chans("p6tn", 2)
        vts = []
        for i in range(NT):
            sl = i % 2
            xa = sb["stage"][:, sl * D:(sl + 1) * D]
            xn = sb["stage"][:, (2 + sl) * D:(3 + sl) * D]
            if i >= 2:
                wait("sync", VD, vts[i - 2])
            dma_c("sync", xa, A["xsum_sp"][i * P:(i + 1) * P, :], Lx[sl])
            wait("vector", Lx[sl])
            wait("scalar", Lx[sl])
            if i >= 2:
                wait("vector", Sp1[sl])
            mu = st[:, 128 + i:128 + i + 1]
            r = st[:, 192 + i:192 + i + 1]
            vop(lambda eng, xa=xa, i=i: eng.reduce_sum(st[:, i:i + 1], xa, AX))
            if i >= 2:
                wait("scalar", VD, vts[i - 2])
                wait("scalar", Sp1[sl])
            sq_ = sop(lambda eng, xa=xa, i=i, xn=xn:
                      eng.activation(out=xn, in_=xa, func=ACT.Square,
                                     accum_out=st[:, 64 + i:65 + i]))
            vop(lambda eng, mu=mu, i=i: eng.tensor_scalar(
                mu, st[:, i:i + 1], 1.0 / D, None, ALU.mult))
            vop(lambda eng, mu=mu, i=i: eng.tensor_tensor(
                st[:, 256 + i:257 + i], mu, mu, ALU.mult))
            wait("vector", SD, sq_)
            vop(lambda eng, i=i: eng.tensor_scalar(
                st[:, 320 + i:321 + i], st[:, 64 + i:65 + i], 1.0 / D, None, ALU.mult))
            vop(lambda eng, i=i: eng.tensor_tensor(
                st[:, 320 + i:321 + i], st[:, 320 + i:321 + i],
                st[:, 256 + i:257 + i], ALU.subtract))
            v5 = vop(lambda eng, i=i: eng.tensor_scalar(
                st[:, 320 + i:321 + i], st[:, 320 + i:321 + i], 1e-5, None, ALU.add))
            wait("scalar", VD, v5)
            s6 = sop(lambda eng, i=i: eng.activation(
                out=st[:, 256 + i:257 + i], in_=st[:, 320 + i:321 + i],
                func=ACT.Sqrt))
            wait("vector", SD, s6)
            vop(lambda eng, r=r, i=i: eng.reciprocal(r, st[:, 256 + i:257 + i]))
            v1 = vop(lambda eng, xn=xn, xa=xa, mu=mu, r=r:
                     eng.tensor_scalar(xn, xa, mu, r, ALU.subtract, ALU.mult))
            wait("sync", VD, v1)
            dma_c("sync", A["xn1_sp"][i * P:(i + 1) * P, :], xn, Sp1[sl])
            am = st[:, 640 + i:641 + i]
            vop(lambda eng, xn=xn, am=am: eng.reduce_max(am, xn, AX, apply_absolute_value=True))
            vop(lambda eng, am=am: eng.tensor_scalar(am, am, 1e-5, None, ALU.max))
            vop(lambda eng, am=am, i=i: eng.tensor_scalar(
                st[:, 656 + i:657 + i], am, 1.0 / 127, None, ALU.mult))
            vop(lambda eng, am=am: eng.reciprocal(am, am))
            v2 = vop(lambda eng, am=am: eng.tensor_scalar(am, am, 127.0, None, ALU.mult))
            xb = sb["bf"][:, sl * D:(sl + 1) * D]
            if i >= 2:
                wait("vector", Tq[sl])
                wait("vector", Tn[sl])
            vop(lambda eng, xa=xa, xn=xn, am=am: eng.tensor_scalar(xa, xn, am, None, ALU.mult))
            vop(lambda eng, xa=xa: eng.tensor_scalar(xa, xa, 12582912.0, 12582912.0,
                                                     ALU.add, ALU.subtract))
            v3 = vop(lambda eng, xb=xb, xa=xa: eng.tensor_copy(xb, xa))
            wait("sync", VD, v3)
            dmat_c(t_dst(sb["qt"], i, tot=TPC), xb, Tq[sl])
            xb2 = sb["bf"][:, (2 + sl) * D:(3 + sl) * D]
            v4 = vop(lambda eng, xb2=xb2, xn=xn: eng.tensor_copy(xb2, xn))
            vts.append(v4)
            wait("sync", VD, v4)
            dmat_c(t_dst(sb["xnT"], i, tot=TPC), xb2, Tn[sl])
        wait("sync", Tn[0])
        wait("sync", Tn[1])
        dma_c("sync", A["xn1T_sp"][:], sb["xnT"][:, 0:NDC * TPC], LD)

    def phase6_kv_cc():
        st = sb["stat"]
        sm = sb["small"]
        C6 = psem("p6c")
        fk = st[:, 704:712]
        fv = st[:, 712:720]
        for c in range(NDC):
            dma_c("sync", fk[:, c:c + 1], A["a1_lnk_s"][0:1, c * P:(c + 1) * P], C6)
            dma_c("sync", fv[:, c:c + 1], A["a1_lnv_s"][0:1, c * P:(c + 1) * P], C6)
        bk_ = st[:, 720:728]
        bv_ = st[:, 728:736]
        for c in range(NDC):
            dma_c("sync", bk_[:, c:c + 1], A["a1_lnk_b"][0:1, c * P:(c + 1) * P], C6)
            dma_c("sync", bv_[:, c:c + 1], A["a1_lnv_b"][0:1, c * P:(c + 1) * P], C6)
        dma_c("sync", sb["rst"][0:1, 0:D], A["a1_bk"], C6)
        dma_c("sync", sb["rst"][32:33, 0:D], A["a1_bv"], C6)
        wait("vector", C6)
        wait("tensor", C6)
        Wch = chans("p6kv", 2)
        KM6 = st[:, 736:752]

        def kv_side(wname, fold, bcol, brow_part, out_region, do_norm):
            psem6_kt = psem("p6kt_" + wname)
            """stream W (fold lnX_s rows) -> wbf bf16; bvec = lnX_b @ W' + bX;
            produce token-major tiles + bias-rep + spill-free: write into
            cc staging then dma to cc_in; do_norm: collect kmax/qsq? (K only)"""
            marks = stream_weight(A[wname], NDC, D, 0, Wch)
            bb16 = sb["bf"][:, 0:8]
            vc_ = vop(lambda eng, bb16=bb16, bcol=bcol: eng.tensor_copy(bb16, bcol))
            wait("tensor", VD, marks[-1])
            wait("tensor", VD, vc_)
            for go, gw in groups(D):
                for c in range(NDC):
                    top(lambda eng, c=c, bb16=bb16, go=go, gw=gw: eng.matmul(
                        psB[0:1, go:go + gw], bb16[:, c:c + 1],
                        sb["wbf"][:, c * D + go:c * D + go + gw],
                        start=(c == 0), stop=(c == NDC - 1)),
                        inc=(c == NDC - 1 and go + gw == D))
            wait("vector", MM)
            for c in range(NDC):
                vop(lambda eng, c=c: eng.tensor_scalar(
                    sb["wbf"][:, c * D:(c + 1) * D], sb["wbf"][:, c * D:(c + 1) * D],
                    fold[:, c:c + 1], None, ALU.mult))
            wait("tensor", VD)
            wait("vector", MM)
            v = vop(lambda eng, brow_part=brow_part: eng.tensor_tensor(
                sb["rst"][brow_part:brow_part + 1, 0:D], sb["rst"][brow_part:brow_part + 1, 0:D],
                psB[0:1, 0:D], ALU.add))
            wait("tensor", VD, v)
            for go, gw in groups(D):
                top(lambda eng, go=go, gw=gw, brow_part=brow_part: eng.matmul(
                    psB[:, go:go + gw],
                    sm[brow_part:brow_part + 1, SM_ONESROW:SM_ONESROW + 128] if brow_part else sm[0:1, SM_ONESROW:SM_ONESROW + 128],
                    sb["rst"][brow_part:brow_part + 1, go:go + gw],
                    start=True, stop=True), inc=(go + gw == D))
            wait("vector", MM)
            brep = sb["rst"][:, D:2 * D]
            vb = vop(lambda eng, brep=brep: eng.tensor_copy(brep, psB[:, 0:D]))
            wait("tensor", VD, vb)
            CS = chans("p6cs", 2)
            for tt in range(NT):
                wait("tensor", VD)
                for oh in range(2):
                    for c in range(NDC):
                        top(lambda eng, c=c, tt=tt, oh=oh: eng.matmul(
                            psA[:, oh * 512:(oh + 1) * 512],
                            sb["xnT"][:, c * TPC + tt * P:c * TPC + (tt + 1) * P],
                            sb["wbf"][:, c * D + oh * 512:c * D + (oh + 1) * 512],
                            start=(c == 0), stop=(c == NDC - 1)),
                            inc=(c == NDC - 1 and oh == 1))
                wait("vector", MM)
                sl = tt % 2
                kb = sb["bf"][:, (1 + sl) * D:(2 + sl) * D]
                if tt >= 2:
                    wait("vector", CS[sl])
                v = vop(lambda eng, kb=kb, brep=brep: eng.tensor_tensor(
                    kb, psA[:, 0:D], brep, ALU.add))
                if do_norm:
                    wait("scalar", VD, v)
                    s_ = sop(lambda eng, kb=kb, sq=sb["stage"][:, 2 * D:3 * D]:
                             eng.activation(out=sq, in_=kb, func=ACT.Square))
                    wait("vector", SD, s_)
                    vop(lambda eng: eng.reduce_sum(
                        st[:, 48:64], sb["stage"][:, 2 * D:3 * D]
                        .rearrange("p (h d) -> p h d", d=64), AX))
                    v = vop(lambda eng: eng.tensor_tensor(KM6, KM6, st[:, 48:64], ALU.max))
                if not do_norm:
                    wait("sync", VD, v)
                    dma_c("sync", A["cc_in"][:, out_region + tt * D:out_region + (tt + 1) * D],
                          kb, CS[sl])
                else:
                    wait("sync", VD, v)
                    KT6 = psem("p6kt")
                    dmat_c(sb["bf"][:, 3 * D:4 * D].rearrange("p (c t) -> p c t", t=P), kb, KT6)
                    wait("sync", KT6)
                    dma_c("sync",
                          A["cc_in"][:, 0:NDC * TPC].rearrange("p (c t) -> p c t", t=TPC)[:, :, tt * P:(tt + 1) * P],
                          sb["bf"][:, 3 * D:4 * D].rearrange("p (c t) -> p c t", t=P), CS[sl])
                    wait("scalar", CS[sl])

        v0 = vop(lambda eng: eng.memset(KM6, 0.0))
        kv_side("a1_Wk", fk, bk_, 0, 0, True)
        kv_side("a1_Wv", fv, bv_, 32, NDC * TPC, False)
        pad = sb["stage"][:, 0:P]
        v = vop(lambda eng, pad=pad: eng.memset(pad, 0.0))
        v = vop(lambda eng, pad=pad: eng.tensor_copy(pad[:, 0:16], KM6))
        ID6 = psem("p6id")
        dma_c("sync", sb["rst"][:, 0:P], A["ident"], ID6)
        wait("tensor", ID6)
        wait("tensor", VD, v)
        m = top(lambda eng, pad=pad: eng.transpose(psA[:, 0:P], pad, sb["rst"][:, 0:P]))
        wait("vector", MM, m)
        v = vop(lambda eng: eng.reduce_max(st[0:16, 0:1], psA[0:16, 0:P], AX))
        v = vop(lambda eng, pad=pad: eng.memset(pad, 0.0))
        v = vop(lambda eng, pad=pad: eng.tensor_copy(pad[0:16, 0:1], st[0:16, 0:1]))
        wait("tensor", VD, v)
        m = top(lambda eng, pad=pad: eng.transpose(psB[:, 0:P], pad, sb["rst"][:, 0:P]))
        wait("vector", MM, m)
        kb = sb["bf"][:, 0:128]
        v = vop(lambda eng, kb=kb: eng.memset(kb, 0.0))
        v = vop(lambda eng, kb=kb: eng.tensor_copy(kb[0:1, 0:16], psB[0:1, 0:16]))
        KX = psem("p6kx")
        wait("sync", VD, v)
        dma_c("sync", A["cc_in"][:, 2 * NDC * TPC:2 * NDC * TPC + 128], kb, KX)
        CCS = ctx.sem("ccs")
        wait("gpsimd", KX)
        for ch in list(ctx.dma_seen):
            wait("gpsimd", ch)
        E("gpsimd", lambda eng: eng.collective_compute(
            "AllGather", ALU.bypass,
            replica_groups=[[0, 1], [2, 3], [4, 5], [6, 7]],
            ins=[A["cc_in"][:]], outs=[A["cc_out"][:]]).then_inc(CCS.h, 1))
        CCS.n += 1
        return CCS

    def phase6_ffn():
        st = sb["stat"]
        EW = sb["xnT"]
        for chp in list(ctx.dma_seen):
            wait("vector", chp)
            wait("scalar", chp)
            wait("sync", chp)
        WL = chans("p6fw", 2)
        GS = chans("p6gs", 2)
        XL = chans("p6x1", 2)
        SPA = chans("p6sa", 2)
        wait("sync", MM)
        for c in range(NDC):
            dma_c("sync", EW[:, c * 4096:(c + 1) * 4096],
                  A["wq_sp"]["f1_wg"][c * P:(c + 1) * P, :], WL[c % 2])
        for c in range(NDC):
            dst = sb["wbf"] if c < 4 else sb["kvc"]
            dma_c("sync", dst[:, (c % 4) * 4096:(c % 4 + 1) * 4096],
                  A["wq_sp"]["f1_wu"][c * P:(c + 1) * P, :], WL[c % 2])
        wait("tensor", WL[0])
        wait("tensor", WL[1])

        def wu_ap(c, col, wd):
            buf = sb["wbf"] if c < 4 else sb["kvc"]
            return buf[:, (c % 4) * 4096 + col:(c % 4) * 4096 + col + wd]

        bg = st[:, 1792:1793]
        bu = st[:, 1794:1795]
        bd = st[:, 1796:1797]
        for i in range(NT):
            inv = st[:, 656 + i:657 + i]
            sg = st[:, 8:9]
            su = st[:, 9:10]
            wait("vector", SD)
            vop(lambda eng, sg=sg, inv=inv, bg=bg: eng.tensor_tensor(sg, inv, bg, ALU.mult))
            v0 = vop(lambda eng, su=su, inv=inv, bu=bu: eng.tensor_tensor(su, inv, bu, ALU.mult))
            wait("scalar", VD, v0)
            am2 = st[:, 672 + i:673 + i]
            vop(lambda eng, am2=am2: eng.memset(am2, 0.0))
            wait("vector", GS[0])
            wait("vector", GS[1])
            for hc in range(8):
                wait("tensor", SD)
                wait("tensor", VD)
                for c in range(NDC):
                    top(lambda eng, c=c, i=i, hc=hc: eng.matmul(
                        psA[:, 0:512],
                        sb["qt"][:, c * TPC + i * P:c * TPC + (i + 1) * P],
                        EW[:, c * 4096 + hc * 512:c * 4096 + (hc + 1) * 512],
                        start=(c == 0), stop=(c == NDC - 1)), inc=(c == NDC - 1))
                for c in range(NDC):
                    top(lambda eng, c=c, i=i, hc=hc: eng.matmul(
                        psB[:, 0:512],
                        sb["qt"][:, c * TPC + i * P:c * TPC + (i + 1) * P],
                        wu_ap(c, hc * 512, 512),
                        start=(c == 0), stop=(c == NDC - 1)), inc=(c == NDC - 1))
                wait("scalar", MM)
                G = sb["stage"][:, 0:512]
                U = sb["stage"][:, 512:1024]
                sop(lambda eng, G=G, sg=sg: eng.activation(
                    out=G, in_=psA[:, 0:512], func=ACT.Silu, scale=sg))
                s1 = sop(lambda eng, U=U, su=su: eng.activation(
                    out=U, in_=psB[:, 0:512], func=ACT.Copy, scale=su))
                wait("vector", SD, s1)
                gu = sb["stage"][:, 1024 + (hc % 2) * 512:1024 + (hc % 2) * 512 + 512]
                if hc >= 2:
                    wait("vector", GS[hc % 2])
                vg = vop(lambda eng, gu=gu, G=G, U=U: eng.tensor_tensor(gu, G, U, ALU.mult))
                vop(lambda eng, gu=gu: eng.reduce_max(st[:, 10:11], gu, AX,
                                                      apply_absolute_value=True))
                vop(lambda eng, am2=am2: eng.tensor_tensor(am2, am2, st[:, 10:11], ALU.max))
                wait("sync", VD, vg)
                dma_c("sync", A["gu_sp"][i * P:(i + 1) * P, hc * 512:(hc + 1) * 512], gu, GS[hc % 2])
            vop(lambda eng, am2=am2: eng.tensor_scalar(am2, am2, 1e-5, None, ALU.max))
            vop(lambda eng, am2=am2, i=i: eng.tensor_scalar(
                st[:, 688 + i:689 + i], am2, 1.0 / 127, None, ALU.mult))
            vop(lambda eng, am2=am2: eng.reciprocal(am2, am2))
            vop(lambda eng, am2=am2: eng.tensor_scalar(am2, am2, 127.0, None, ALU.mult))
        wait("sync", MM)
        wait("sync", SD)
        for c in range(4096 // P):
            dma_c("sync", EW[:, c * D:(c + 1) * D],
                  A["wq_sp"]["f1_wd"][c * P:(c + 1) * P, :], WL[c % 2])
        wait("tensor", WL[0])
        wait("tensor", WL[1])
        GT = chans("p6gt", 2)
        wait("sync", GS[0])
        wait("sync", GS[1])
        for i in range(NT):
            gu = sb["stage"][:, 0:4096]
            wait("sync", VD)
            wait("sync", SD)
            dma_c("sync", gu, A["gu_sp"][i * P:(i + 1) * P, :], GS[0])
            wait("vector", GS[0])
            s2 = st[:, 672 + i:673 + i]
            vop(lambda eng, gu=gu, s2=s2: eng.tensor_scalar(gu, gu, s2, None, ALU.mult))
            vop(lambda eng, gu=gu: eng.tensor_scalar(gu, gu, 12582912.0, 12582912.0,
                                                     ALU.add, ALU.subtract))
            vop(lambda eng, gu=gu: eng.tensor_copy(sb["bf"][:, 0:4096], gu))
            v2 = VD.n
            wait("sync", VD, v2)
            dmat_c(sb["qt"][:, 0:4096].rearrange("p (c t) -> p c t", t=P),
                   sb["bf"][:, 0:4096], GT[0])
            x1t = sb["rst"][:, 0:D]
            dma_c("sync", x1t, A["xsum_sp"][i * P:(i + 1) * P, :], XL[0])
            c2 = st[:, 11:12]
            vop(lambda eng, c2=c2, i=i, bd=bd: eng.tensor_tensor(
                c2, st[:, 688 + i:689 + i], bd, ALU.mult))
            wait("tensor", GT[0])
            wait("tensor", VD)
            for oh in range(2):
                for c in range(4096 // P):
                    top(lambda eng, c=c, oh=oh: eng.matmul(
                        psA[:, oh * 512:(oh + 1) * 512],
                        sb["qt"][:, c * P: (c + 1) * P],
                        EW[:, c * D + oh * 512:c * D + (oh + 1) * 512],
                        start=(c == 0), stop=(c == 4096 // P - 1)),
                        inc=(c == 4096 // P - 1 and oh == 1))
            wait("vector", MM)
            wait("vector", XL[0])
            out = sb["rst"][:, D:2 * D]
            if i >= 1:
                wait("vector", SPA[0])
            va = vop(lambda eng, out=out, x1t=x1t, c2=c2: eng.scalar_tensor_tensor(
                out, psA[:, 0:D], c2, x1t, ALU.mult, ALU.add))
            wait("sync", VD, va)
            dma_c("sync", A["xacc_sp"][i * P:(i + 1) * P, :], out, SPA[0])

    def phase7(CCS):
        st = sb["stat"]
        sm = sb["small"]
        EW = sb["xnT"]
        SC1 = float(D) ** -0.5
        XT7 = psem("p7xt")
        wait("sync", SD)
        wait("sync", VD)
        wait("sync", MM)
        dma_c("sync", EW[:, 0:NDC * TPC], A["xn1T_sp"][:], XT7)
        wait("tensor", XT7)
        C7 = psem("p7c")
        fq = st[:, 704:712]
        bq_ = st[:, 720:728]
        for c in range(NDC):
            dma_c("sync", fq[:, c:c + 1], A["a1_lnq_s"][0:1, c * P:(c + 1) * P], C7)
            dma_c("sync", bq_[:, c:c + 1], A["a1_lnq_b"][0:1, c * P:(c + 1) * P], C7)
        dma_c("sync", sb["rst"][0:1, 0:D], A["a1_bq"], C7)
        wait("vector", C7)
        wait("tensor", C7)
        Wch = chans("p7w", 2)
        marks = stream_weight(A["a1_Wq"], NDC, D, 0, Wch)
        bb16 = sb["bf"][:, 0:8]
        vc_ = vop(lambda eng, bb16=bb16: eng.tensor_copy(bb16, bq_))
        wait("tensor", VD, marks[-1])
        wait("tensor", VD, vc_)
        for go, gw in groups(D):
            for c in range(NDC):
                top(lambda eng, c=c, bb16=bb16, go=go, gw=gw: eng.matmul(
                    psB[0:1, go:go + gw], bb16[:, c:c + 1],
                    sb["wbf"][:, c * D + go:c * D + go + gw],
                    start=(c == 0), stop=(c == NDC - 1)),
                    inc=(c == NDC - 1 and go + gw == D))
        wait("vector", MM)
        for c in range(NDC):
            vop(lambda eng, c=c: eng.tensor_scalar(
                sb["wbf"][:, c * D:(c + 1) * D], sb["wbf"][:, c * D:(c + 1) * D],
                fq[:, c:c + 1], None, ALU.mult))
        wait("tensor", VD)
        wait("vector", MM)
        v = vop(lambda eng: eng.tensor_tensor(sb["rst"][0:1, 0:D], sb["rst"][0:1, 0:D],
                                              psB[0:1, 0:D], ALU.add))
        wait("tensor", VD, v)
        for go, gw in groups(D):
            top(lambda eng, go=go, gw=gw: eng.matmul(
                psB[:, go:go + gw], sm[0:1, SM_ONESROW:SM_ONESROW + 128],
                sb["rst"][0:1, go:go + gw], start=True, stop=True), inc=(go + gw == D))
        wait("vector", MM)
        brep = sb["rst"][:, D:2 * D]
        vb = vop(lambda eng, brep=brep: eng.tensor_copy(brep, psB[:, 0:D]))
        wait("tensor", VD, vb)
        TQ7 = chans("p7tq", 2)
        for tt in range(NT):
            wait("tensor", VD)
            for oh in range(2):
                for c in range(NDC):
                    top(lambda eng, c=c, tt=tt, oh=oh: eng.matmul(
                        psA[:, oh * 512:(oh + 1) * 512],
                        EW[:, c * TPC + tt * P:c * TPC + (tt + 1) * P],
                        sb["wbf"][:, c * D + oh * 512:c * D + (oh + 1) * 512],
                        start=(c == 0), stop=(c == NDC - 1)),
                        inc=(c == NDC - 1 and oh == 1))
            wait("vector", MM)
            sl = tt % 2
            kb = sb["bf"][:, (1 + sl) * D:(2 + sl) * D]
            if tt >= 2:
                wait("vector", TQ7[sl])
            v = vop(lambda eng, kb=kb, brep=brep: eng.tensor_tensor(kb, psA[:, 0:D], brep, ALU.add))
            wait("scalar", VD, v)
            s_ = sop(lambda eng, kb=kb, sq=sb["stage"][:, 2 * D:3 * D]:
                     eng.activation(out=sq, in_=kb, func=ACT.Square))
            wait("vector", SD, s_)
            vop(lambda eng, tt=tt: eng.reduce_sum(
                st[:, 768 + 16 * tt:768 + 16 * (tt + 1)],
                sb["stage"][:, 2 * D:3 * D].rearrange("p (h d) -> p h d", d=64), AX))
            wait("sync", VD)
            dmat_c(t_dst(sb["qt"], tt, tot=TPC), kb, TQ7[sl])
        wait("tensor", TQ7[0])
        wait("tensor", TQ7[1])
        for e in ("vector", "scalar", "tensor", "sync", "gpsimd"):
            wait(e, CCS)
        KX7 = [ctx.sem("kx7a"), ctx.sem("kx7b")]
        kmx = sm[0:1, 600:616]
        kmx2 = sm[0:1, 616:632]
        dma_c("gpsimd", kmx,
              A["cc_out"][0][0:1, 2 * NDC * TPC:2 * NDC * TPC + 16], KX7[0])
        dma_c("gpsimd", kmx2,
              A["cc_out"][1][0:1, 2 * NDC * TPC:2 * NDC * TPC + 16], KX7[1])
        wait("vector", KX7[0])
        wait("vector", KX7[1])
        v = vop(lambda eng, kmx=kmx, kmx2=kmx2: eng.tensor_tensor(kmx, kmx, kmx2, ALU.max))
        wait("tensor", VD, v)
        m = top(lambda eng, kmx=kmx: eng.matmul(psA[:, 0:16], sm[0:1, SM_ONESROW:SM_ONESROW + 128],
                                                kmx, start=True, stop=True))
        wait("vector", MM, m)
        kmrep = st[:, 32:48]
        v = vop(lambda eng, kmrep=kmrep: eng.tensor_copy(kmrep, psA[:, 0:16]))
        for tt in range(NT):
            v = vop(lambda eng, tt=tt, kmrep=kmrep: eng.tensor_tensor(
                st[:, 1024 + 16 * tt:1024 + 16 * (tt + 1)],
                st[:, 768 + 16 * tt:768 + 16 * (tt + 1)], kmrep, ALU.mult))
        wait("scalar", VD, v)
        s_ = sop(lambda eng: eng.activation(out=st[:, 1024:1024 + 16 * NT],
                                            in_=st[:, 1024:1024 + 16 * NT],
                                            func=ACT.Sqrt, scale=SC1 * SC1))
        wait("vector", SD, s_)
        vop(lambda eng: eng.tensor_scalar(st[:, 1024:1024 + 16 * NT],
                                          st[:, 1024:1024 + 16 * NT], -1.0, None, ALU.mult))
        CR = psem("p7cr")
        dma_c("sync", sm[0:1, 632:633], A["corner"], CR)
        wait("vector", CR)
        KL = chans("p7kl", 2)
        VL = chans("p7vl", 2)
        TpP = chans("p7pt", 4)
        NT2 = NT // 2
        rw_ = T // 2
        PTB = 4 * T
        wait("scalar", LD)
        wait("vector", LD)
        PTALL = EW[:, PTB:PTB + 4 * T].rearrange("p (x s q) -> p x s q", s=2, q=P)
        pend_pv = [None]
        pv_evac = [0]
        nm_mark = [0]
        for j in range(NDC):
            jq = j * TPC
            wait("sync", VD)
            wait("sync", SD)
            wait("sync", MM)
            wait("sync", KL[0])
            wait("sync", VL[0])
            for r in range(2):
                dma_c("sync", sb["kvc"][:, r * TPC:(r + 1) * TPC],
                      A["cc_out"][r][:, 0:NDC * TPC]
                      .rearrange("p (c t) -> p c t", t=TPC)[:, j, :], KL[0])
                dma_c("sync", sb["kvc"][:, T + r * NT * P:T + (r + 1) * NT * P],
                      A["cc_out"][r][:, NDC * TPC:2 * NDC * TPC]
                      .rearrange("p (t d) -> p t d", d=D)[:, :, j * P:(j + 1) * P], VL[0])
            wait("tensor", KL[0])

            def flush_pv(j=j):
                if pend_pv[0] is None:
                    return
                jp, g = pend_pv[0]
                pend_pv[0] = None
                wait("tensor", VD, pv_evac[0])
                wait("tensor", SD)
                wait("tensor", TpP[0])
                wait("tensor", TpP[1])
                wait("tensor", TpP[2])
                wait("tensor", TpP[3])
                wait("tensor", VL[0])
                for x in range(2 * NT):
                    for hh in range(2):
                        ps = psA if hh == 0 else psB
                        top(lambda eng, x=x, hh=hh, ps=ps:
                            eng.matmul(ps[64 * hh:64 * hh + 64, 1536:1536 + 256],
                                       sb["kvc"][:, T + x * P + 64 * hh:
                                                 T + x * P + 64 * hh + 64],
                                       EW[:, PTB + (hh * 2 * NT + x) * 256:
                                             PTB + (hh * 2 * NT + x + 1) * 256],
                                       start=(x == 0), stop=(x == 2 * NT - 1),
                                       tile_position=(0, 64 * hh), skip_group_check=True),
                            inc=(x == 2 * NT - 1 and hh == 1))
                wait("vector", MM)
                vop(lambda eng, jp=jp, g=g: eng.tensor_copy(
                    sb["wbf"][0:64, jp * TPC + 2 * g * P: jp * TPC + (2 * g + 2) * P],
                    psA[0:64, 1536:1536 + 256]))
                pv_evac[0] = vop(lambda eng, jp=jp, g=g: eng.tensor_copy(
                    sb["wbf"][64:128, jp * TPC + 2 * g * P: jp * TPC + (2 * g + 2) * P],
                    psB[64:128, 1536:1536 + 256]))

            def p7_slot(s, j=j, jq=jq):
                par = s % 2
                EB = par * 2 * T
                zc = st[:, 48:56]
                wait("scalar", VD)
                first_exp = True
                exps = []
                for run in range(2):
                    wait("tensor", SD)
                    wait("tensor", VD)
                    for go, gw in groups(rw_):
                        top(lambda eng, go=go, gw=gw, s=s, run=run, jq=jq:
                            eng.matmul(psA[:, go:go + gw],
                                       sb["qt"][0:64, jq + s * P: jq + (s + 1) * P],
                                       sb["kvc"][0:64, run * rw_ + go: run * rw_ + go + gw],
                                       start=True, stop=True, tile_position=(0, 0)),
                            inc=False)
                        top(lambda eng, go=go, gw=gw, s=s, run=run, jq=jq:
                            eng.matmul(psB[:, go:go + gw],
                                       sb["qt"][64:128, jq + s * P: jq + (s + 1) * P],
                                       sb["kvc"][64:128, run * rw_ + go: run * rw_ + go + gw],
                                       start=True, stop=True, tile_position=(64, 0)),
                            inc=(go + gw == rw_))
                    m = MM.n
                    vv = None
                    if s == 0 and run == 1:
                        wait("vector", MM, m)
                        vop(lambda eng: eng.tensor_scalar(
                            psA[0:1, rw_ - 1:rw_], psA[0:1, rw_ - 1:rw_],
                            sm[0:1, 632:633], None, ALU.add))
                        vv = vop(lambda eng: eng.tensor_scalar(
                            psB[0:1, rw_ - 1:rw_], psB[0:1, rw_ - 1:rw_],
                            sm[0:1, 632:633], None, ALU.add))
                    if first_exp:
                        wait("scalar", TpP[2 * par])
                        wait("scalar", TpP[2 * par + 1])
                        first_exp = False
                    if vv is not None:
                        wait("scalar", VD, vv)
                    else:
                        wait("scalar", MM, m)
                    for hh, ps in ((0, psA), (1, psB)):
                        bias = st[:, 1024 + s * 16 + 2 * j + hh: 1024 + s * 16 + 2 * j + hh + 1]
                        zcol = zc[:, run * 2 + hh: run * 2 + hh + 1]
                        e_ = sop(lambda eng, run=run, hh=hh, bias=bias, zcol=zcol,
                                 ps=ps, EB=EB:
                                 eng.activation(out=EW[:, EB + (hh * 2 + run) * rw_:
                                                       EB + (hh * 2 + run + 1) * rw_],
                                                in_=ps[:, 0:rw_], func=ACT.Exp, bias=bias,
                                                scale=SC1, accum_out=zcol))
                        exps.append(e_)
                wait("vector", SD, exps[-1])
                vop(lambda eng, zc=zc: eng.tensor_tensor(zc[:, 4:6], zc[:, 0:2], zc[:, 2:4], ALU.add))
                vop(lambda eng, zc=zc: eng.reciprocal(zc[:, 6:8], zc[:, 4:6]))
                for hh in range(2):
                    ehh = EW[:, EB + hh * T:EB + (hh + 1) * T]
                    nm_mark[0] = vop(lambda eng, ehh=ehh, zc=zc, hh=hh: eng.tensor_scalar(
                        ehh, ehh, zc[:, 6 + hh:7 + hh], None, ALU.mult))

            def do_dt7(s):
                par = s % 2
                EB = par * 2 * T
                for hh in range(2):
                    ehh = EW[:, EB + hh * T:EB + (hh + 1) * T]
                    wait("sync", VD, nm_mark[0])
                    wait("sync", MM)
                    dmat_c(PTALL[:, hh * 2 * NT:(hh + 1) * 2 * NT, par, :], ehh,
                           TpP[2 * par + hh])

            for g in range(NT2):
                s0, s1 = 2 * g, 2 * g + 1
                p7_slot(s0)
                flush_pv()
                do_dt7(s0)
                p7_slot(s1)
                do_dt7(s1)
                pend_pv[0] = (j, g)
            flush_pv()
        ctx.barrier("p7att")
        atT = sb["wbf"]
        marks = []
        wait("sync", VD)
        for c in range(NDC):
            sl = c % 2
            stg = sb["stage"][:, sl * D:(sl + 1) * D]
            if c >= 2:
                wait("sync", VD, marks[c - 2])
            dma_c("sync", stg, A["a1_Wo"][c * P:(c + 1) * P, :], Wch[sl])
            wait("vector", Wch[sl])
            v = vop(lambda eng, c=c, stg=stg: eng.tensor_copy(
                sb["kvc"][:, c * D:(c + 1) * D], stg))
            marks.append(v)
        BO = psem("p7bo")
        dma_c("sync", sb["rst"][0:1, 0:D], A["a1_bo"], BO)
        wait("tensor", BO)
        wait("tensor", VD, marks[-1])
        for go, gw in groups(D):
            top(lambda eng, go=go, gw=gw: eng.matmul(
                psB[:, go:go + gw], sm[0:1, SM_ONESROW:SM_ONESROW + 128],
                sb["rst"][0:1, go:go + gw], start=True, stop=True), inc=(go + gw == D))
        wait("vector", MM)
        borep = sb["rst"][:, D:2 * D]
        vb = vop(lambda eng, borep=borep: eng.tensor_copy(borep, psB[:, 0:D]))
        wait("tensor", VD, vb)
        XN7 = chans("p7xn", 2)
        SPH = chans("p7h", 2)
        hm = []
        for s in range(NT):
            wait("tensor", VD)
            for oh in range(2):
                for c in range(NDC):
                    top(lambda eng, c=c, s=s, oh=oh: eng.matmul(
                        psA[:, oh * 512:(oh + 1) * 512],
                        atT[:, c * TPC + s * P: c * TPC + (s + 1) * P],
                        sb["kvc"][:, c * D + oh * 512: c * D + (oh + 1) * 512],
                        start=(c == 0), stop=(c == NDC - 1)),
                        inc=(c == NDC - 1 and oh == 1))
            sl = s % 2
            xn1t_ = sb["stage"][:, (2 + sl) * D:(3 + sl) * D]
            if s >= 2:
                wait("sync", VD, hm[s - 2])
                wait("sync", SD)
                wait("sync", SPH[sl])
            dma_c("sync", xn1t_, A["xn1_sp"][s * P:(s + 1) * P, :], XN7[sl])
            wait("vector", XN7[sl])
            wait("vector", MM)
            if s >= 2:
                wait("vector", SPH[sl])
            vop(lambda eng, xn1t_=xn1t_, borep=borep: eng.tensor_tensor(
                xn1t_, xn1t_, borep, ALU.add))
            vh = vop(lambda eng, xn1t_=xn1t_: eng.tensor_tensor(
                xn1t_, xn1t_, psA[:, 0:D], ALU.add))
            hm.append(vh)
            vop(lambda eng, xn1t_=xn1t_, s=s: eng.reduce_sum(st[:, 768 + s:769 + s], xn1t_, AX))
            wait("scalar", VD, vh)
            sop(lambda eng, xn1t_=xn1t_, s=s, sq=sb["bf"][:, 2 * D:3 * D]:
                eng.activation(out=sq, in_=xn1t_, func=ACT.Square,
                               accum_out=st[:, 832 + s:833 + s]))
            wait("sync", VD, vh)
            wait("sync", SD)
            dma_c("sync", A["h_sp"][s * P:(s + 1) * P, :], xn1t_, SPH[sl])
        ln_finalize(NT, 1e-5, base=768)
        ctx.barrier("p7h")
        fm = st[:, 704:712]
        bm_ = st[:, 720:728]
        C7b = psem("p7c2")
        for c in range(NDC):
            dma_c("sync", fm[:, c:c + 1], A["a1_lnm_s"][0:1, c * P:(c + 1) * P], C7b)
            dma_c("sync", bm_[:, c:c + 1], A["a1_lnm_b"][0:1, c * P:(c + 1) * P], C7b)
        dma_c("sync", sb["rst"][0:1, 0:2048], A["a1_b1"], C7b)
        wait("vector", C7b)
        wait("tensor", C7b)
        W1L = chans("p7w1", 2)
        wait("sync", VD)
        w1marks = []
        for c in range(NDC):
            sl = c % 2
            stg = sb["stage"][:, sl * 2048:(sl + 1) * 2048]
            if c >= 2:
                wait("sync", VD, w1marks[c - 2])
            dma_c("sync", stg, A["a1_W1"][c * P:(c + 1) * P, :], W1L[sl])
            wait("vector", W1L[sl])
            v = vop(lambda eng, c=c, stg=stg: eng.tensor_copy(
                EW[:, c * 2048:(c + 1) * 2048], stg))
            w1marks.append(v)
        bb16b = sb["bf"][:, 8:16]
        vc2 = vop(lambda eng, bb16b=bb16b: eng.tensor_copy(bb16b, bm_))
        wait("tensor", VD, w1marks[-1])
        wait("tensor", VD, vc2)
        for go, gw in groups(2048):
            for c in range(NDC):
                top(lambda eng, c=c, bb16b=bb16b, go=go, gw=gw: eng.matmul(
                    psB[0:1, go:go + gw], bb16b[:, c:c + 1],
                    EW[:, c * 2048 + go:c * 2048 + go + gw],
                    start=(c == 0), stop=(c == NDC - 1)),
                    inc=(c == NDC - 1 and go + gw == 2048))
        wait("vector", MM)
        for c in range(NDC):
            vop(lambda eng, c=c: eng.tensor_scalar(
                EW[:, c * 2048:(c + 1) * 2048], EW[:, c * 2048:(c + 1) * 2048],
                fm[:, c:c + 1], None, ALU.mult))
        wait("tensor", VD)
        wait("vector", MM)
        v = vop(lambda eng: eng.tensor_tensor(sb["rst"][0:1, 0:2048], sb["rst"][0:1, 0:2048],
                                              psB[0:1, 0:2048], ALU.add))
        BV1 = psem("p7bv")
        wait("sync", VD, v)
        dma_c("sync", A["bv1_sp"][:], sb["rst"][0:1, 0:2048], BV1)
        wait("sync", BV1)
        b1c = st[:, 1536:1552]
        dma_c("sync", b1c, A["bv1_sp"][0:1, :].rearrange("o (c p) -> p (o c)", p=P), BV1, slow=True)
        wait("scalar", BV1)
        ZL = chans("p7z", 2)
        TZ = chans("p7tz", 2)
        zm = []
        for s in range(NT):
            sl = s % 2
            ht = sb["stage"][:, sl * D:(sl + 1) * D]
            if s >= 2:
                wait("sync", VD, zm[s - 2])
            dma_c("sync", ht, A["h_sp"][s * P:(s + 1) * P, :], ZL[sl])
            wait("vector", ZL[sl])
            mu = st[:, 768 + 128 + s:768 + 128 + s + 1]
            r = st[:, 768 + 192 + s:768 + 192 + s + 1]
            zb = sb["bf"][:, sl * D:(sl + 1) * D]
            if s >= 2:
                wait("vector", TZ[sl])
            vz = vop(lambda eng, zb=zb, ht=ht, mu=mu, r=r:
                     eng.tensor_scalar(zb, ht, mu, r, ALU.subtract, ALU.mult))
            zm.append(vz)
            wait("sync", VD, vz)
            dmat_c(t_dst(sb["qt"], s, tot=TPC), zb, TZ[sl])
        W2L = chans("p7w2", 2)
        wait("sync", VD)
        wait("sync", MM)
        w2marks = []
        for c in range(2048 // P):
            sl = c % 2
            stg = sb["stage"][:, sl * D:(sl + 1) * D]
            if c >= 2:
                wait("sync", VD, w2marks[c - 2])
            dma_c("sync", stg, A["a1_W2"][c * P:(c + 1) * P, :], W2L[sl])
            wait("vector", W2L[sl])
            v = vop(lambda eng, c=c, stg=stg: eng.tensor_copy(
                sb["kvc"][:, c * D:(c + 1) * D], stg))
            w2marks.append(v)
        BV2 = psem("p7b2")
        dma_c("sync", sb["rst"][0:1, 0:D], A["a1_b2"], BV2)
        wait("tensor", BV2)
        wait("tensor", VD, w2marks[-1])
        for go, gw in groups(D):
            top(lambda eng, go=go, gw=gw: eng.matmul(
                psB[:, go:go + gw], sm[0:1, SM_ONESROW:SM_ONESROW + 128],
                sb["rst"][0:1, go:go + gw], start=True, stop=True), inc=(go + gw == D))
        wait("vector", MM)
        b2rep = sb["rst"][:, D:2 * D]
        vb2 = vop(lambda eng, b2rep=b2rep: eng.tensor_copy(b2rep, psB[:, 0:D]))
        wait("tensor", VD, vb2)
        wait("tensor", TZ[0])
        wait("tensor", TZ[1])
        wait("sync", VD)
        HL = chans("p7hl", 2)
        AL7 = chans("p7al", 2)
        YS = chans("p7y", 2)
        for grp, (gq, gw) in enumerate(groups(TPC, 512)):
            wait("tensor", VD)
            for oc in range(16):
                wait("tensor", SD)
                ps = psA if oc % 2 == 0 else psB
                for c in range(NDC):
                    top(lambda eng, c=c, oc=oc, gq=gq, gw=gw, ps=ps: eng.matmul(
                        ps[:, 0:gw],
                        EW[:, c * 2048 + oc * P:c * 2048 + (oc + 1) * P],
                        sb["qt"][:, c * TPC + gq:c * TPC + gq + gw],
                        start=(c == 0), stop=(c == NDC - 1)), inc=(c == NDC - 1))
                wait("scalar", MM)
                sop(lambda eng, oc=oc, ps=ps, gw=gw: eng.activation(
                    out=sb["wbf"][:, oc * gw:(oc + 1) * gw], in_=ps[:, 0:gw],
                    func=ACT.Gelu_apprx_tanh, bias=b1c[:, oc:oc + 1]))
            for tg in range(gw // P):
                s = gq // P + tg
                wait("tensor", SD)
                wait("tensor", VD)
                for oh in range(2):
                    for c in range(2048 // P):
                        top(lambda eng, c=c, tg=tg, oh=oh, gw=gw: eng.matmul(
                            psA[:, oh * 512:(oh + 1) * 512],
                            sb["wbf"][:, c * gw + tg * P:  c * gw + (tg + 1) * P],
                            sb["kvc"][:, c * D + oh * 512:c * D + (oh + 1) * 512],
                            start=(c == 0), stop=(c == 2048 // P - 1)),
                            inc=(c == 2048 // P - 1 and oh == 1))
                sl = s % 2
                ht = sb["stage"][:, sl * D:(sl + 1) * D]
                xat = sb["stage"][:, (2 + sl) * D:(3 + sl) * D]
                if s >= 2:
                    wait("sync", VD, ym[s - 2])
                    wait("sync", YS[sl])
                dma_c("sync", ht, A["h_sp"][s * P:(s + 1) * P, :], HL[sl])
                dma_c("sync", xat, A["xacc_sp"][s * P:(s + 1) * P, :], AL7[sl])
                wait("vector", HL[sl])
                wait("vector", AL7[sl])
                wait("vector", MM)
                if s >= 2:
                    wait("vector", YS[sl])
                vop(lambda eng, ht=ht, xat=xat: eng.tensor_tensor(ht, ht, xat, ALU.add))
                vop(lambda eng, ht=ht, b2rep=b2rep: eng.tensor_tensor(ht, ht, b2rep, ALU.add))
                vy = vop(lambda eng, ht=ht: eng.tensor_tensor(ht, ht, psA[:, 0:D], ALU.add))
                ym.append(vy)
                wait("sync", VD, vy)
                dma_c("sync", A["y"][s * P:(s + 1) * P, :], ht, YS[sl])
        wait("sync", YS[0])
        wait("sync", YS[1])

    ym = []
    CCS6 = None
    phase6_ln_quant()
    ctx.barrier("p6l")
    CCS6 = phase6_kv_cc()
    phase6_ffn()
    ctx.barrier("p6f")
    phase7(CCS6)


    if "y_done" not in globals():
        pass
    ctx.finalize()
    return nc, dbg



def _in_maps(inputs, T):
    g = _build_host(inputs, T)
    maps = []
    for core in range(8):
        d = _percore_host(inputs, g, T, core)
        m = {}
        for k, v in g.items():
            if k.startswith("_"):
                continue
            if k in ("rope_ck", "rope_sk"):
                continue
            m[k] = np.asarray(v, np.float32)
        for k, v in d.items():
            m[k] = np.asarray(v, np.float32)
        maps.append(m)
    return maps


def kernel(**inputs):
    T = int(np.asarray(inputs["L0_attn_x"]).shape[1])
    nc, _ = build_program(T)
    maps = _in_maps(inputs, T)
    res = run_bass_kernel_spmd(nc, maps, core_ids=list(range(8)))
    TPC = T // 2
    x = np.asarray(inputs["L0_attn_x"])
    out = np.zeros((4, T, D), np.float32)
    for core in range(8):
        b, par = core // 2, core % 2
        pos = _own_positions(T, par)
        out[b][pos] = res.results[core]["y"]
    return out
